# version 07
# baseline (speedup 1.0000x reference)
import numpy as np
import ml_dtypes
from contextlib import ExitStack
import concourse.bass as bass
import concourse.mybir as mybir
from concourse.bass_utils import run_bass_kernel_spmd

F32 = mybir.dt.float32
BF16 = mybir.dt.bfloat16
F16 = mybir.dt.float16
I16 = mybir.dt.int16
AF = mybir.ActivationFunctionType
ALU = mybir.AluOpType
AX = mybir.AxisListType
BF = ml_dtypes.bfloat16

D = 1024
NH = 16
IFF = 4096
TAU = 128
SUP = 256
TPS = SUP // TAU
MQ = 1536.0
N_CORES = 8
DBG = False


class Sem:
    __slots__ = ("h", "n")

    def __init__(self, es, nc, name):
        self.h = es.enter_context(nc.semaphore(name))
        self.n = 0

    def inc(self, instr, k=1):
        instr.then_inc(self.h, k)
        self.n += k
        return self.n


_WRAP_OPS = {
    "tensor_copy", "tensor_tensor", "tensor_add", "tensor_sub", "tensor_mul",
    "tensor_max", "tensor_scalar", "tensor_single_scalar", "tensor_scalar_add",
    "tensor_scalar_sub", "tensor_scalar_mul", "tensor_scalar_max",
    "tensor_scalar_min", "tensor_relu", "tensor_reduce", "reduce_sum",
    "reduce_max", "scalar_tensor_tensor", "reciprocal", "memset", "transpose",
    "activation", "square", "sqrt", "copy", "mul", "add", "sign", "select",
    "bn_stats", "bn_aggr", "engine_nop", "iota",
}


class EngSem:
    __slots__ = ("e",)

    def __init__(self, e):
        self.e = e

    @property
    def h(self):
        return self.e._sem

    @property
    def n(self):
        return self.e._n

    def inc(self, instr, k=1):
        return self.e._n


class SerialEng:

    def __init__(self, eng, nc, es, name):
        self._eng = eng
        self._sem = es.enter_context(nc.semaphore("ser_" + name))
        self._n = 0
        self._last = {}

    def _tnames(self, v, out):
        if isinstance(v, bass.AP):
            out.append(v.tensor.name)

    def __getattr__(self, name):
        fn = getattr(self._eng, name)
        if name not in _WRAP_OPS:
            return fn

        def wrapped(*a, **k):
            names = []
            for v in a:
                self._tnames(v, names)
            for v in k.values():
                self._tnames(v, names)
            need = 0
            for t in names:
                need = max(need, self._last.get(t, 0))
            if need:
                self._eng.wait_ge(self._sem, need)
            instr = fn(*a, **k)
            instr.then_inc(self._sem, 1)
            self._n += 1
            wtoks = []
            for v in list(a[:1]) + [k[x] for x in ("accum_out", "out") if x in k]:
                if isinstance(v, bass.AP):
                    self._last[v.tensor.name] = self._n
            if a and isinstance(a[0], bass.AP):
                self._last[a[0].tensor.name] = self._n
            return instr

        return wrapped


def build_program(S):
    NT = S // TAU
    NS = S // SUP
    nc = bass.Bass("TRN2", target_bir_lowering=False,
                   detect_race_conditions=False)

    x_in = nc.dram_tensor("x", [S, D], F32, kind="ExternalInput")
    wqkv_d = nc.dram_tensor("wqkv", [D, 3 * D], F32, kind="ExternalInput")
    btile_d = nc.dram_tensor("btile", [128, 24], F32, kind="ExternalInput")
    wout_d = nc.dram_tensor("wout", [D, D], F32, kind="ExternalInput")
    bout_d = nc.dram_tensor("bout", [1, D], F32, kind="ExternalInput")
    wg_d = nc.dram_tensor("wg_t", [D, IFF], F32, kind="ExternalInput")
    wu_d = nc.dram_tensor("wu_t", [D, IFF], F32, kind="ExternalInput")
    wd_d = nc.dram_tensor("wd_t", [IFF, D], F32, kind="ExternalInput")
    id_d = nc.dram_tensor("ident", [128, 128], BF16, kind="ExternalInput")
    mq_d = nc.dram_tensor("maskq", [9, 16 * SUP], BF16, kind="ExternalInput")
    mk_d = nc.dram_tensor("maskk", [9, 16 * TAU], BF16, kind="ExternalInput")
    out_d = nc.dram_tensor("out", [S, D], F32, kind="ExternalOutput")
    kind_dbg = {"kind": "ExternalOutput"} if DBG else {}
    qt_sp = nc.dram_tensor("qt_sp", [NT * D * TAU], BF16, **kind_dbg)
    at_sp = nc.dram_tensor("at_sp", [S, D], F32, **kind_dbg)
    h2_sp = nc.dram_tensor("h2_sp", [NT * IFF * TAU], BF16, **kind_dbg)
    if DBG:
        dbg_p = nc.dram_tensor("dbg_p", [128, 1024], F32, kind="ExternalOutput")

    es = ExitStack()
    with es:
        TE, SY = nc.tensor, nc.sync
        VE = SerialEng(nc.vector, nc, es, "ve")
        SE = SerialEng(nc.scalar, nc, es, "se")
        GE = SerialEng(nc.gpsimd, nc, es, "ge")
        sb = lambda n, sh, dt: es.enter_context(nc.sbuf_tensor(n, sh, dt))
        sid = sb("sid", [128, 128], BF16)
        p_m = sb("p_m", [128, NT], F32)
        p_rs = sb("p_rs", [128, NT], F32)
        p_sg = sb("p_sg", [128, NT], F32)
        p_sq = sb("p_sq", [128, NT], F32)
        p_qb = sb("p_qb", [128, NT], F32)
        p_nm = sb("p_nm", [128, NT], F32)
        p_al = sb("p_al", [128, NT], F32)
        p_ct = sb("p_ct", [128, NT], F32)
        p_sh = sb("p_sh", [128, NT], F32)
        p_dd = sb("p_dd", [128, NT], F32)
        t_a = sb("t_a", [128, NT], F32)
        t_b = sb("t_b", [128, NT], F32)
        bbc = sb("bbc", [128, D], F32)
        brow = sb("brow", [1, D], F32)
        ones1 = sb("ones1", [1, 128], F32)
        onesc = sb("onesc", [128, 1], F32)
        bcol = sb("bcol", [128, 24], F32)
        wabs = sb("wabs", [128, 3], F32)
        mu3 = sb("mu3", [128, 3], F32)
        thr_n = sb("thr_n", [128, 3], F32)
        murow = sb("murow", [1, 3], F32)
        hx4 = sb("hx4", [128, 8], F32)
        rcp4 = sb("rcp4", [128, 4], F32)
        chf = sb("chf", [128, 2], F32)

        SEMS = {}
        _VE_SEMS = {"lnv", "qbb", "kst", "vev", "rcp", "cev", "mid", "wab",
                    "ter", "rl2", "hpw", "hmx", "h2b", "och", "ini"}
        _SE_SEMS = {"sqd", "sig", "xnb", "fme", "exp", "nrm", "aev", "wap",
                    "ts2", "gev", "h2a", "bbc"}
        _GE_SEMS = {"vpr"}

        def sem(name):
            if name not in SEMS:
                if name in _VE_SEMS:
                    SEMS[name] = EngSem(VE)
                elif name in _SE_SEMS:
                    SEMS[name] = EngSem(SE)
                elif name in _GE_SEMS:
                    SEMS[name] = EngSem(GE)
                else:
                    SEMS[name] = Sem(es, nc, name)
            return SEMS[name]

        block = es.enter_context(nc.Block())

        def barrier(tag):
            b = sem("bar_" + tag)
            for eng in (TE, VE, SE, GE, SY):
                b.inc(eng.nop(), 1)
            for eng in (TE, VE, SE, GE, SY):
                eng.wait_ge(b.h, b.n)

        @block.sync
        def _(_q):
            esA = ExitStack()
            with esA:
                sa = lambda n, sh, dt: esA.enter_context(nc.sbuf_tensor(n, sh, dt))
                wq = sa("wq", [128, 8 * 3072], BF16)
                wo = sa("wo", [64, 16 * 1024], BF16)
                xr = [sa(f"xr{i}", [128, D], F32) for i in range(2)]
                xsq = sa("xsq", [128, D], F32)
                xnst = [sa(f"xnst{i}", [128, D], BF16) for i in range(2)]
                qtf = sa("qtf", [128, D], F32)
                qi0 = sa("qi0", [128, D], mybir.dt.int32)
                qd = xsq
                qs1 = sa("qs1", [128, D], BF16)
                qs2 = sa("qs2", [128, D], BF16)
                qcs = sa("qcs", [128, D], BF16)
                qbst = [sa(f"qbst{i}", [128, D], BF16) for i in range(2)]
                qts = [sa(f"qts{i}", [128, D], BF16) for i in range(2)]
                xnt = sa("xnt", [128, 4 * D], BF16)
                fmst = sa("fmst", [128, 8 * SUP], BF16)
                qfl = sa("qfl", [128, 16 * SUP], BF16)
                kfl = sa("kfl", [128, 16 * SUP], BF16)
                vfl = sa("vfl", [128, 16 * SUP], BF16)
                kstk = [sa("kstk0", [128, 16 * TAU], BF16)] * 2
                vpre = [sa(f"vpre{i}", [128, 16 * TAU], BF16) for i in range(2)]
                est = [sa("est0", [128, 16 * TAU], BF16)] * 2
                vstk = [sa("vstk0", [128, 16 * 65], BF16)] * 2
                ctxn = [sa(f"ctxn{i}", [128, 16 * 64], BF16) for i in range(2)]
                ctxt = [sa(f"ctxt{i}", [128, 16 * 128], BF16) for i in range(2)]
                atst = [sa("atst0", [128, D], F32)] * 2
                wfs_t = [sa(f"wfs{i}", [128, 512], F32) for i in range(2)]
                wfs = [t[:] for t in wfs_t]
                wsc = sa("wsc", [128, 512], F32)

                pq = esA.enter_context(nc.psum_tensor("pq", [128, 512], F32))
                pwo = esA.enter_context(nc.psum_tensor("pwo", [128, 1024], F32))
                ps_s = esA.enter_context(nc.psum_tensor("ps_s", [128, 1024], F32))
                pcx = esA.enter_context(nc.psum_tensor("pcx", [128, 512], F32))
                ptr = esA.enter_context(nc.psum_tensor("ptr", [128, 1024], BF16))
                ptc = esA.enter_context(nc.psum_tensor("ptc", [128, 1024], BF16))

                s_w0 = sem("w0")
                s_cst = sem("cst")
                s_msk = sem("msk")
                s_ini = sem("ini")
                s_bbc = sem("bbc")
                s_w0.inc(SY.dma_start(out=sid[:], in_=id_d.ap()), 16)
                s_w0.inc(SY.dma_start(out=brow[:], in_=bout_d.ap()), 16)
                s_w0.inc(SY.dma_start(out=bcol[:], in_=btile_d.ap()), 16)
                s_msk.inc(SY.dma_start(out=qfl[64:73, :], in_=mq_d.ap()), 16)
                s_msk.inc(SY.dma_start(out=kstk[0][64:73, :], in_=mk_d.ap()), 16)
                for c in range(8):
                    s_cst.inc(GE.dma_start(
                        out=wq[:, c * 3072:(c + 1) * 3072],
                        in_=wqkv_d.ap()[c * 128:(c + 1) * 128, :]), 16)
                for h in range(16):
                    s_cst.inc(GE.dma_start(
                        out=wo[:, h * 1024:(h + 1) * 1024],
                        in_=wout_d.ap()[h * 64:(h + 1) * 64, :]), 16)
                VE.memset(ones1[:], 1.0)
                VE.memset(onesc[:], 1.0)
                VE.memset(wabs[:], 0.0)
                VE.memset(chf[:, 0:1], -0.5)
                VE.memset(chf[:, 1:2], 0.5)
                VE.memset(bass.AP(vstk[0], 64, [[16 * 65, 128], [65, 16], [1, 1]]), 1.0)
                s_ini.inc(VE.engine_nop(), 1)
                TE.wait_ge(s_w0.h, s_w0.n)
                TE.wait_ge(s_ini.h, s_ini.n)
                s_mmb = sem("mmb")
                TE.matmul(pq[:], ones1[:], brow[0:1, 0:512], start=True, stop=True)
                s_mmb.inc(TE.matmul(ps_s[:, 0:512], ones1[:], brow[0:1, 512:1024],
                                    start=True, stop=True), 1)
                SE.wait_ge(s_mmb.h, s_mmb.n)
                SE.copy(bbc[:, 0:512], pq[:])
                s_bbc.inc(SE.copy(bbc[:, 512:1024], ps_s[:, 0:512]), 1)
                qkv_bank_free = 0
                if DBG:
                    barrier("pre")

                s_xld = sem("xld")
                s_lnv = sem("lnv")
                s_sqd = sem("sqd")
                s_sig = sem("sig")
                s_xnb = sem("xnb")
                s_qbb = sem("qbb")
                s_xnt = sem("xnt")
                s_qtt = sem("qtt")
                s_qsp = sem("qsp")
                s_qkv = sem("qkv")
                s_fme = sem("fme")
                s_fla = sem("fla")
                s_kst = sem("kst")
                s_vpr = sem("vpr")
                s_smm = sem("smm")
                s_exp = sem("exp")
                s_vtr = sem("vtr")
                s_vev = sem("vev")
                s_cmm = sem("cmm")
                s_rcp = sem("rcp")
                s_nrm = sem("nrm")
                s_ctr = sem("ctr")
                s_cev = sem("cev")
                s_wom = sem("wom")
                s_aev = sem("aev")
                s_asp = sem("asp")
                s_mid = sem("mid")
                s_wfl = sem("wfl")
                s_wab = sem("wab")
                s_wap = sem("wap")

                xr_free = [0, 0]
                xn_free = [0, 0]
                qb_free = [0, 0]
                qt_free = [0, 0]
                xnt_free = [0, 0, 0, 0]
                xsq_free = 0
                kst_free = [0, 0]
                vpr_free = [0, 0]
                est_free = [0, 0]
                vstk_free = [0, 0]
                ctxn_free = [0, 0]
                ctxt_free = [0, 0]
                atst_free = [0, 0]
                wfs_free = [0, 0]
                fla_prev_mid = 0
                fla_prev_smm = 0
                fla_prev_kst = 0
                fla_prev_vpr = 0
                fme_tok = [0] * 24
                q_sig_prev = [0]
                q_cs_prev = [0]
                q_fin_prev = [0]
                wab_se_prev = [0]
                exp_prev2 = [0, 0]
                vev_prev = 0
                cev_prev = 0
                nrm_prev = 0
                aev_prev = 0

                def f_ln(ti):
                    nonlocal xsq_free
                    bi = ti % 2
                    col = slice(ti, ti + 1)
                    SE.wait_ge(s_lnv.h, xr_free[bi])
                    xtok = s_xld.inc(SE.dma_start(
                        out=xr[bi][:], in_=x_in.ap()[ti * TAU:(ti + 1) * TAU, :]), 16)
                    VE.wait_ge(s_xld.h, xtok)
                    VE.reduce_sum(t_a[:, col], xr[bi][:], AX.X)
                    VE.reduce_max(t_b[:, col], xr[bi][:], AX.X)
                    VE.tensor_reduce(hx4[:, 0:1], xr[bi][:], AX.X, ALU.min)
                    VE.tensor_scalar_mul(p_m[:, col], t_a[:, col], 1.0 / D)
                    SE.wait_ge(s_xld.h, xtok)
                    SE.wait_ge(s_lnv.h, xsq_free)
                    sq_tok = s_sqd.inc(SE.square(xsq[:], xr[bi][:]), 1)
                    VE.wait_ge(s_sqd.h, sq_tok)
                    VE.reduce_sum(t_a[:, col], xsq[:], AX.X)
                    xsq_free = s_lnv.inc(VE.engine_nop(), 1)
                    VE.tensor_scalar_mul(t_a[:, col], t_a[:, col], 1.0 / D)
                    VE.tensor_mul(hx4[:, 1:2], p_m[:, col], p_m[:, col])
                    VE.tensor_sub(t_a[:, col], t_a[:, col], hx4[:, 1:2])
                    var_tok = s_lnv.inc(
                        VE.tensor_scalar_add(t_a[:, col], t_a[:, col], 1e-5), 1)
                    SE.wait_ge(s_lnv.h, var_tok)
                    sig_tok = s_sig.inc(SE.sqrt(p_sg[:, col], t_a[:, col]), 1)
                    VE.wait_ge(s_sig.h, sig_tok)
                    VE.reciprocal(p_rs[:, col], p_sg[:, col])
                    VE.tensor_sub(t_b[:, col], t_b[:, col], p_m[:, col])
                    VE.tensor_sub(hx4[:, 0:1], p_m[:, col], hx4[:, 0:1])
                    VE.tensor_max(t_b[:, col], t_b[:, col], hx4[:, 0:1])
                    VE.tensor_scalar_mul(hx4[:, 1:2], p_sg[:, col], 1e-5)
                    VE.tensor_max(t_b[:, col], t_b[:, col], hx4[:, 1:2])
                    VE.reciprocal(hx4[:, 2:3], t_b[:, col])
                    VE.tensor_scalar_mul(p_sq[:, col], hx4[:, 2:3], 127.0)
                    VE.tensor_mul(hx4[:, 3:4], t_b[:, col], p_rs[:, col])
                    VE.tensor_scalar_mul(p_al[:, col], hx4[:, 3:4], 1.0 / 254.0)
                    VE.tensor_mul(hx4[:, 1:2], p_m[:, col], p_rs[:, col])
                    VE.tensor_scalar_mul(p_nm[:, col], hx4[:, 1:2], -1.0)
                    VE.tensor_mul(hx4[:, 1:2], p_m[:, col], p_sq[:, col])
                    sml_tok = s_lnv.inc(VE.tensor_scalar_mul(
                        p_qb[:, col], hx4[:, 1:2], -1.0), 1)
                    SE.wait_ge(s_lnv.h, sml_tok)
                    SE.wait_ge(s_xnt.h, xn_free[bi])
                    xnb_tok = s_xnb.inc(SE.activation(
                        xnst[bi][:], xr[bi][:], AF.Identity,
                        bias=p_nm[:, col], scale=p_rs[:, col]), 1)
                    VE.tensor_scalar(qtf[:], xr[bi][:],
                                     p_sq[:, col], p_qb[:, col], ALU.mult, ALU.add)
                    VE.tensor_copy(qi0[:], qtf[:])
                    ve_q = VE._n
                    GE.wait_ge(VE._sem, ve_q)
                    if q_sig_prev[0]:
                        GE.wait_ge(SE._sem, q_sig_prev[0])
                    GE.tensor_tensor(qd[:], qtf[:], qi0[:], ALU.subtract)
                    ge_d = GE._n
                    SE.wait_ge(GE._sem, ge_d)
                    if q_cs_prev[0]:
                        SE.wait_ge(GE._sem, q_cs_prev[0])
                    SE.activation(qs1[:], qd[:], AF.Sign, bias=chf[:, 0:1], scale=1.0)
                    SE.activation(qs2[:], qd[:], AF.Sign, bias=chf[:, 1:2], scale=1.0)
                    q_sig_prev[0] = SE._n
                    GE.wait_ge(SE._sem, q_sig_prev[0])
                    if q_fin_prev[0]:
                        GE.wait_ge(VE._sem, q_fin_prev[0])
                    GE.tensor_add(qcs[:], qs1[:], qs2[:])
                    q_cs_prev[0] = GE._n
                    VE.wait_ge(GE._sem, q_cs_prev[0])
                    VE.wait_ge(s_qtt.h, qb_free[bi])
                    qbb_tok = s_qbb.inc(VE.scalar_tensor_tensor(
                        qbst[bi][:], qi0[:], 2.0, qcs[:], ALU.mult, ALU.add), 1)
                    q_fin_prev[0] = qbb_tok
                    xr_free[bi] = s_lnv.inc(VE.engine_nop(), 1)
                    slot = ti % 4
                    SY.wait_ge(s_xnb.h, xnb_tok)
                    SY.wait_ge(s_qkv.h, xnt_free[slot])
                    xnt_tok = s_xnt.inc(SY.dma_start_transpose(
                        bass.AP(xnt, slot * D, [[4 * D, 128], [128, 8], [1, 128]]),
                        bass.AP(xnst[bi], 0, [[D, 128], [128, 8], [1, 128]])), 16)
                    xn_free[bi] = xnt_tok
                    SY.wait_ge(s_qbb.h, qbb_tok)
                    SY.wait_ge(s_qsp.h, qt_free[bi])
                    qtt_tok = s_qtt.inc(SY.dma_start_transpose(
                        bass.AP(qts[bi], 0, [[D, 128], [128, 8], [1, 128]]),
                        bass.AP(qbst[bi], 0, [[D, 128], [128, 8], [1, 128]])), 16)
                    qb_free[bi] = qtt_tok
                    GE.wait_ge(s_qtt.h, qtt_tok)
                    qsp_tok = s_qsp.inc(GE.dma_start(
                        out=bass.AP(qt_sp, ti * D * TAU,
                                    [[128, 128], [128 * 128, 8], [1, 128]]),
                        in_=qts[bi][:]), 16)
                    qt_free[bi] = qsp_tok
                    return xnt_tok

                def f_qkv(si, xnt_toks):
                    nonlocal fla_prev_mid, qkv_bank_free
                    TE.wait_ge(s_cst.h, s_cst.n)
                    TE.wait_ge(s_msk.h, s_msk.n)
                    TE.wait_ge(s_bbc.h, s_bbc.n)
                    TE.wait_ge(s_xnt.h, max(xnt_toks))
                    if si > 0:
                        TE.wait_ge(s_fla.h, s_fla.n)
                    sl0 = (si * TPS) % 4
                    for r, dst in ((0, qfl), (1, kfl), (2, vfl)):
                        fla_reg = s_fla.n
                        for j in range(8):
                            m = r * 8 + j
                            TE.wait_ge(s_fme.h, qkv_bank_free)
                            for f in range(8):
                                rhs = bass.AP(xnt, sl0 * D + f * 128,
                                              [[4 * D, 128], [D, TPS], [1, 128]])
                                mm = TE.matmul(
                                    pq[:, 0:SUP],
                                    wq[:, f * 3072 + m * 128:f * 3072 + (m + 1) * 128],
                                    rhs, start=(f == 0), stop=(f == 7))
                            qkv_tok = s_qkv.inc(mm, 1)
                            SE.wait_ge(s_qkv.h, qkv_tok)
                            if r > 0 or si > 0:
                                SE.wait_ge(s_fla.h, fla_reg)
                            fme_tok[m] = s_fme.inc(SE.activation(
                                fmst[:, j * SUP:(j + 1) * SUP],
                                pq[:, 0:SUP],
                                AF.Identity, bias=bcol[:, m:m + 1], scale=1.0), 1)
                            qkv_bank_free = fme_tok[m]
                        SY.wait_ge(s_fme.h, fme_tok[r * 8 + 7])
                        if r == 0:
                            SY.wait_ge(s_mid.h, fla_prev_mid)
                            SY.wait_ge(s_smm.h, fla_prev_smm)
                            SY.wait_ge(s_kst.h, fla_prev_kst)
                            SY.wait_ge(s_vpr.h, fla_prev_vpr)
                        s_fla.inc(SY.dma_start(
                            out=bass.AP(dst, 0,
                                        [[16 * SUP, 64], [2 * SUP, 8], [1, SUP]]),
                            in_=bass.AP(fmst, 0,
                                        [[8 * SUP, 64], [SUP, 8], [1, SUP]])), 16)
                        s_fla.inc(SY.dma_start(
                            out=bass.AP(dst, SUP,
                                        [[16 * SUP, 64], [2 * SUP, 8], [1, SUP]]),
                            in_=bass.AP(fmst, 64 * 8 * SUP,
                                        [[8 * SUP, 64], [SUP, 8], [1, SUP]])), 16)
                    for tl in range(TPS):
                        xnt_free[(si * TPS + tl) % 4] = s_qkv.n

                def f_middle(ti, si):
                    nonlocal vev_prev, cev_prev, nrm_prev, aev_prev
                    nonlocal fla_prev_smm, fla_prev_kst, fla_prev_vpr
                    bi = ti % 2
                    tl = ti - si * TPS
                    fla_tok = s_fla.n
                    VE.wait_ge(s_fla.h, fla_tok)
                    VE.wait_ge(s_smm.h, kst_free[bi])
                    kst_tok = s_kst.inc(VE.tensor_copy(
                        kstk[bi][0:64, :],
                        bass.AP(kfl, tl * TAU,
                                [[16 * SUP, 64], [8, 16], [1, 8], [SUP, 16]])), 1)
                    GE.wait_ge(s_fla.h, fla_tok)
                    GE.wait_ge(s_vtr.h, vpr_free[bi])
                    vpr_tok = s_vpr.inc(GE.tensor_copy(
                        vpre[bi][0:64, :],
                        bass.AP(vfl, tl * TAU,
                                [[16 * SUP, 64], [8, 16], [1, 8], [SUP, 16]])), 1)
                    smm_t = [0] * 16
                    exp_t = [0] * 4
                    vtr_t = [0] * 16
                    cmm_t = [0] * 16
                    ctr_t = [0] * 16
                    nrm_t = [0] * 4
                    vev_t = [0, 0]
                    cev_t = [0, 0]
                    for half in range(2):
                        for b4 in (2 * half, 2 * half + 1):
                            qb_ = (b4 % 2) * 512
                            for g in range(4):
                                gi = b4 * 4 + g
                                if g == 0:
                                    TE.wait_ge(s_kst.h, kst_tok)
                                    TE.wait_ge(s_exp.h, exp_prev2[b4 % 2])
                                rhs = bass.AP(qfl, tl * TAU + gi * 8,
                                              [[16 * SUP, 73], [1, 8], [SUP, 16]])
                                mm = TE.matmul(ps_s[:, qb_ + g * 128:qb_ + (g + 1) * 128],
                                               kstk[bi][0:73, gi * 128:(gi + 1) * 128],
                                               rhs, start=True, stop=True)
                                smm_t[gi] = s_smm.inc(mm, 1)
                            SE.wait_ge(s_smm.h, smm_t[b4 * 4 + 3])
                            if b4 == 0:
                                SE.wait_ge(s_cmm.h, est_free[bi])
                            exp_t[b4] = s_exp.inc(SE.activation(
                                est[bi][:, b4 * 512:(b4 + 1) * 512],
                                ps_s[:, qb_:qb_ + 512],
                                AF.Exp, bias=0.0, scale=0.125), 1)
                            exp_prev2[b4 % 2] = exp_t[b4]
                        for g8 in range(8):
                            gi = half * 8 + g8
                            if g8 == 0:
                                TE.wait_ge(s_vpr.h, vpr_tok)
                                TE.wait_ge(s_vev.h, vev_t[0] if half == 1 else vev_prev)
                            tr = TE.transpose(ptr[:, gi * 64:gi * 64 + 64],
                                              vpre[bi][0:64, gi * 128:(gi + 1) * 128],
                                              sid[0:64, 0:64])
                            vtr_t[gi] = s_vtr.inc(tr, 1)
                        VE.wait_ge(s_vtr.h, vtr_t[half * 8 + 7])
                        if half == 0:
                            VE.wait_ge(s_cmm.h, vstk_free[bi])
                        vev_t[half] = s_vev.inc(VE.tensor_copy(
                            bass.AP(vstk[bi], half * 8 * 65,
                                    [[16 * 65, 128], [65, 8], [1, 64]]),
                            bass.AP(ptr, half * 512,
                                    [[1024, 128], [64, 8], [1, 64]])), 1)
                        for b4 in (2 * half, 2 * half + 1):
                            for g in range(4):
                                gi = b4 * 4 + g
                                if g == 0:
                                    TE.wait_ge(s_exp.h, exp_t[b4])
                                    TE.wait_ge(s_vev.h, vev_t[half])
                                    TE.wait_ge(s_nrm.h, nrm_prev)
                                mm = TE.matmul(pcx[:, g * 128:g * 128 + 65],
                                               est[bi][:, gi * 128:(gi + 1) * 128],
                                               vstk[bi][:, gi * 65:(gi + 1) * 65],
                                               start=True, stop=True)
                                cmm_t[gi] = s_cmm.inc(mm, 1)
                            VE.wait_ge(s_cmm.h, cmm_t[b4 * 4 + 3])
                            VE.wait_ge(s_nrm.h, nrm_prev)
                            rcp_tok = s_rcp.inc(VE.reciprocal(
                                rcp4[:], bass.AP(pcx, 64, [[512, 128], [128, 4]])), 1)
                            SE.wait_ge(s_rcp.h, rcp_tok)
                            if b4 == 0:
                                SE.wait_ge(s_ctr.h, ctxn_free[bi])
                            for g in range(4):
                                gi = b4 * 4 + g
                                nr = SE.activation(
                                    ctxn[bi][:, gi * 64:(gi + 1) * 64],
                                    pcx[:, g * 128:g * 128 + 64],
                                    AF.Identity, bias=0.0, scale=rcp4[:, g:g + 1])
                                s_nrm.inc(nr, 1)
                            nrm_t[b4] = s_nrm.n
                            nrm_prev = s_nrm.n
                        for g8 in range(8):
                            gi = half * 8 + g8
                            if g8 == 0:
                                TE.wait_ge(s_nrm.h, nrm_t[2 * half + 1])
                                TE.wait_ge(s_cev.h, cev_prev)
                            tr = TE.transpose(ptc[0:64, g8 * 128:(g8 + 1) * 128],
                                              ctxn[bi][:, gi * 64:(gi + 1) * 64],
                                              sid[:])
                            ctr_t[gi] = s_ctr.inc(tr, 1)
                        VE.wait_ge(s_ctr.h, ctr_t[half * 8 + 7])
                        if half == 0:
                            VE.wait_ge(s_wom.h, ctxt_free[bi])
                        cev_t[half] = s_cev.inc(VE.tensor_copy(
                            bass.AP(ctxt[bi], half * 8 * 8,
                                    [[16 * 128, 64], [8, 8], [1, 8], [128, 16]]),
                            bass.AP(ptc, 0, [[1024, 64], [128, 8], [16, 8], [1, 16]])), 1)
                        cev_prev = cev_t[half]
                    kst_free[bi] = smm_t[15]
                    vpr_free[bi] = vtr_t[15]
                    est_free[bi] = cmm_t[15]
                    vstk_free[bi] = cmm_t[15]
                    ctxn_free[bi] = ctr_t[15]
                    vev_prev = vev_t[1]
                    TE.wait_ge(s_cev.h, cev_t[1])
                    TE.wait_ge(s_cst.h, s_cst.n)
                    TE.wait_ge(VE._sem, aev_prev)
                    for h in range(16):
                        for eh in range(2):
                            mm = TE.matmul(
                                pwo[:, eh * 512:(eh + 1) * 512],
                                ctxt[bi][0:64, h * 128:(h + 1) * 128],
                                wo[:, h * 1024 + eh * 512:h * 1024 + (eh + 1) * 512],
                                start=(h == 0), stop=(h == 15))
                    wom_tok = s_wom.inc(mm, 1)
                    ctxt_free[bi] = wom_tok
                    VE.wait_ge(s_wom.h, wom_tok)
                    VE.wait_ge(s_asp.h, atst_free[bi])
                    VE.tensor_copy(atst[bi][:], pwo[:])
                    aev_tok = VE._n
                    aev_prev = aev_tok
                    GE.wait_ge(VE._sem, aev_tok)
                    atst_free[bi] = s_asp.inc(GE.dma_start(
                        out=at_sp.ap()[ti * TAU:(ti + 1) * TAU, :],
                        in_=atst[bi][:]), 16)
                    s_mid.inc(VE.engine_nop(), 1)
                    fla_prev_smm = smm_t[15]
                    fla_prev_kst = kst_tok
                    fla_prev_vpr = vpr_tok

                def f_wchunk(mi, ci, src_d):
                    bi = ci % 2
                    ncc = src_d.shape[1] // 512
                    rb, cb = ci // ncc, ci % ncc
                    SE.wait_ge(s_wab.h, wfs_free[bi])
                    ldt = s_wfl.inc(SE.dma_start(
                        out=wfs[bi],
                        in_=src_d.ap()[rb * 128:(rb + 1) * 128,
                                       cb * 512:(cb + 1) * 512]), 16)
                    SE.wait_ge(s_wfl.h, ldt)
                    SE.wait_ge(s_wab.h, s_wab.n)
                    abt = s_wap.inc(SE.activation(
                        wsc[:], wfs[bi], AF.Abs, bias=0.0, scale=1.0,
                        accum_out=hx4[:, 7:8]), 1)
                    VE.wait_ge(s_wap.h, abt)
                    wab_se_prev[0] = abt
                    wfs_free[bi] = s_wab.inc(VE.tensor_add(
                        wabs[:, mi:mi + 1], wabs[:, mi:mi + 1], hx4[:, 7:8]), 1)

                wsrc = [(0, wg_d), (1, wu_d), (2, wd_d)]
                wchunks = [(mi, ci, d) for (mi, d) in wsrc for ci in range(64)]
                wi = 0
                ln_toks = {}
                for tl in range(min(2 * TPS, NT)):
                    ln_toks[tl] = f_ln(tl)
                for si in range(NS):
                    f_qkv(si, [ln_toks[si * TPS + tl] for tl in range(TPS)])
                    for tl in range(TPS):
                        f_middle(si * TPS + tl, si)
                    fla_prev_mid = s_mid.n
                    if si + 2 < NS:
                        for tl in range(TPS):
                            ti2 = (si + 2) * TPS + tl
                            ln_toks[ti2] = f_ln(ti2)
                    n_w = 13 if si < NS - 1 else len(wchunks) - wi
                    for _ in range(n_w):
                        if wi < len(wchunks):
                            f_wchunk(*wchunks[wi]); wi += 1
                s_mum = sem("mum")
                TE.wait_ge(s_wab.h, s_wab.n)
                TE.wait_ge(s_nrm.h, s_nrm.n)
                for mi in range(3):
                    s_mum.inc(TE.matmul(pcx[0:1, mi:mi + 1], wabs[:, mi:mi + 1],
                                        onesc[:], start=True, stop=True), 1)
                SE.wait_ge(s_mum.h, s_mum.n)
                SE.copy(murow[:], pcx[0:1, 0:3])
                SE.mul(murow[:], murow[:], 1.0 / (D * IFF))
                se_tok = SE._n
                VE.wait_ge(SE._sem, se_tok)
                VE.tensor_scalar_max(murow[:], murow[:], 1e-5)
                VE.tensor_scalar_mul(murow[:], murow[:], 0.5)
                mur_tok = VE._n
                TE.wait_ge(VE._sem, mur_tok)
                for mi in range(3):
                    s_mum.inc(TE.matmul(pcx[:, 4 + mi:5 + mi], ones1[:],
                                        murow[0:1, mi:mi + 1], start=True, stop=True), 1)
                SE.wait_ge(s_mum.h, s_mum.n)
                SE.copy(mu3[:], pcx[:, 4:7])
                se_tok2 = SE._n
                VE.wait_ge(SE._sem, se_tok2)
                VE.tensor_scalar_mul(thr_n[:], mu3[:], -1.0)
                GE.wait_ge(s_qsp.h, s_qsp.n)
                GE.wait_ge(s_asp.h, s_asp.n)
                SY.wait_ge(s_fla.h, s_fla.n)

            barrier("A")

            esB = ExitStack()
            with esB:
                sbb = lambda n, sh, dt: esB.enter_context(nc.sbuf_tensor(n, sh, dt))
                wgt = sbb("wgt", [128, 8 * IFF], BF16)
                wut = sbb("wut", [128, 8 * IFF], BF16)
                s_twl = sem("twl")
                s_ts1 = sem("ts1")
                s_ts2 = sem("ts2")
                s_ter = sem("ter")

                with ExitStack() as esT:
                    st = lambda n, sh, dt: esT.enter_context(nc.sbuf_tensor(n, sh, dt))
                    twf = [st(f"twf{i}", [128, 2048], F32) for i in range(2)]
                    ts1 = [st(f"ts1_{i}", [128, 2048], BF16) for i in range(2)]
                    ts2 = [st(f"ts2_{i}", [128, 2048], BF16) for i in range(2)]
                    tst_free = [0, 0]

                    def f_tern(mi, src_d, dst):
                        for ck in range(16):
                            bi = ck % 2
                            ldt = s_twl.inc(SE.dma_start(
                                out=twf[bi][:],
                                in_=src_d.ap()[(ck // 2) * 128:(ck // 2) * 128 + 128,
                                               (ck % 2) * 2048:(ck % 2) * 2048 + 2048]), 16)
                            SE.wait_ge(s_twl.h, ldt)
                            SE.wait_ge(s_ter.h, tst_free[bi])
                            SE.activation(ts1[bi][:], twf[bi][:], AF.Sign,
                                          bias=thr_n[:, mi:mi + 1], scale=1.0)
                            t2 = s_ts2.inc(SE.activation(
                                ts2[bi][:], twf[bi][:], AF.Sign,
                                bias=mu3[:, mi:mi + 1], scale=1.0), 1)
                            VE.wait_ge(s_ts2.h, t2)
                            tt = s_ter.inc(VE.tensor_add(
                                dst[:, (ck // 2) * IFF + (ck % 2) * 2048:
                                    (ck // 2) * IFF + (ck % 2) * 2048 + 2048],
                                ts1[bi][:], ts2[bi][:]), 1)
                            tst_free[bi] = tt

                    f_tern(0, wg_d, wgt)
                    f_tern(1, wu_d, wut)

                barrier("T1")

                qtl = [sbb(f"qtl{i}", [128, D], BF16) for i in range(2)]
                gst = [sbb(f"gst{i}", [128, 1024], F32) for i in range(2)]
                xst = [sbb("xst0", [128, 1024], F32)] * 2
                hps = sbb("hps", [128, IFF], F32)
                htf = sbb("htf", [128, 1024], F32)
                hi0 = sbb("hi0", [128, 1024], mybir.dt.int32)
                hd = sbb("hd", [128, 1024], F32)
                hs1 = sbb("hs1", [128, 1024], BF16)
                hs2 = sbb("hs2", [128, 1024], BF16)
                hcs = sbb("hcs", [128, 1024], BF16)
                h2b = sbb("h2b", [128, IFF], BF16)
                h2ts = sbb("h2ts", [128, IFF], BF16)
                pg = [esB.enter_context(nc.psum_tensor(f"pg{i}", [128, 1024], F32))
                      for i in range(2)]
                pu = [esB.enter_context(nc.psum_tensor(f"pu{i}", [128, 1024], F32))
                      for i in range(2)]

                s_qtl = sem("qtl")
                s_gmm = sem("gmm")
                s_umm = sem("umm")
                s_gev = sem("gev")
                s_rl2 = sem("rl2")
                s_hpw = sem("hpw")
                s_hmx = sem("hmx")
                s_h2a = sem("h2a")
                s_h2b = sem("h2b")
                s_h2t = sem("h2t")
                s_h2s = sem("h2s")

                VE.tensor_scalar(t_a[:, :], p_al[:, :], mu3[:, 0:1], None, ALU.mult)
                VE.tensor_scalar(t_b[:, :], p_al[:, :], mu3[:, 1:2], None, ALU.mult)
                VE.tensor_mul(t_a[:, :], t_a[:, :], t_a[:, :])
                VE.tensor_mul(p_ct[:, :], t_a[:, :], t_b[:, :])
                ctd_tok = VE._n

                qtl_free = [0, 0]
                gst_free = [0, 0]
                xst_free = [0, 0]
                pg_free = [0, 0]
                pu_free = [0, 0]
                hps_free = 0
                hmx_free = 0
                hmx_t = [0, 0, 0, 0]
                h2b_free = 0
                h2t_free = 0
                h_i0_prev = [0]
                h_d_prev = [0]
                h_sig_prev = [0]
                h_cs_prev = [0]
                h_fin_prev = [0]

                for ti in range(NT):
                    bi = ti % 2
                    col = slice(ti, ti + 1)
                    SE.wait_ge(s_umm.h, qtl_free[bi])
                    qt_tok = s_qtl.inc(SE.dma_start(
                        out=qtl[bi][:],
                        in_=bass.AP(qt_sp, ti * D * TAU,
                                    [[128, 128], [128 * 128, 8], [1, 128]])), 16)
                    gmm_t = [0] * 4
                    umm_t = [0] * 4
                    for q in range(4):
                        qs = q % 2
                        TE.wait_ge(s_qtl.h, qt_tok)
                        TE.wait_ge(s_gev.h, pg_free[qs])
                        TE.wait_ge(s_hpw.h, pu_free[qs])
                        for f in range(8):
                            lhs = qtl[bi][:, f * 128:(f + 1) * 128]
                            for c2 in range(2):
                                mmg = TE.matmul(
                                    pg[qs][:, c2 * 512:(c2 + 1) * 512], lhs,
                                    wgt[:, f * IFF + q * 1024 + c2 * 512:
                                        f * IFF + q * 1024 + (c2 + 1) * 512],
                                    start=(f == 0), stop=(f == 7))
                            for c2 in range(2):
                                mmu = TE.matmul(
                                    pu[qs][:, c2 * 512:(c2 + 1) * 512], lhs,
                                    wut[:, f * IFF + q * 1024 + c2 * 512:
                                        f * IFF + q * 1024 + (c2 + 1) * 512],
                                    start=(f == 0), stop=(f == 7))
                        gmm_t[q] = s_gmm.inc(mmg, 1)
                        umm_t[q] = s_umm.inc(mmu, 1)
                        SE.wait_ge(s_gmm.h, gmm_t[q])
                        SE.wait_ge(s_rl2.h, gst_free[bi])
                        gev_tok = s_gev.inc(
                            SE.copy(gst[bi][:], pg[qs][:]), 1)
                        pg_free[qs] = gev_tok
                        VE.wait_ge(s_gev.h, gev_tok)
                        rl2_tok = s_rl2.inc(VE.scalar_tensor_tensor(
                            xst[bi][:], gst[bi][:], 0.0, gst[bi][:],
                            ALU.max, ALU.mult), 1)
                        gst_free[bi] = rl2_tok
                        VE.wait_ge(s_umm.h, umm_t[q])
                        if q == 0:
                            VE.wait_ge(SE._sem, hps_free)
                            VE.wait_ge(s_hmx.h, hmx_free)
                        hpw_tok = s_hpw.inc(VE.tensor_mul(
                            hps[:, q * 1024:(q + 1) * 1024], xst[bi][:], pu[qs][:]), 1)
                        xst_free[bi] = hpw_tok
                        pu_free[qs] = hpw_tok
                        hmx_t[q] = s_hmx.inc(VE.reduce_max(
                            hx4[:, q:q + 1], hps[:, q * 1024:(q + 1) * 1024],
                            AX.X, apply_absolute_value=True), 1)
                    VE.wait_ge(VE._sem, ctd_tok)
                    VE.reduce_max(hx4[:, 4:5], hx4[:, 0:4], AX.X)
                    VE.tensor_mul(hx4[:, 5:6], p_ct[:, col], hx4[:, 4:5])
                    VE.tensor_scalar_max(hx4[:, 5:6], hx4[:, 5:6], 1e-5)
                    VE.reciprocal(hx4[:, 6:7], hx4[:, 5:6])
                    VE.tensor_mul(hx4[:, 6:7], hx4[:, 6:7], p_ct[:, col])
                    VE.tensor_scalar_mul(p_sh[:, col], hx4[:, 6:7], 127.0)
                    shd_tok = s_hmx.inc(VE.tensor_scalar_mul(
                        p_dd[:, col], hx4[:, 5:6], 1.0 / 254.0), 1)
                    for hf in range(4):
                        cs2 = slice(hf * 1024, (hf + 1) * 1024)
                        SE.wait_ge(s_hmx.h, shd_tok)
                        if h_i0_prev[0]:
                            SE.wait_ge(VE._sem, h_i0_prev[0])
                        SE.activation(htf[:], hps[:, cs2], AF.Copy,
                                      bias=0.0, scale=p_sh[:, col])
                        se_t = SE._n
                        VE.wait_ge(SE._sem, se_t)
                        if h_d_prev[0]:
                            VE.wait_ge(GE._sem, h_d_prev[0])
                        VE.tensor_copy(hi0[:], htf[:])
                        h_i0_prev[0] = VE._n
                        GE.wait_ge(VE._sem, h_i0_prev[0])
                        if h_sig_prev[0]:
                            GE.wait_ge(SE._sem, h_sig_prev[0])
                        GE.tensor_tensor(hd[:], htf[:], hi0[:], ALU.subtract)
                        h_d_prev[0] = GE._n
                        SE.wait_ge(GE._sem, h_d_prev[0])
                        if h_cs_prev[0]:
                            SE.wait_ge(GE._sem, h_cs_prev[0])
                        SE.activation(hs1[:], hd[:], AF.Sign, bias=chf[:, 0:1], scale=1.0)
                        SE.activation(hs2[:], hd[:], AF.Sign, bias=chf[:, 1:2], scale=1.0)
                        h_sig_prev[0] = SE._n
                        GE.wait_ge(SE._sem, h_sig_prev[0])
                        if h_fin_prev[0]:
                            GE.wait_ge(VE._sem, h_fin_prev[0])
                        GE.tensor_add(hcs[:], hs1[:], hs2[:])
                        h_cs_prev[0] = GE._n
                        VE.wait_ge(GE._sem, h_cs_prev[0])
                        VE.wait_ge(s_h2t.h, h2b_free)
                        VE.scalar_tensor_tensor(h2b[:, cs2], hi0[:], 2.0,
                                                hcs[:], ALU.mult, ALU.add)
                        h_fin_prev[0] = VE._n
                    hps_free = se_t
                    h2b_tok = h_fin_prev[0]
                    SY.wait_ge(VE._sem, h2b_tok)
                    SY.wait_ge(s_h2s.h, h2t_free)
                    for hf in range(2):
                        h2t_tok = s_h2t.inc(SY.dma_start_transpose(
                            bass.AP(h2ts, hf * 2048, [[IFF, 128], [128, 16], [1, 128]]),
                            bass.AP(h2b, hf * 2048, [[IFF, 128], [128, 16], [1, 128]])), 16)
                    h2b_free = h2t_tok
                    GE.wait_ge(s_h2t.h, h2t_tok)
                    h2t_free = s_h2s.inc(GE.dma_start(
                        out=bass.AP(h2_sp, ti * IFF * TAU,
                                    [[128, 128], [128 * 128, 32], [1, 128]]),
                        in_=h2ts[:]), 16)
                    hmx_free = hmx_t[3]
                    qtl_free[bi] = umm_t[3]
                GE.wait_ge(s_h2s.h, s_h2s.n)

            barrier("B1")

            esC = ExitStack()
            with esC:
                sc = lambda n, sh, dt: esC.enter_context(nc.sbuf_tensor(n, sh, dt))
                wdt = sc("wdt", [128, 32 * 1024], BF16)
                s_twl = sem("twl")
                s_ts2 = sem("ts2")
                s_ter = sem("ter")
                with ExitStack() as esT:
                    st = lambda n, sh, dt: esT.enter_context(nc.sbuf_tensor(n, sh, dt))
                    twf = [st(f"dwf{i}", [128, 1024], F32) for i in range(2)]
                    ts1 = [st(f"ds1_{i}", [128, 1024], BF16) for i in range(2)]
                    ts2 = [st(f"ds2_{i}", [128, 1024], BF16) for i in range(2)]
                    tst_free = [s_ter.n, s_ter.n]
                    for ck in range(32):
                        bi = ck % 2
                        ldt = s_twl.inc(SE.dma_start(
                            out=twf[bi][:],
                            in_=wd_d.ap()[ck * 128:(ck + 1) * 128, :]), 16)
                        SE.wait_ge(s_twl.h, ldt)
                        SE.wait_ge(s_ter.h, tst_free[bi])
                        SE.activation(ts1[bi][:], twf[bi][:], AF.Sign,
                                      bias=thr_n[:, 2:3], scale=1.0)
                        t2 = s_ts2.inc(SE.activation(
                            ts2[bi][:], twf[bi][:], AF.Sign,
                            bias=mu3[:, 2:3], scale=1.0), 1)
                        VE.wait_ge(s_ts2.h, t2)
                        tt = s_ter.inc(VE.tensor_add(
                            wdt[:, ck * 1024:(ck + 1) * 1024],
                            ts1[bi][:], ts2[bi][:]), 1)
                        tst_free[bi] = tt

                barrier("T2")

                h2l = [sc(f"h2l{i}", [128, IFF], BF16) for i in range(2)]
                x2 = [sc(f"x2_{i}", [128, D], F32) for i in range(2)]
                at2 = [sc(f"at2_{i}", [128, D], F32) for i in range(2)]
                ost = [sc(f"ost{i}", [128, D], F32) for i in range(2)]
                o1b = sc("o1b", [128, D], F32)
                pd = [esC.enter_context(nc.psum_tensor(f"pd{i}", [128, 1024], F32))
                      for i in range(2)]

                s_h2l = sem("h2l")
                s_x2l = sem("x2l")
                s_a2l = sem("a2l")
                s_dmm = sem("dmm")
                s_och = sem("och")
                s_osp = sem("osp")

                VE.tensor_scalar(p_dd[:, :], p_dd[:, :], mu3[:, 2:3], None, ALU.mult)
                ddf_tok = VE._n

                h2l_free = [0, 0]
                x2_free = [0, 0]
                at2_free = [0, 0]
                ost_free = [0, 0]
                pd_free = [0, 0]

                for ti in range(NT):
                    bi = ti % 2
                    col = slice(ti, ti + 1)
                    SE.wait_ge(s_dmm.h, h2l_free[bi])
                    h2tok = s_h2l.inc(SE.dma_start(
                        out=h2l[bi][:],
                        in_=bass.AP(h2_sp, ti * IFF * TAU,
                                    [[128, 128], [128 * 128, 32], [1, 128]])), 16)
                    SE.wait_ge(s_och.h, x2_free[bi])
                    x2tok = s_x2l.inc(SE.dma_start(
                        out=x2[bi][:], in_=x_in.ap()[ti * TAU:(ti + 1) * TAU, :]), 16)
                    SE.wait_ge(s_och.h, at2_free[bi])
                    a2tok = s_a2l.inc(SE.dma_start(
                        out=at2[bi][:], in_=at_sp.ap()[ti * TAU:(ti + 1) * TAU, :]), 16)
                    qs = ti % 2
                    TE.wait_ge(s_h2l.h, h2tok)
                    TE.wait_ge(s_och.h, pd_free[qs])
                    for f2 in range(32):
                        for c2 in range(2):
                            mm = TE.matmul(
                                pd[qs][:, c2 * 512:(c2 + 1) * 512],
                                h2l[bi][:, f2 * 128:(f2 + 1) * 128],
                                wdt[:, f2 * 1024 + c2 * 512:f2 * 1024 + (c2 + 1) * 512],
                                start=(f2 == 0), stop=(f2 == 31))
                    dmm_tok = s_dmm.inc(mm, 1)
                    h2l_free[bi] = dmm_tok
                    VE.wait_ge(s_dmm.h, dmm_tok)
                    VE.wait_ge(s_x2l.h, x2tok)
                    VE.wait_ge(s_a2l.h, a2tok)
                    VE.wait_ge(VE._sem, ddf_tok)
                    VE.wait_ge(s_osp.h, ost_free[bi])
                    VE.scalar_tensor_tensor(o1b[:], pd[qs][:], p_dd[:, col],
                                            x2[bi][:], ALU.mult, ALU.add)
                    VE.tensor_add(o1b[:], o1b[:], at2[bi][:])
                    och_tok = s_och.inc(VE.tensor_add(ost[bi][:], o1b[:], bbc[:]), 1)
                    pd_free[qs] = och_tok
                    x2_free[bi] = och_tok
                    at2_free[bi] = och_tok
                    GE.wait_ge(s_och.h, och_tok)
                    ost_free[bi] = s_osp.inc(GE.dma_start(
                        out=out_d.ap()[ti * TAU:(ti + 1) * TAU, :],
                        in_=ost[bi][:]), 16)
                if DBG:
                    s_dbg = sem("dbg")
                    for i, t in enumerate((p_m, p_rs, p_sg, p_sq, p_al, p_ct,
                                           p_sh, p_dd, mu3, thr_n, wabs)):
                        w = min(t.shape[1], 64)
                        s_dbg.inc(GE.dma_start(
                            out=dbg_p.ap()[:, i * 64:i * 64 + w], in_=t[:, 0:w]), 16)
                    GE.wait_ge(s_dbg.h, s_dbg.n)
                GE.wait_ge(s_osp.h, s_osp.n)
                SE.wait_ge(s_osp.h, s_osp.n)

    return nc


def make_host_layouts(W_qkv, b_qkv):
    perm = np.zeros(3 * D, dtype=np.int64)
    for r in range(3):
        for j in range(8):
            for hh in range(2):
                h = 2 * j + hh
                cols = 192 * h + 64 * r + np.arange(64)
                m = r * 8 + j
                perm[m * 128 + hh * 64: m * 128 + hh * 64 + 64] = cols
    Wp = np.ascontiguousarray(np.asarray(W_qkv)[:, perm])
    bp = np.asarray(b_qkv)[perm]
    btile = np.ascontiguousarray(bp.reshape(24, 128).T).astype(np.float32)
    return Wp, btile


def make_masks():
    mq = np.zeros((9, 16 * SUP), dtype=BF)
    mq[0, :] = 32.0
    cols = np.arange(16 * SUP)
    for tp in range(8):
        mq[1 + tp, (cols % 8) == tp] = 32.0
    mk = np.zeros((9, 16 * TAU), dtype=BF)
    mk[0, :] = -32.0
    for tp in range(8):
        mk[1 + tp, ((cols[:16 * TAU] % 128) // 16) == tp] = 32.0
    return mq, mk


def kernel(L0_attn_x, L0_attn_W_qkv, L0_attn_b_qkv, L0_attn_W_out,
           L0_attn_b_out, L0_ffn_w_gate, L0_ffn_w_up, L0_ffn_w_down):
    x = np.ascontiguousarray(np.asarray(L0_attn_x, dtype=np.float32))
    B, S, _ = x.shape
    Wp, btile = make_host_layouts(L0_attn_W_qkv, L0_attn_b_qkv)
    mq, mk = make_masks()
    shared = {
        "wqkv": Wp.astype(np.float32),
        "btile": btile,
        "wout": np.ascontiguousarray(np.asarray(L0_attn_W_out, dtype=np.float32)),
        "bout": np.asarray(L0_attn_b_out, dtype=np.float32).reshape(1, D),
        "wg_t": np.ascontiguousarray(np.asarray(L0_ffn_w_gate, dtype=np.float32).T),
        "wu_t": np.ascontiguousarray(np.asarray(L0_ffn_w_up, dtype=np.float32).T),
        "wd_t": np.ascontiguousarray(np.asarray(L0_ffn_w_down, dtype=np.float32).T),
        "ident": np.eye(128, dtype=BF),
        "maskq": mq,
        "maskk": mk,
    }
    nc = build_program(S)
    in_maps = [dict(shared, x=np.ascontiguousarray(x[b])) for b in range(B)]
    res = run_bass_kernel_spmd(nc, in_maps, core_ids=list(range(B)))
    out = np.stack([np.asarray(res.results[b]["out"]) for b in range(B)], axis=0)
    return out.astype(np.float32)


# version 08
# speedup vs baseline: 1.0984x; 1.0984x over previous
import numpy as np
import ml_dtypes
from contextlib import ExitStack
import concourse.bass as bass
import concourse.mybir as mybir
from concourse.bass_utils import run_bass_kernel_spmd

F32 = mybir.dt.float32
BF16 = mybir.dt.bfloat16
F16 = mybir.dt.float16
I16 = mybir.dt.int16
AF = mybir.ActivationFunctionType
ALU = mybir.AluOpType
AX = mybir.AxisListType
BF = ml_dtypes.bfloat16

D = 1024
NH = 16
IFF = 4096
TAU = 128
SUP = 256
TPS = SUP // TAU
MQ = 1536.0
N_CORES = 8
DBG = False


class Sem:
    __slots__ = ("h", "n")

    def __init__(self, es, nc, name):
        self.h = es.enter_context(nc.semaphore(name))
        self.n = 0

    def inc(self, instr, k=1):
        instr.then_inc(self.h, k)
        self.n += k
        return self.n


_WRAP_OPS = {
    "tensor_copy", "tensor_tensor", "tensor_add", "tensor_sub", "tensor_mul",
    "tensor_max", "tensor_scalar", "tensor_single_scalar", "tensor_scalar_add",
    "tensor_scalar_sub", "tensor_scalar_mul", "tensor_scalar_max",
    "tensor_scalar_min", "tensor_relu", "tensor_reduce", "reduce_sum",
    "reduce_max", "scalar_tensor_tensor", "reciprocal", "memset", "transpose",
    "activation", "square", "sqrt", "copy", "mul", "add", "sign", "select",
    "bn_stats", "bn_aggr", "engine_nop", "iota",
}


class EngSem:
    __slots__ = ("e",)

    def __init__(self, e):
        self.e = e

    @property
    def h(self):
        return self.e._sem

    @property
    def n(self):
        return self.e._n

    def inc(self, instr, k=1):
        return self.e._n


class SerialEng:

    def __init__(self, eng, nc, es, name):
        self._eng = eng
        self._sem = es.enter_context(nc.semaphore("ser_" + name))
        self._n = 0
        self._last = {}

    def _tnames(self, v, out):
        if isinstance(v, bass.AP):
            out.append(v.tensor.name)

    def __getattr__(self, name):
        fn = getattr(self._eng, name)
        if name not in _WRAP_OPS:
            return fn

        def wrapped(*a, **k):
            names = []
            for v in a:
                self._tnames(v, names)
            for v in k.values():
                self._tnames(v, names)
            need = 0
            for t in names:
                need = max(need, self._last.get(t, 0))
            if need:
                self._eng.wait_ge(self._sem, need)
            instr = fn(*a, **k)
            instr.then_inc(self._sem, 1)
            self._n += 1
            wtoks = []
            for v in list(a[:1]) + [k[x] for x in ("accum_out", "out") if x in k]:
                if isinstance(v, bass.AP):
                    self._last[v.tensor.name] = self._n
            if a and isinstance(a[0], bass.AP):
                self._last[a[0].tensor.name] = self._n
            return instr

        return wrapped


def build_program(S):
    NT = S // TAU
    NS = S // SUP
    nc = bass.Bass("TRN2", target_bir_lowering=False,
                   detect_race_conditions=False)

    x_in = nc.dram_tensor("x", [S, D], F32, kind="ExternalInput")
    wqkv_d = nc.dram_tensor("wqkv", [D, 3 * D], F32, kind="ExternalInput")
    btile_d = nc.dram_tensor("btile", [128, 24], F32, kind="ExternalInput")
    wout_d = nc.dram_tensor("wout", [D, D], F32, kind="ExternalInput")
    bout_d = nc.dram_tensor("bout", [1, D], F32, kind="ExternalInput")
    wg_d = nc.dram_tensor("wg_t", [D, IFF], F32, kind="ExternalInput")
    wu_d = nc.dram_tensor("wu_t", [D, IFF], F32, kind="ExternalInput")
    wd_d = nc.dram_tensor("wd_t", [IFF, D], F32, kind="ExternalInput")
    id_d = nc.dram_tensor("ident", [128, 128], BF16, kind="ExternalInput")
    mq_d = nc.dram_tensor("maskq", [9, 16 * SUP], BF16, kind="ExternalInput")
    mk_d = nc.dram_tensor("maskk", [9, 16 * TAU], BF16, kind="ExternalInput")
    out_d = nc.dram_tensor("out", [S, D], F32, kind="ExternalOutput")
    kind_dbg = {"kind": "ExternalOutput"} if DBG else {}
    qt_sp = nc.dram_tensor("qt_sp", [NT * D * TAU], BF16, **kind_dbg)
    at_sp = nc.dram_tensor("at_sp", [S, D], F32, **kind_dbg)
    h2_sp = nc.dram_tensor("h2_sp", [NT * IFF * TAU], BF16, **kind_dbg)
    if DBG:
        dbg_p = nc.dram_tensor("dbg_p", [128, 1024], F32, kind="ExternalOutput")

    es = ExitStack()
    with es:
        TE, SY = nc.tensor, nc.sync
        VE = SerialEng(nc.vector, nc, es, "ve")
        SE = SerialEng(nc.scalar, nc, es, "se")
        GE = SerialEng(nc.gpsimd, nc, es, "ge")
        sb = lambda n, sh, dt: es.enter_context(nc.sbuf_tensor(n, sh, dt))
        sid = sb("sid", [128, 128], BF16)
        p_m = sb("p_m", [128, NT], F32)
        p_rs = sb("p_rs", [128, NT], F32)
        p_sg = sb("p_sg", [128, NT], F32)
        p_sq = sb("p_sq", [128, NT], F32)
        p_qb = sb("p_qb", [128, NT], F32)
        p_nm = sb("p_nm", [128, NT], F32)
        p_al = sb("p_al", [128, NT], F32)
        p_ct = sb("p_ct", [128, NT], F32)
        p_sh = sb("p_sh", [128, NT], F32)
        p_dd = sb("p_dd", [128, NT], F32)
        t_a = sb("t_a", [128, NT], F32)
        t_b = sb("t_b", [128, NT], F32)
        p_mx = sb("p_mx", [128, NT], F32)
        p_mn = sb("p_mn", [128, NT], F32)
        bbc = sb("bbc", [128, D], F32)
        brow = sb("brow", [1, D], F32)
        ones1 = sb("ones1", [1, 128], F32)
        onesc = sb("onesc", [128, 1], F32)
        bcol = sb("bcol", [128, 24], F32)
        wabs = sb("wabs", [128, 3], F32)
        mu3 = sb("mu3", [128, 3], F32)
        thr_n = sb("thr_n", [128, 3], F32)
        murow = sb("murow", [1, 3], F32)
        hx4 = sb("hx4", [128, 8], F32)
        rcp4 = sb("rcp4", [128, 4], F32)
        chf = sb("chf", [128, 2], F32)

        SEMS = {}
        _VE_SEMS = {"lnv", "qbb", "kst", "vev", "rcp", "cev", "mid", "wab",
                    "ter", "rl2", "hpw", "hmx", "h2b", "och", "ini"}
        _SE_SEMS = {"sqd", "sig", "xnb", "fme", "exp", "nrm", "aev", "wap",
                    "ts2", "gev", "h2a", "bbc"}
        _GE_SEMS = {"vpr"}

        def sem(name):
            if name not in SEMS:
                if name in _VE_SEMS:
                    SEMS[name] = EngSem(VE)
                elif name in _SE_SEMS:
                    SEMS[name] = EngSem(SE)
                elif name in _GE_SEMS:
                    SEMS[name] = EngSem(GE)
                else:
                    SEMS[name] = Sem(es, nc, name)
            return SEMS[name]

        block = es.enter_context(nc.Block())

        def barrier(tag):
            b = sem("bar_" + tag)
            for eng in (TE, VE, SE, GE, SY):
                b.inc(eng.nop(), 1)
            for eng in (TE, VE, SE, GE, SY):
                eng.wait_ge(b.h, b.n)

        @block.sync
        def _(_q):
            esA = ExitStack()
            with esA:
                sa = lambda n, sh, dt: esA.enter_context(nc.sbuf_tensor(n, sh, dt))
                wq = sa("wq", [128, 8 * 3072], BF16)
                wo = sa("wo", [64, 16 * 1024], BF16)
                xr = [sa(f"xr{i}", [128, D], F32) for i in range(2)]
                xsq = sa("xsq", [128, D], F32)
                xnst = [sa(f"xnst{i}", [128, D], BF16) for i in range(2)]
                qtf = sa("qtf", [128, D], F32)
                qi0 = sa("qi0", [128, D], mybir.dt.int32)
                qd = xsq
                qs1 = sa("qs1", [128, D], BF16)
                qs2 = sa("qs2", [128, D], BF16)
                qcs = sa("qcs", [128, D], BF16)
                qbst = [sa(f"qbst{i}", [128, D], BF16) for i in range(2)]
                qts = [sa(f"qts{i}", [128, D], BF16) for i in range(2)]
                xnt = sa("xnt", [128, 4 * D], BF16)
                fmst = sa("fmst", [128, 8 * SUP], BF16)
                qfl = sa("qfl", [128, 16 * SUP], BF16)
                kfl = sa("kfl", [128, 16 * SUP], BF16)
                vfl = sa("vfl", [128, 16 * SUP], BF16)
                kstk = [sa("kstk0", [128, 16 * TAU], BF16)] * 2
                vpre = [sa(f"vpre{i}", [128, 16 * TAU], BF16) for i in range(2)]
                est = [sa("est0", [128, 16 * TAU], BF16)] * 2
                vstk = [sa("vstk0", [128, 16 * 65], BF16)] * 2
                ctxn = [sa(f"ctxn{i}", [128, 16 * 64], BF16) for i in range(2)]
                ctxt = [sa(f"ctxt{i}", [128, 16 * 128], BF16) for i in range(2)]
                atst = [sa("atst0", [128, D], F32)] * 2
                wfs_t = [sa(f"wfs{i}", [128, 512], F32) for i in range(2)]
                wfs = [t[:] for t in wfs_t]
                wneg = sa("wneg", [128, 512], F32)
                wacc = sa("wacc", [128, 512], F32)

                pq = esA.enter_context(nc.psum_tensor("pq", [128, 512], F32))
                pwo = esA.enter_context(nc.psum_tensor("pwo", [128, 1024], F32))
                ps_s = esA.enter_context(nc.psum_tensor("ps_s", [128, 1024], F32))
                pcx = esA.enter_context(nc.psum_tensor("pcx", [128, 512], F32))
                ptr = esA.enter_context(nc.psum_tensor("ptr", [128, 1024], BF16))
                ptc = esA.enter_context(nc.psum_tensor("ptc", [128, 1024], BF16))

                s_w0 = sem("w0")
                s_cst = sem("cst")
                s_msk = sem("msk")
                s_ini = sem("ini")
                s_bbc = sem("bbc")
                s_w0.inc(SY.dma_start(out=sid[:], in_=id_d.ap()), 16)
                s_w0.inc(SY.dma_start(out=brow[:], in_=bout_d.ap()), 16)
                s_w0.inc(SY.dma_start(out=bcol[:], in_=btile_d.ap()), 16)
                s_msk.inc(SY.dma_start(out=qfl[64:73, :], in_=mq_d.ap()), 16)
                s_msk.inc(SY.dma_start(out=kstk[0][64:73, :], in_=mk_d.ap()), 16)
                for c in range(8):
                    s_cst.inc(GE.dma_start(
                        out=wq[:, c * 3072:(c + 1) * 3072],
                        in_=wqkv_d.ap()[c * 128:(c + 1) * 128, :]), 16)
                for h in range(16):
                    s_cst.inc(GE.dma_start(
                        out=wo[:, h * 1024:(h + 1) * 1024],
                        in_=wout_d.ap()[h * 64:(h + 1) * 64, :]), 16)
                VE.memset(ones1[:], 1.0)
                VE.memset(onesc[:], 1.0)
                VE.memset(wabs[:], 0.0)
                VE.memset(chf[:, 0:1], -0.5)
                VE.memset(chf[:, 1:2], 0.5)
                VE.memset(bass.AP(vstk[0], 64, [[16 * 65, 128], [65, 16], [1, 1]]), 1.0)
                s_ini.inc(VE.engine_nop(), 1)
                TE.wait_ge(s_w0.h, s_w0.n)
                TE.wait_ge(s_ini.h, s_ini.n)
                s_mmb = sem("mmb")
                TE.matmul(pq[:], ones1[:], brow[0:1, 0:512], start=True, stop=True)
                s_mmb.inc(TE.matmul(ps_s[:, 0:512], ones1[:], brow[0:1, 512:1024],
                                    start=True, stop=True), 1)
                SE.wait_ge(s_mmb.h, s_mmb.n)
                SE.copy(bbc[:, 0:512], pq[:])
                s_bbc.inc(SE.copy(bbc[:, 512:1024], ps_s[:, 0:512]), 1)
                qkv_bank_free = 0
                if DBG:
                    barrier("pre")

                s_xld = sem("xld")
                s_lnv = sem("lnv")
                s_sqd = sem("sqd")
                s_sig = sem("sig")
                s_xnb = sem("xnb")
                s_qbb = sem("qbb")
                s_xnt = sem("xnt")
                s_qtt = sem("qtt")
                s_qsp = sem("qsp")
                s_qkv = sem("qkv")
                s_fme = sem("fme")
                s_fla = sem("fla")
                s_kst = sem("kst")
                s_vpr = sem("vpr")
                s_smm = sem("smm")
                s_exp = sem("exp")
                s_vtr = sem("vtr")
                s_vev = sem("vev")
                s_cmm = sem("cmm")
                s_rcp = sem("rcp")
                s_nrm = sem("nrm")
                s_ctr = sem("ctr")
                s_cev = sem("cev")
                s_wom = sem("wom")
                s_aev = sem("aev")
                s_asp = sem("asp")
                s_mid = sem("mid")
                s_wfl = sem("wfl")
                s_wab = sem("wab")
                s_wap = sem("wap")

                xr_free = [0, 0]
                xn_free = [0, 0]
                qb_free = [0, 0]
                qt_free = [0, 0]
                xnt_free = [0, 0, 0, 0]
                xsq_free = 0
                kst_free = [0, 0]
                vpr_free = [0, 0]
                est_free = [0, 0]
                vstk_free = [0, 0]
                ctxn_free = [0, 0]
                ctxt_free = [0, 0]
                atst_free = [0, 0]
                wfs_free = [0, 0]
                fla_prev_mid = 0
                fla_prev_smm = 0
                fla_prev_kst = 0
                fla_prev_vpr = 0
                fme_tok = [0] * 24
                q_sig_prev = [0]
                q_cs_prev = [0]
                q_fin_prev = [0]
                wab_se_prev = [0]
                exp_prev2 = [0, 0]
                vev_prev = 0
                cev_prev = 0
                nrm_prev = 0
                aev_prev = 0

                def f_ln_pre(ti):
                    nonlocal xsq_free
                    bi = ti % 2
                    col = slice(ti, ti + 1)
                    SE.wait_ge(s_lnv.h, xr_free[bi])
                    xtok = s_xld.inc(SE.dma_start(
                        out=xr[bi][:], in_=x_in.ap()[ti * TAU:(ti + 1) * TAU, :]), 16)
                    VE.wait_ge(s_xld.h, xtok)
                    VE.reduce_sum(t_a[:, col], xr[bi][:], AX.X)
                    VE.reduce_max(p_mx[:, col], xr[bi][:], AX.X)
                    VE.tensor_reduce(p_mn[:, col], xr[bi][:], AX.X, ALU.min)
                    SE.wait_ge(s_xld.h, xtok)
                    SE.wait_ge(s_lnv.h, xsq_free)
                    sq_tok = s_sqd.inc(SE.square(xsq[:], xr[bi][:]), 1)
                    VE.wait_ge(s_sqd.h, sq_tok)
                    VE.reduce_sum(t_b[:, col], xsq[:], AX.X)
                    xsq_free = s_lnv.inc(VE.engine_nop(), 1)
                    return xtok

                def f_ln_smalls(cols):
                    c = cols
                    VE.tensor_scalar_mul(p_m[:, c], t_a[:, c], 1.0 / D)
                    VE.tensor_scalar_mul(t_b[:, c], t_b[:, c], 1.0 / D)
                    VE.tensor_mul(t_a[:, c], p_m[:, c], p_m[:, c])
                    VE.tensor_sub(t_b[:, c], t_b[:, c], t_a[:, c])
                    var_tok = s_lnv.inc(
                        VE.tensor_scalar_add(t_b[:, c], t_b[:, c], 1e-5), 1)
                    SE.wait_ge(s_lnv.h, var_tok)
                    sig_tok = s_sig.inc(SE.sqrt(p_sg[:, c], t_b[:, c]), 1)
                    VE.wait_ge(s_sig.h, sig_tok)
                    VE.reciprocal(p_rs[:, c], p_sg[:, c])
                    VE.tensor_sub(p_mx[:, c], p_mx[:, c], p_m[:, c])
                    VE.tensor_sub(p_mn[:, c], p_m[:, c], p_mn[:, c])
                    VE.tensor_max(p_mx[:, c], p_mx[:, c], p_mn[:, c])
                    VE.tensor_scalar_mul(p_mn[:, c], p_sg[:, c], 1e-5)
                    VE.tensor_max(p_mx[:, c], p_mx[:, c], p_mn[:, c])
                    VE.reciprocal(p_sq[:, c], p_mx[:, c])
                    VE.tensor_scalar_mul(p_sq[:, c], p_sq[:, c], 127.0)
                    VE.tensor_mul(p_al[:, c], p_mx[:, c], p_rs[:, c])
                    VE.tensor_scalar_mul(p_al[:, c], p_al[:, c], 1.0 / 254.0)
                    VE.tensor_mul(p_nm[:, c], p_m[:, c], p_rs[:, c])
                    VE.tensor_scalar_mul(p_nm[:, c], p_nm[:, c], -1.0)
                    VE.tensor_mul(p_qb[:, c], p_m[:, c], p_sq[:, c])
                    sml_tok = s_lnv.inc(VE.tensor_scalar_mul(
                        p_qb[:, c], p_qb[:, c], -1.0), 1)
                    return sml_tok

                def f_ln(ti, sml_tok):
                    bi = ti % 2
                    col = slice(ti, ti + 1)
                    SE.wait_ge(s_lnv.h, sml_tok)
                    SE.wait_ge(s_xnt.h, xn_free[bi])
                    xnb_tok = s_xnb.inc(SE.activation(
                        xnst[bi][:], xr[bi][:], AF.Identity,
                        bias=p_nm[:, col], scale=p_rs[:, col]), 1)
                    VE.tensor_scalar(qtf[:], xr[bi][:],
                                     p_sq[:, col], p_qb[:, col], ALU.mult, ALU.add)
                    VE.tensor_copy(qi0[:], qtf[:])
                    ve_q = VE._n
                    GE.wait_ge(VE._sem, ve_q)
                    if q_sig_prev[0]:
                        GE.wait_ge(SE._sem, q_sig_prev[0])
                    GE.tensor_tensor(qd[:], qtf[:], qi0[:], ALU.subtract)
                    ge_d = GE._n
                    SE.wait_ge(GE._sem, ge_d)
                    if q_cs_prev[0]:
                        SE.wait_ge(GE._sem, q_cs_prev[0])
                    SE.activation(qs1[:], qd[:], AF.Sign, bias=chf[:, 0:1], scale=1.0)
                    SE.activation(qs2[:], qd[:], AF.Sign, bias=chf[:, 1:2], scale=1.0)
                    q_sig_prev[0] = SE._n
                    GE.wait_ge(SE._sem, q_sig_prev[0])
                    if q_fin_prev[0]:
                        GE.wait_ge(VE._sem, q_fin_prev[0])
                    GE.tensor_add(qcs[:], qs1[:], qs2[:])
                    q_cs_prev[0] = GE._n
                    VE.wait_ge(GE._sem, q_cs_prev[0])
                    VE.wait_ge(s_qtt.h, qb_free[bi])
                    qbb_tok = s_qbb.inc(VE.scalar_tensor_tensor(
                        qbst[bi][:], qi0[:], 2.0, qcs[:], ALU.mult, ALU.add), 1)
                    q_fin_prev[0] = qbb_tok
                    xr_free[bi] = s_lnv.inc(VE.engine_nop(), 1)
                    slot = ti % 4
                    SY.wait_ge(s_xnb.h, xnb_tok)
                    SY.wait_ge(s_qkv.h, xnt_free[slot])
                    xnt_tok = s_xnt.inc(SY.dma_start_transpose(
                        bass.AP(xnt, slot * D, [[4 * D, 128], [128, 8], [1, 128]]),
                        bass.AP(xnst[bi], 0, [[D, 128], [128, 8], [1, 128]])), 16)
                    xn_free[bi] = xnt_tok
                    SY.wait_ge(s_qbb.h, qbb_tok)
                    SY.wait_ge(s_qsp.h, qt_free[bi])
                    qtt_tok = s_qtt.inc(SY.dma_start_transpose(
                        bass.AP(qts[bi], 0, [[D, 128], [128, 8], [1, 128]]),
                        bass.AP(qbst[bi], 0, [[D, 128], [128, 8], [1, 128]])), 16)
                    qb_free[bi] = qtt_tok
                    GE.wait_ge(s_qtt.h, qtt_tok)
                    qsp_tok = s_qsp.inc(GE.dma_start(
                        out=bass.AP(qt_sp, ti * D * TAU,
                                    [[128, 128], [128 * 128, 8], [1, 128]]),
                        in_=qts[bi][:]), 16)
                    qt_free[bi] = qsp_tok
                    return xnt_tok

                def f_qkv(si, xnt_toks):
                    nonlocal fla_prev_mid, qkv_bank_free
                    TE.wait_ge(s_cst.h, s_cst.n)
                    TE.wait_ge(s_msk.h, s_msk.n)
                    TE.wait_ge(s_bbc.h, s_bbc.n)
                    TE.wait_ge(s_xnt.h, max(xnt_toks))
                    if si > 0:
                        TE.wait_ge(s_fla.h, s_fla.n)
                    sl0 = (si * TPS) % 4
                    for r, dst in ((0, qfl), (1, kfl), (2, vfl)):
                        fla_reg = s_fla.n
                        for j in range(8):
                            m = r * 8 + j
                            TE.wait_ge(s_fme.h, qkv_bank_free)
                            for f in range(8):
                                rhs = bass.AP(xnt, sl0 * D + f * 128,
                                              [[4 * D, 128], [D, TPS], [1, 128]])
                                mm = TE.matmul(
                                    pq[:, 0:SUP],
                                    wq[:, f * 3072 + m * 128:f * 3072 + (m + 1) * 128],
                                    rhs, start=(f == 0), stop=(f == 7))
                            qkv_tok = s_qkv.inc(mm, 1)
                            SE.wait_ge(s_qkv.h, qkv_tok)
                            if r > 0 or si > 0:
                                SE.wait_ge(s_fla.h, fla_reg)
                            fme_tok[m] = s_fme.inc(SE.activation(
                                fmst[:, j * SUP:(j + 1) * SUP],
                                pq[:, 0:SUP],
                                AF.Identity, bias=bcol[:, m:m + 1], scale=1.0), 1)
                            qkv_bank_free = fme_tok[m]
                        SY.wait_ge(s_fme.h, fme_tok[r * 8 + 7])
                        if r == 0:
                            SY.wait_ge(s_mid.h, fla_prev_mid)
                            SY.wait_ge(s_smm.h, fla_prev_smm)
                            SY.wait_ge(s_kst.h, fla_prev_kst)
                            SY.wait_ge(s_vpr.h, fla_prev_vpr)
                        s_fla.inc(SY.dma_start(
                            out=bass.AP(dst, 0,
                                        [[16 * SUP, 64], [2 * SUP, 8], [1, SUP]]),
                            in_=bass.AP(fmst, 0,
                                        [[8 * SUP, 64], [SUP, 8], [1, SUP]])), 16)
                        s_fla.inc(SY.dma_start(
                            out=bass.AP(dst, SUP,
                                        [[16 * SUP, 64], [2 * SUP, 8], [1, SUP]]),
                            in_=bass.AP(fmst, 64 * 8 * SUP,
                                        [[8 * SUP, 64], [SUP, 8], [1, SUP]])), 16)
                    for tl in range(TPS):
                        xnt_free[(si * TPS + tl) % 4] = s_qkv.n

                def f_middle(ti, si):
                    nonlocal vev_prev, cev_prev, nrm_prev, aev_prev
                    nonlocal fla_prev_smm, fla_prev_kst, fla_prev_vpr
                    bi = ti % 2
                    tl = ti - si * TPS
                    fla_tok = s_fla.n
                    VE.wait_ge(s_fla.h, fla_tok)
                    VE.wait_ge(s_smm.h, kst_free[bi])
                    kst_tok = s_kst.inc(VE.tensor_copy(
                        kstk[bi][0:64, :],
                        bass.AP(kfl, tl * TAU,
                                [[16 * SUP, 64], [8, 16], [1, 8], [SUP, 16]])), 1)
                    GE.wait_ge(s_fla.h, fla_tok)
                    GE.wait_ge(s_vtr.h, vpr_free[bi])
                    vpr_tok = s_vpr.inc(GE.tensor_copy(
                        vpre[bi][0:64, :],
                        bass.AP(vfl, tl * TAU,
                                [[16 * SUP, 64], [8, 16], [1, 8], [SUP, 16]])), 1)
                    smm_t = [0] * 16
                    exp_t = [0] * 4
                    vtr_t = [0] * 16
                    cmm_t = [0] * 16
                    ctr_t = [0] * 16
                    nrm_t = [0] * 4
                    vev_t = [0, 0]
                    cev_t = [0, 0]
                    for half in range(2):
                        for b4 in (2 * half, 2 * half + 1):
                            qb_ = (b4 % 2) * 512
                            for g in range(4):
                                gi = b4 * 4 + g
                                if g == 0:
                                    TE.wait_ge(s_kst.h, kst_tok)
                                    TE.wait_ge(s_exp.h, exp_prev2[b4 % 2])
                                rhs = bass.AP(qfl, tl * TAU + gi * 8,
                                              [[16 * SUP, 73], [1, 8], [SUP, 16]])
                                mm = TE.matmul(ps_s[:, qb_ + g * 128:qb_ + (g + 1) * 128],
                                               kstk[bi][0:73, gi * 128:(gi + 1) * 128],
                                               rhs, start=True, stop=True)
                                smm_t[gi] = s_smm.inc(mm, 1)
                            SE.wait_ge(s_smm.h, smm_t[b4 * 4 + 3])
                            if b4 == 0:
                                SE.wait_ge(s_cmm.h, est_free[bi])
                            exp_t[b4] = s_exp.inc(SE.activation(
                                est[bi][:, b4 * 512:(b4 + 1) * 512],
                                ps_s[:, qb_:qb_ + 512],
                                AF.Exp, bias=0.0, scale=0.125), 1)
                            exp_prev2[b4 % 2] = exp_t[b4]
                        for g8 in range(8):
                            gi = half * 8 + g8
                            if g8 == 0:
                                TE.wait_ge(s_vpr.h, vpr_tok)
                                TE.wait_ge(s_vev.h, vev_t[0] if half == 1 else vev_prev)
                            tr = TE.transpose(ptr[:, gi * 64:gi * 64 + 64],
                                              vpre[bi][0:64, gi * 128:(gi + 1) * 128],
                                              sid[0:64, 0:64])
                            vtr_t[gi] = s_vtr.inc(tr, 1)
                        VE.wait_ge(s_vtr.h, vtr_t[half * 8 + 7])
                        if half == 0:
                            VE.wait_ge(s_cmm.h, vstk_free[bi])
                        vev_t[half] = s_vev.inc(VE.tensor_copy(
                            bass.AP(vstk[bi], half * 8 * 65,
                                    [[16 * 65, 128], [65, 8], [1, 64]]),
                            bass.AP(ptr, half * 512,
                                    [[1024, 128], [64, 8], [1, 64]])), 1)
                        for b4 in (2 * half, 2 * half + 1):
                            for g in range(4):
                                gi = b4 * 4 + g
                                if g == 0:
                                    TE.wait_ge(s_exp.h, exp_t[b4])
                                    TE.wait_ge(s_vev.h, vev_t[half])
                                    TE.wait_ge(s_nrm.h, nrm_prev)
                                mm = TE.matmul(pcx[:, g * 128:g * 128 + 65],
                                               est[bi][:, gi * 128:(gi + 1) * 128],
                                               vstk[bi][:, gi * 65:(gi + 1) * 65],
                                               start=True, stop=True)
                                cmm_t[gi] = s_cmm.inc(mm, 1)
                            VE.wait_ge(s_cmm.h, cmm_t[b4 * 4 + 3])
                            VE.wait_ge(s_nrm.h, nrm_prev)
                            rcp_tok = s_rcp.inc(VE.reciprocal(
                                rcp4[:], bass.AP(pcx, 64, [[512, 128], [128, 4]])), 1)
                            SE.wait_ge(s_rcp.h, rcp_tok)
                            if b4 == 0:
                                SE.wait_ge(s_ctr.h, ctxn_free[bi])
                            for g in range(4):
                                gi = b4 * 4 + g
                                nr = SE.activation(
                                    ctxn[bi][:, gi * 64:(gi + 1) * 64],
                                    pcx[:, g * 128:g * 128 + 64],
                                    AF.Identity, bias=0.0, scale=rcp4[:, g:g + 1])
                                s_nrm.inc(nr, 1)
                            nrm_t[b4] = s_nrm.n
                            nrm_prev = s_nrm.n
                        for g8 in range(8):
                            gi = half * 8 + g8
                            if g8 == 0:
                                TE.wait_ge(s_nrm.h, nrm_t[2 * half + 1])
                                TE.wait_ge(s_cev.h, cev_prev)
                            tr = TE.transpose(ptc[0:64, g8 * 128:(g8 + 1) * 128],
                                              ctxn[bi][:, gi * 64:(gi + 1) * 64],
                                              sid[:])
                            ctr_t[gi] = s_ctr.inc(tr, 1)
                        VE.wait_ge(s_ctr.h, ctr_t[half * 8 + 7])
                        if half == 0:
                            VE.wait_ge(s_wom.h, ctxt_free[bi])
                        cev_t[half] = s_cev.inc(VE.tensor_copy(
                            bass.AP(ctxt[bi], half * 8 * 8,
                                    [[16 * 128, 64], [8, 8], [1, 8], [128, 16]]),
                            bass.AP(ptc, 0, [[1024, 64], [128, 8], [16, 8], [1, 16]])), 1)
                        cev_prev = cev_t[half]
                    kst_free[bi] = smm_t[15]
                    vpr_free[bi] = vtr_t[15]
                    est_free[bi] = cmm_t[15]
                    vstk_free[bi] = cmm_t[15]
                    ctxn_free[bi] = ctr_t[15]
                    vev_prev = vev_t[1]
                    TE.wait_ge(s_cev.h, cev_t[1])
                    TE.wait_ge(s_cst.h, s_cst.n)
                    TE.wait_ge(VE._sem, aev_prev)
                    for h in range(16):
                        for eh in range(2):
                            mm = TE.matmul(
                                pwo[:, eh * 512:(eh + 1) * 512],
                                ctxt[bi][0:64, h * 128:(h + 1) * 128],
                                wo[:, h * 1024 + eh * 512:h * 1024 + (eh + 1) * 512],
                                start=(h == 0), stop=(h == 15))
                    wom_tok = s_wom.inc(mm, 1)
                    ctxt_free[bi] = wom_tok
                    VE.wait_ge(s_wom.h, wom_tok)
                    VE.wait_ge(s_asp.h, atst_free[bi])
                    VE.tensor_copy(atst[bi][:], pwo[:])
                    aev_tok = VE._n
                    aev_prev = aev_tok
                    GE.wait_ge(VE._sem, aev_tok)
                    atst_free[bi] = s_asp.inc(GE.dma_start(
                        out=at_sp.ap()[ti * TAU:(ti + 1) * TAU, :],
                        in_=atst[bi][:]), 16)
                    s_mid.inc(VE.engine_nop(), 1)
                    fla_prev_smm = smm_t[15]
                    fla_prev_kst = kst_tok
                    fla_prev_vpr = vpr_tok

                def f_wchunk(mi, ci, src_d):
                    bi = ci % 2
                    ncc = src_d.shape[1] // 512
                    rb, cb = ci // ncc, ci % ncc
                    if mi == 0 and ci == 0:
                        VE.memset(wacc[:], 0.0)
                    SY.wait_ge(VE._sem, wfs_free[bi])
                    ldt = s_wfl.inc(SY.dma_start(
                        out=wfs[bi],
                        in_=src_d.ap()[rb * 128:(rb + 1) * 128,
                                       cb * 512:(cb + 1) * 512]), 16)
                    VE.wait_ge(s_wfl.h, ldt)
                    VE.scalar_tensor_tensor(wneg[:], wfs[bi], -1.0, wfs[bi],
                                            ALU.mult, ALU.max)
                    wfs_free[bi] = VE._n
                    VE.tensor_add(wacc[:], wacc[:], wneg[:])
                    if ci == 63:
                        s_wab.inc(VE.reduce_sum(wabs[:, mi:mi + 1], wacc[:],
                                                AX.X), 1)
                        VE.memset(wacc[:], 0.0)

                wsrc = [(0, wg_d), (1, wu_d), (2, wd_d)]
                wchunks = [(mi, ci, d) for (mi, d) in wsrc for ci in range(64)]
                wi = 0
                ln_toks = {}

                def ln_super(sj):
                    for tl in range(TPS):
                        f_ln_pre(sj * TPS + tl)
                    smt = f_ln_smalls(slice(sj * TPS, (sj + 1) * TPS))
                    for tl in range(TPS):
                        ln_toks[sj * TPS + tl] = f_ln(sj * TPS + tl, smt)

                for sj in range(min(2, NS)):
                    ln_super(sj)
                for si in range(NS):
                    f_qkv(si, [ln_toks[si * TPS + tl] for tl in range(TPS)])
                    for tl in range(TPS):
                        f_middle(si * TPS + tl, si)
                    fla_prev_mid = s_mid.n
                    if si + 2 < NS:
                        ln_super(si + 2)
                    n_w = 13 if si < NS - 1 else len(wchunks) - wi
                    for _ in range(n_w):
                        if wi < len(wchunks):
                            f_wchunk(*wchunks[wi]); wi += 1
                s_mum = sem("mum")
                TE.wait_ge(s_wab.h, s_wab.n)
                TE.wait_ge(s_nrm.h, s_nrm.n)
                for mi in range(3):
                    s_mum.inc(TE.matmul(pcx[0:1, mi:mi + 1], wabs[:, mi:mi + 1],
                                        onesc[:], start=True, stop=True), 1)
                SE.wait_ge(s_mum.h, s_mum.n)
                SE.copy(murow[:], pcx[0:1, 0:3])
                SE.mul(murow[:], murow[:], 1.0 / (D * IFF))
                se_tok = SE._n
                VE.wait_ge(SE._sem, se_tok)
                VE.tensor_scalar_max(murow[:], murow[:], 1e-5)
                VE.tensor_scalar_mul(murow[:], murow[:], 0.5)
                mur_tok = VE._n
                TE.wait_ge(VE._sem, mur_tok)
                for mi in range(3):
                    s_mum.inc(TE.matmul(pcx[:, 4 + mi:5 + mi], ones1[:],
                                        murow[0:1, mi:mi + 1], start=True, stop=True), 1)
                SE.wait_ge(s_mum.h, s_mum.n)
                SE.copy(mu3[:], pcx[:, 4:7])
                se_tok2 = SE._n
                VE.wait_ge(SE._sem, se_tok2)
                VE.tensor_scalar_mul(thr_n[:], mu3[:], -1.0)
                GE.wait_ge(s_qsp.h, s_qsp.n)
                GE.wait_ge(s_asp.h, s_asp.n)
                SY.wait_ge(s_fla.h, s_fla.n)

            barrier("A")

            esB = ExitStack()
            with esB:
                sbb = lambda n, sh, dt: esB.enter_context(nc.sbuf_tensor(n, sh, dt))
                wgt = sbb("wgt", [128, 8 * IFF], BF16)
                wut = sbb("wut", [128, 8 * IFF], BF16)
                s_twl = sem("twl")
                s_ts1 = sem("ts1")
                s_ts2 = sem("ts2")
                s_ter = sem("ter")

                with ExitStack() as esT:
                    st = lambda n, sh, dt: esT.enter_context(nc.sbuf_tensor(n, sh, dt))
                    twf = [st(f"twf{i}", [128, 2048], F32) for i in range(2)]
                    ts1 = [st(f"ts1_{i}", [128, 2048], BF16) for i in range(2)]
                    ts2 = [st(f"ts2_{i}", [128, 2048], BF16) for i in range(2)]
                    tst_free = [0, 0]

                    def f_tern(mi, src_d, dst):
                        for ck in range(16):
                            bi = ck % 2
                            ldt = s_twl.inc(SE.dma_start(
                                out=twf[bi][:],
                                in_=src_d.ap()[(ck // 2) * 128:(ck // 2) * 128 + 128,
                                               (ck % 2) * 2048:(ck % 2) * 2048 + 2048]), 16)
                            SE.wait_ge(s_twl.h, ldt)
                            SE.wait_ge(s_ter.h, tst_free[bi])
                            SE.activation(ts1[bi][:], twf[bi][:], AF.Sign,
                                          bias=thr_n[:, mi:mi + 1], scale=1.0)
                            t2 = s_ts2.inc(SE.activation(
                                ts2[bi][:], twf[bi][:], AF.Sign,
                                bias=mu3[:, mi:mi + 1], scale=1.0), 1)
                            VE.wait_ge(s_ts2.h, t2)
                            tt = s_ter.inc(VE.tensor_add(
                                dst[:, (ck // 2) * IFF + (ck % 2) * 2048:
                                    (ck // 2) * IFF + (ck % 2) * 2048 + 2048],
                                ts1[bi][:], ts2[bi][:]), 1)
                            tst_free[bi] = tt

                    f_tern(0, wg_d, wgt)
                    f_tern(1, wu_d, wut)

                barrier("T1")

                qtl = [sbb(f"qtl{i}", [128, D], BF16) for i in range(2)]
                gst = [sbb(f"gst{i}", [128, 1024], F32) for i in range(2)]
                xst = [sbb("xst0", [128, 1024], F32)] * 2
                hps = sbb("hps", [128, IFF], F32)
                htf = sbb("htf", [128, 1024], F32)
                hi0 = sbb("hi0", [128, 1024], mybir.dt.int32)
                hd = sbb("hd", [128, 1024], F32)
                hs1 = sbb("hs1", [128, 1024], BF16)
                hs2 = sbb("hs2", [128, 1024], BF16)
                hcs = sbb("hcs", [128, 1024], BF16)
                h2b = sbb("h2b", [128, IFF], BF16)
                h2ts = sbb("h2ts", [128, IFF], BF16)
                pg = [esB.enter_context(nc.psum_tensor(f"pg{i}", [128, 1024], F32))
                      for i in range(2)]
                pu = [esB.enter_context(nc.psum_tensor(f"pu{i}", [128, 1024], F32))
                      for i in range(2)]

                s_qtl = sem("qtl")
                s_gmm = sem("gmm")
                s_umm = sem("umm")
                s_gev = sem("gev")
                s_rl2 = sem("rl2")
                s_hpw = sem("hpw")
                s_hmx = sem("hmx")
                s_h2a = sem("h2a")
                s_h2b = sem("h2b")
                s_h2t = sem("h2t")
                s_h2s = sem("h2s")

                VE.tensor_scalar(t_a[:, :], p_al[:, :], mu3[:, 0:1], None, ALU.mult)
                VE.tensor_scalar(t_b[:, :], p_al[:, :], mu3[:, 1:2], None, ALU.mult)
                VE.tensor_mul(t_a[:, :], t_a[:, :], t_a[:, :])
                VE.tensor_mul(p_ct[:, :], t_a[:, :], t_b[:, :])
                ctd_tok = VE._n

                qtl_free = [0, 0]
                gst_free = [0, 0]
                xst_free = [0, 0]
                pg_free = [0, 0]
                pu_free = [0, 0]
                hps_free = 0
                hmx_free = 0
                hmx_t = [0, 0, 0, 0]
                h2b_free = 0
                h2t_free = 0
                h_i0_prev = [0]
                h_d_prev = [0]
                h_sig_prev = [0]
                h_cs_prev = [0]
                h_fin_prev = [0]

                for ti in range(NT):
                    bi = ti % 2
                    col = slice(ti, ti + 1)
                    SE.wait_ge(s_umm.h, qtl_free[bi])
                    qt_tok = s_qtl.inc(SE.dma_start(
                        out=qtl[bi][:],
                        in_=bass.AP(qt_sp, ti * D * TAU,
                                    [[128, 128], [128 * 128, 8], [1, 128]])), 16)
                    gmm_t = [0] * 4
                    umm_t = [0] * 4
                    for q in range(4):
                        qs = q % 2
                        TE.wait_ge(s_qtl.h, qt_tok)
                        TE.wait_ge(s_gev.h, pg_free[qs])
                        TE.wait_ge(s_hpw.h, pu_free[qs])
                        for f in range(8):
                            lhs = qtl[bi][:, f * 128:(f + 1) * 128]
                            for c2 in range(2):
                                mmg = TE.matmul(
                                    pg[qs][:, c2 * 512:(c2 + 1) * 512], lhs,
                                    wgt[:, f * IFF + q * 1024 + c2 * 512:
                                        f * IFF + q * 1024 + (c2 + 1) * 512],
                                    start=(f == 0), stop=(f == 7))
                            for c2 in range(2):
                                mmu = TE.matmul(
                                    pu[qs][:, c2 * 512:(c2 + 1) * 512], lhs,
                                    wut[:, f * IFF + q * 1024 + c2 * 512:
                                        f * IFF + q * 1024 + (c2 + 1) * 512],
                                    start=(f == 0), stop=(f == 7))
                        gmm_t[q] = s_gmm.inc(mmg, 1)
                        umm_t[q] = s_umm.inc(mmu, 1)
                        SE.wait_ge(s_gmm.h, gmm_t[q])
                        SE.wait_ge(s_rl2.h, gst_free[bi])
                        gev_tok = s_gev.inc(
                            SE.copy(gst[bi][:], pg[qs][:]), 1)
                        pg_free[qs] = gev_tok
                        VE.wait_ge(s_gev.h, gev_tok)
                        rl2_tok = s_rl2.inc(VE.scalar_tensor_tensor(
                            xst[bi][:], gst[bi][:], 0.0, gst[bi][:],
                            ALU.max, ALU.mult), 1)
                        gst_free[bi] = rl2_tok
                        VE.wait_ge(s_umm.h, umm_t[q])
                        if q == 0:
                            VE.wait_ge(SE._sem, hps_free)
                            VE.wait_ge(s_hmx.h, hmx_free)
                        hpw_tok = s_hpw.inc(VE.tensor_mul(
                            hps[:, q * 1024:(q + 1) * 1024], xst[bi][:], pu[qs][:]), 1)
                        xst_free[bi] = hpw_tok
                        pu_free[qs] = hpw_tok
                        hmx_t[q] = s_hmx.inc(VE.reduce_max(
                            hx4[:, q:q + 1], hps[:, q * 1024:(q + 1) * 1024],
                            AX.X, apply_absolute_value=True), 1)
                    VE.wait_ge(VE._sem, ctd_tok)
                    VE.reduce_max(hx4[:, 4:5], hx4[:, 0:4], AX.X)
                    VE.tensor_mul(hx4[:, 5:6], p_ct[:, col], hx4[:, 4:5])
                    VE.tensor_scalar_max(hx4[:, 5:6], hx4[:, 5:6], 1e-5)
                    VE.reciprocal(hx4[:, 6:7], hx4[:, 5:6])
                    VE.tensor_mul(hx4[:, 6:7], hx4[:, 6:7], p_ct[:, col])
                    VE.tensor_scalar_mul(p_sh[:, col], hx4[:, 6:7], 127.0)
                    shd_tok = s_hmx.inc(VE.tensor_scalar_mul(
                        p_dd[:, col], hx4[:, 5:6], 1.0 / 254.0), 1)
                    for hf in range(4):
                        cs2 = slice(hf * 1024, (hf + 1) * 1024)
                        SE.wait_ge(s_hmx.h, shd_tok)
                        if h_i0_prev[0]:
                            SE.wait_ge(VE._sem, h_i0_prev[0])
                        SE.activation(htf[:], hps[:, cs2], AF.Copy,
                                      bias=0.0, scale=p_sh[:, col])
                        se_t = SE._n
                        VE.wait_ge(SE._sem, se_t)
                        if h_d_prev[0]:
                            VE.wait_ge(GE._sem, h_d_prev[0])
                        VE.tensor_copy(hi0[:], htf[:])
                        h_i0_prev[0] = VE._n
                        GE.wait_ge(VE._sem, h_i0_prev[0])
                        if h_sig_prev[0]:
                            GE.wait_ge(SE._sem, h_sig_prev[0])
                        GE.tensor_tensor(hd[:], htf[:], hi0[:], ALU.subtract)
                        h_d_prev[0] = GE._n
                        SE.wait_ge(GE._sem, h_d_prev[0])
                        if h_cs_prev[0]:
                            SE.wait_ge(GE._sem, h_cs_prev[0])
                        SE.activation(hs1[:], hd[:], AF.Sign, bias=chf[:, 0:1], scale=1.0)
                        SE.activation(hs2[:], hd[:], AF.Sign, bias=chf[:, 1:2], scale=1.0)
                        h_sig_prev[0] = SE._n
                        GE.wait_ge(SE._sem, h_sig_prev[0])
                        if h_fin_prev[0]:
                            GE.wait_ge(VE._sem, h_fin_prev[0])
                        GE.tensor_add(hcs[:], hs1[:], hs2[:])
                        h_cs_prev[0] = GE._n
                        VE.wait_ge(GE._sem, h_cs_prev[0])
                        VE.wait_ge(s_h2t.h, h2b_free)
                        VE.scalar_tensor_tensor(h2b[:, cs2], hi0[:], 2.0,
                                                hcs[:], ALU.mult, ALU.add)
                        h_fin_prev[0] = VE._n
                    hps_free = se_t
                    h2b_tok = h_fin_prev[0]
                    SY.wait_ge(VE._sem, h2b_tok)
                    SY.wait_ge(s_h2s.h, h2t_free)
                    for hf in range(2):
                        h2t_tok = s_h2t.inc(SY.dma_start_transpose(
                            bass.AP(h2ts, hf * 2048, [[IFF, 128], [128, 16], [1, 128]]),
                            bass.AP(h2b, hf * 2048, [[IFF, 128], [128, 16], [1, 128]])), 16)
                    h2b_free = h2t_tok
                    GE.wait_ge(s_h2t.h, h2t_tok)
                    h2t_free = s_h2s.inc(GE.dma_start(
                        out=bass.AP(h2_sp, ti * IFF * TAU,
                                    [[128, 128], [128 * 128, 32], [1, 128]]),
                        in_=h2ts[:]), 16)
                    hmx_free = hmx_t[3]
                    qtl_free[bi] = umm_t[3]
                GE.wait_ge(s_h2s.h, s_h2s.n)

            barrier("B1")

            esC = ExitStack()
            with esC:
                sc = lambda n, sh, dt: esC.enter_context(nc.sbuf_tensor(n, sh, dt))
                wdt = sc("wdt", [128, 32 * 1024], BF16)
                s_twl = sem("twl")
                s_ts2 = sem("ts2")
                s_ter = sem("ter")
                with ExitStack() as esT:
                    st = lambda n, sh, dt: esT.enter_context(nc.sbuf_tensor(n, sh, dt))
                    twf = [st(f"dwf{i}", [128, 1024], F32) for i in range(2)]
                    ts1 = [st(f"ds1_{i}", [128, 1024], BF16) for i in range(2)]
                    ts2 = [st(f"ds2_{i}", [128, 1024], BF16) for i in range(2)]
                    tst_free = [s_ter.n, s_ter.n]
                    for ck in range(32):
                        bi = ck % 2
                        ldt = s_twl.inc(SE.dma_start(
                            out=twf[bi][:],
                            in_=wd_d.ap()[ck * 128:(ck + 1) * 128, :]), 16)
                        SE.wait_ge(s_twl.h, ldt)
                        SE.wait_ge(s_ter.h, tst_free[bi])
                        SE.activation(ts1[bi][:], twf[bi][:], AF.Sign,
                                      bias=thr_n[:, 2:3], scale=1.0)
                        t2 = s_ts2.inc(SE.activation(
                            ts2[bi][:], twf[bi][:], AF.Sign,
                            bias=mu3[:, 2:3], scale=1.0), 1)
                        VE.wait_ge(s_ts2.h, t2)
                        tt = s_ter.inc(VE.tensor_add(
                            wdt[:, ck * 1024:(ck + 1) * 1024],
                            ts1[bi][:], ts2[bi][:]), 1)
                        tst_free[bi] = tt

                barrier("T2")

                h2l = [sc(f"h2l{i}", [128, IFF], BF16) for i in range(2)]
                x2 = [sc(f"x2_{i}", [128, D], F32) for i in range(2)]
                at2 = [sc(f"at2_{i}", [128, D], F32) for i in range(2)]
                ost = [sc(f"ost{i}", [128, D], F32) for i in range(2)]
                o1b = sc("o1b", [128, D], F32)
                pd = [esC.enter_context(nc.psum_tensor(f"pd{i}", [128, 1024], F32))
                      for i in range(2)]

                s_h2l = sem("h2l")
                s_x2l = sem("x2l")
                s_a2l = sem("a2l")
                s_dmm = sem("dmm")
                s_och = sem("och")
                s_osp = sem("osp")

                VE.tensor_scalar(p_dd[:, :], p_dd[:, :], mu3[:, 2:3], None, ALU.mult)
                ddf_tok = VE._n

                h2l_free = [0, 0]
                x2_free = [0, 0]
                at2_free = [0, 0]
                ost_free = [0, 0]
                pd_free = [0, 0]

                for ti in range(NT):
                    bi = ti % 2
                    col = slice(ti, ti + 1)
                    SE.wait_ge(s_dmm.h, h2l_free[bi])
                    h2tok = s_h2l.inc(SE.dma_start(
                        out=h2l[bi][:],
                        in_=bass.AP(h2_sp, ti * IFF * TAU,
                                    [[128, 128], [128 * 128, 32], [1, 128]])), 16)
                    SE.wait_ge(s_och.h, x2_free[bi])
                    x2tok = s_x2l.inc(SE.dma_start(
                        out=x2[bi][:], in_=x_in.ap()[ti * TAU:(ti + 1) * TAU, :]), 16)
                    SE.wait_ge(s_och.h, at2_free[bi])
                    a2tok = s_a2l.inc(SE.dma_start(
                        out=at2[bi][:], in_=at_sp.ap()[ti * TAU:(ti + 1) * TAU, :]), 16)
                    qs = ti % 2
                    TE.wait_ge(s_h2l.h, h2tok)
                    TE.wait_ge(s_och.h, pd_free[qs])
                    for f2 in range(32):
                        for c2 in range(2):
                            mm = TE.matmul(
                                pd[qs][:, c2 * 512:(c2 + 1) * 512],
                                h2l[bi][:, f2 * 128:(f2 + 1) * 128],
                                wdt[:, f2 * 1024 + c2 * 512:f2 * 1024 + (c2 + 1) * 512],
                                start=(f2 == 0), stop=(f2 == 31))
                    dmm_tok = s_dmm.inc(mm, 1)
                    h2l_free[bi] = dmm_tok
                    VE.wait_ge(s_dmm.h, dmm_tok)
                    VE.wait_ge(s_x2l.h, x2tok)
                    VE.wait_ge(s_a2l.h, a2tok)
                    VE.wait_ge(VE._sem, ddf_tok)
                    VE.wait_ge(s_osp.h, ost_free[bi])
                    VE.scalar_tensor_tensor(o1b[:], pd[qs][:], p_dd[:, col],
                                            x2[bi][:], ALU.mult, ALU.add)
                    VE.tensor_add(o1b[:], o1b[:], at2[bi][:])
                    och_tok = s_och.inc(VE.tensor_add(ost[bi][:], o1b[:], bbc[:]), 1)
                    pd_free[qs] = och_tok
                    x2_free[bi] = och_tok
                    at2_free[bi] = och_tok
                    GE.wait_ge(s_och.h, och_tok)
                    ost_free[bi] = s_osp.inc(GE.dma_start(
                        out=out_d.ap()[ti * TAU:(ti + 1) * TAU, :],
                        in_=ost[bi][:]), 16)
                if DBG:
                    s_dbg = sem("dbg")
                    for i, t in enumerate((p_m, p_rs, p_sg, p_sq, p_al, p_ct,
                                           p_sh, p_dd, mu3, thr_n, wabs)):
                        w = min(t.shape[1], 64)
                        s_dbg.inc(GE.dma_start(
                            out=dbg_p.ap()[:, i * 64:i * 64 + w], in_=t[:, 0:w]), 16)
                    GE.wait_ge(s_dbg.h, s_dbg.n)
                GE.wait_ge(s_osp.h, s_osp.n)
                SE.wait_ge(s_osp.h, s_osp.n)

    return nc


def make_host_layouts(W_qkv, b_qkv):
    perm = np.zeros(3 * D, dtype=np.int64)
    for r in range(3):
        for j in range(8):
            for hh in range(2):
                h = 2 * j + hh
                cols = 192 * h + 64 * r + np.arange(64)
                m = r * 8 + j
                perm[m * 128 + hh * 64: m * 128 + hh * 64 + 64] = cols
    Wp = np.ascontiguousarray(np.asarray(W_qkv)[:, perm])
    bp = np.asarray(b_qkv)[perm]
    btile = np.ascontiguousarray(bp.reshape(24, 128).T).astype(np.float32)
    return Wp, btile


def make_masks():
    mq = np.zeros((9, 16 * SUP), dtype=BF)
    mq[0, :] = 32.0
    cols = np.arange(16 * SUP)
    for tp in range(8):
        mq[1 + tp, (cols % 8) == tp] = 32.0
    mk = np.zeros((9, 16 * TAU), dtype=BF)
    mk[0, :] = -32.0
    for tp in range(8):
        mk[1 + tp, ((cols[:16 * TAU] % 128) // 16) == tp] = 32.0
    return mq, mk


def kernel(L0_attn_x, L0_attn_W_qkv, L0_attn_b_qkv, L0_attn_W_out,
           L0_attn_b_out, L0_ffn_w_gate, L0_ffn_w_up, L0_ffn_w_down):
    x = np.ascontiguousarray(np.asarray(L0_attn_x, dtype=np.float32))
    B, S, _ = x.shape
    Wp, btile = make_host_layouts(L0_attn_W_qkv, L0_attn_b_qkv)
    mq, mk = make_masks()
    shared = {
        "wqkv": Wp.astype(np.float32),
        "btile": btile,
        "wout": np.ascontiguousarray(np.asarray(L0_attn_W_out, dtype=np.float32)),
        "bout": np.asarray(L0_attn_b_out, dtype=np.float32).reshape(1, D),
        "wg_t": np.ascontiguousarray(np.asarray(L0_ffn_w_gate, dtype=np.float32).T),
        "wu_t": np.ascontiguousarray(np.asarray(L0_ffn_w_up, dtype=np.float32).T),
        "wd_t": np.ascontiguousarray(np.asarray(L0_ffn_w_down, dtype=np.float32).T),
        "ident": np.eye(128, dtype=BF),
        "maskq": mq,
        "maskk": mk,
    }
    nc = build_program(S)
    in_maps = [dict(shared, x=np.ascontiguousarray(x[b])) for b in range(B)]
    res = run_bass_kernel_spmd(nc, in_maps, core_ids=list(range(B)))
    out = np.stack([np.asarray(res.results[b]["out"]) for b in range(B)], axis=0)
    return out.astype(np.float32)


# version 10
# speedup vs baseline: 1.1887x; 1.0822x over previous
import numpy as np
import ml_dtypes
from contextlib import ExitStack
import concourse.bass as bass
import concourse.mybir as mybir
from concourse.bass_utils import run_bass_kernel_spmd

F32 = mybir.dt.float32
BF16 = mybir.dt.bfloat16
F16 = mybir.dt.float16
I16 = mybir.dt.int16
AF = mybir.ActivationFunctionType
ALU = mybir.AluOpType
AX = mybir.AxisListType
BF = ml_dtypes.bfloat16

D = 1024
NH = 16
IFF = 4096
TAU = 128
SUP = 256
TPS = SUP // TAU
MQ = 1536.0
N_CORES = 8
DBG = False


class Sem:
    __slots__ = ("h", "n")

    def __init__(self, es, nc, name):
        self.h = es.enter_context(nc.semaphore(name))
        self.n = 0

    def inc(self, instr, k=1):
        instr.then_inc(self.h, k)
        self.n += k
        return self.n


_WRAP_OPS = {
    "tensor_copy", "tensor_tensor", "tensor_add", "tensor_sub", "tensor_mul",
    "tensor_max", "tensor_scalar", "tensor_single_scalar", "tensor_scalar_add",
    "tensor_scalar_sub", "tensor_scalar_mul", "tensor_scalar_max",
    "tensor_scalar_min", "tensor_relu", "tensor_reduce", "reduce_sum",
    "reduce_max", "scalar_tensor_tensor", "reciprocal", "memset", "transpose",
    "activation", "square", "sqrt", "copy", "mul", "add", "sign", "select",
    "bn_stats", "bn_aggr", "engine_nop", "iota",
}


class EngSem:
    __slots__ = ("e",)

    def __init__(self, e):
        self.e = e

    @property
    def h(self):
        return self.e._sem

    @property
    def n(self):
        return self.e._n

    def inc(self, instr, k=1):
        return self.e._n


class SerialEng:

    def __init__(self, eng, nc, es, name):
        self._eng = eng
        self._sem = es.enter_context(nc.semaphore("ser_" + name))
        self._n = 0
        self._last = {}

    def _tnames(self, v, out):
        if isinstance(v, bass.AP):
            out.append(v.tensor.name)

    def __getattr__(self, name):
        fn = getattr(self._eng, name)
        if name not in _WRAP_OPS:
            return fn

        def wrapped(*a, **k):
            names = []
            for v in a:
                self._tnames(v, names)
            for v in k.values():
                self._tnames(v, names)
            need = 0
            for t in names:
                need = max(need, self._last.get(t, 0))
            if need:
                self._eng.wait_ge(self._sem, need)
            instr = fn(*a, **k)
            instr.then_inc(self._sem, 1)
            self._n += 1
            wtoks = []
            for v in list(a[:1]) + [k[x] for x in ("accum_out", "out") if x in k]:
                if isinstance(v, bass.AP):
                    self._last[v.tensor.name] = self._n
            if a and isinstance(a[0], bass.AP):
                self._last[a[0].tensor.name] = self._n
            return instr

        return wrapped


def build_program(S):
    NT = S // TAU
    NS = S // SUP
    nc = bass.Bass("TRN2", target_bir_lowering=False,
                   detect_race_conditions=False)

    x_in = nc.dram_tensor("x", [S, D], F32, kind="ExternalInput")
    wqkv_d = nc.dram_tensor("wqkv", [D, 3 * D], F32, kind="ExternalInput")
    btile_d = nc.dram_tensor("btile", [128, 24], F32, kind="ExternalInput")
    wout_d = nc.dram_tensor("wout", [D, D], F32, kind="ExternalInput")
    bout_d = nc.dram_tensor("bout", [1, D], F32, kind="ExternalInput")
    wg_d = nc.dram_tensor("wg_t", [D, IFF], F32, kind="ExternalInput")
    wu_d = nc.dram_tensor("wu_t", [D, IFF], F32, kind="ExternalInput")
    wd_d = nc.dram_tensor("wd_t", [IFF, D], F32, kind="ExternalInput")
    id_d = nc.dram_tensor("ident", [128, 128], BF16, kind="ExternalInput")
    mq_d = nc.dram_tensor("maskq", [9, 16 * SUP], BF16, kind="ExternalInput")
    mk_d = nc.dram_tensor("maskk", [9, 16 * TAU], BF16, kind="ExternalInput")
    out_d = nc.dram_tensor("out", [S, D], F32, kind="ExternalOutput")
    kind_dbg = {"kind": "ExternalOutput"} if DBG else {}
    qt_sp = nc.dram_tensor("qt_sp", [NT * D * TAU], BF16, **kind_dbg)
    at_sp = nc.dram_tensor("at_sp", [S, D], F32, **kind_dbg)
    h2_sp = nc.dram_tensor("h2_sp", [NT * IFF * TAU], BF16, **kind_dbg)
    if DBG:
        dbg_p = nc.dram_tensor("dbg_p", [128, 1024], F32, kind="ExternalOutput")

    es = ExitStack()
    with es:
        TE, SY = nc.tensor, nc.sync
        VE = SerialEng(nc.vector, nc, es, "ve")
        SE = SerialEng(nc.scalar, nc, es, "se")
        GE = SerialEng(nc.gpsimd, nc, es, "ge")
        sb = lambda n, sh, dt: es.enter_context(nc.sbuf_tensor(n, sh, dt))
        sid = sb("sid", [128, 128], BF16)
        p_m = sb("p_m", [128, NT], F32)
        p_rs = sb("p_rs", [128, NT], F32)
        p_sg = sb("p_sg", [128, NT], F32)
        p_sq = sb("p_sq", [128, NT], F32)
        p_qb = sb("p_qb", [128, NT], F32)
        p_nm = sb("p_nm", [128, NT], F32)
        p_al = sb("p_al", [128, NT], F32)
        p_ct = sb("p_ct", [128, NT], F32)
        p_sh = sb("p_sh", [128, NT], F32)
        p_dd = sb("p_dd", [128, NT], F32)
        t_a = sb("t_a", [128, NT], F32)
        t_b = sb("t_b", [128, NT], F32)
        p_mx = sb("p_mx", [128, NT], F32)
        p_mn = sb("p_mn", [128, NT], F32)
        bbc = sb("bbc", [128, D], F32)
        brow = sb("brow", [1, D], F32)
        ones1 = sb("ones1", [1, 128], F32)
        onesc = sb("onesc", [128, 1], F32)
        bcol = sb("bcol", [128, 24], F32)
        wabs = sb("wabs", [128, 3], F32)
        mu3 = sb("mu3", [128, 3], F32)
        thr_n = sb("thr_n", [128, 3], F32)
        murow = sb("murow", [1, 3], F32)
        hx4 = sb("hx4", [128, 8], F32)
        rcp4 = sb("rcp4", [128, 4], F32)
        chf = sb("chf", [128, 2], F32)

        SEMS = {}
        _VE_SEMS = {"lnv", "qbb", "kst", "vev", "rcp", "cev", "mid", "wab",
                    "ter", "rl2", "hpw", "hmx", "h2b", "och", "ini"}
        _SE_SEMS = {"sqd", "sig", "xnb", "fme", "exp", "nrm", "aev", "wap",
                    "ts2", "gev", "h2a", "bbc"}
        _GE_SEMS = {"vpr"}

        def sem(name):
            if name not in SEMS:
                if name in _VE_SEMS:
                    SEMS[name] = EngSem(VE)
                elif name in _SE_SEMS:
                    SEMS[name] = EngSem(SE)
                elif name in _GE_SEMS:
                    SEMS[name] = EngSem(GE)
                else:
                    SEMS[name] = Sem(es, nc, name)
            return SEMS[name]

        block = es.enter_context(nc.Block())

        def barrier(tag):
            b = sem("bar_" + tag)
            for eng in (TE, VE, SE, GE, SY):
                b.inc(eng.nop(), 1)
            for eng in (TE, VE, SE, GE, SY):
                eng.wait_ge(b.h, b.n)

        @block.sync
        def _(_q):
            esA = ExitStack()
            with esA:
                sa = lambda n, sh, dt: esA.enter_context(nc.sbuf_tensor(n, sh, dt))
                wq = sa("wq", [128, 8 * 3072], BF16)
                wo = sa("wo", [64, 16 * 1024], BF16)
                xr = [sa(f"xr{i}", [128, D], F32) for i in range(2)]
                xsq = sa("xsq", [128, D], F32)
                xnst = [sa(f"xnst{i}", [128, D], BF16) for i in range(2)]
                qtf = sa("qtf", [128, D], F32)
                qi0 = sa("qi0", [128, D], mybir.dt.int32)
                qd = xsq
                qs1 = sa("qs1", [128, D], BF16)
                qs2 = sa("qs2", [128, D], BF16)
                qcs = sa("qcs", [128, D], BF16)
                qbst = [sa(f"qbst{i}", [128, D], BF16) for i in range(2)]
                qts = [sa(f"qts{i}", [128, D], BF16) for i in range(2)]
                xnt = sa("xnt", [128, 4 * D], BF16)
                fmst = sa("fmst", [128, 8 * SUP], BF16)
                qfl = sa("qfl", [128, 16 * SUP], BF16)
                kfl = sa("kfl", [128, 16 * SUP], BF16)
                vfl = sa("vfl", [128, 16 * SUP], BF16)
                kstk = [sa("kstk0", [128, 16 * TAU], BF16)] * 2
                vpre = [sa(f"vpre{i}", [128, 16 * TAU], BF16) for i in range(2)]
                est = [sa("est0", [128, 16 * TAU], BF16)] * 2
                vstk = [sa("vstk0", [128, 16 * 65], BF16)] * 2
                ctxn = [sa(f"ctxn{i}", [128, 16 * 64], BF16) for i in range(2)]
                ctxt = [sa(f"ctxt{i}", [128, 16 * 128], BF16) for i in range(2)]
                atst = [sa("atst0", [128, D], F32)] * 2
                wfs_t = [sa(f"wfs{i}", [128, 512], F32) for i in range(2)]
                wfs = [t[:] for t in wfs_t]
                wneg = sa("wneg", [128, 512], F32)
                wacc = sa("wacc", [128, 512], F32)

                pq = esA.enter_context(nc.psum_tensor("pq", [128, 512], F32))
                pwo = esA.enter_context(nc.psum_tensor("pwo", [128, 1024], F32))
                ps_s = esA.enter_context(nc.psum_tensor("ps_s", [128, 1024], F32))
                pcx = esA.enter_context(nc.psum_tensor("pcx", [128, 512], F32))
                ptr = esA.enter_context(nc.psum_tensor("ptr", [128, 1024], BF16))
                ptc = esA.enter_context(nc.psum_tensor("ptc", [128, 1024], BF16))

                s_w0 = sem("w0")
                s_cst = sem("cst")
                s_msk = sem("msk")
                s_ini = sem("ini")
                s_bbc = sem("bbc")
                s_w0.inc(SY.dma_start(out=sid[:], in_=id_d.ap()), 16)
                s_w0.inc(SY.dma_start(out=brow[:], in_=bout_d.ap()), 16)
                s_w0.inc(SY.dma_start(out=bcol[:], in_=btile_d.ap()), 16)
                s_msk.inc(SY.dma_start(out=qfl[64:73, :], in_=mq_d.ap()), 16)
                s_msk.inc(SY.dma_start(out=kstk[0][64:73, :], in_=mk_d.ap()), 16)
                for c in range(8):
                    s_cst.inc(GE.dma_start(
                        out=wq[:, c * 3072:(c + 1) * 3072],
                        in_=wqkv_d.ap()[c * 128:(c + 1) * 128, :]), 16)
                for h in range(16):
                    s_cst.inc(GE.dma_start(
                        out=wo[:, h * 1024:(h + 1) * 1024],
                        in_=wout_d.ap()[h * 64:(h + 1) * 64, :]), 16)
                VE.memset(ones1[:], 1.0)
                VE.memset(onesc[:], 1.0)
                VE.memset(wabs[:], 0.0)
                VE.memset(chf[:, 0:1], -0.5)
                VE.memset(chf[:, 1:2], 0.5)
                VE.memset(bass.AP(vstk[0], 64, [[16 * 65, 128], [65, 16], [1, 1]]), 1.0)
                s_ini.inc(VE.engine_nop(), 1)
                TE.wait_ge(s_w0.h, s_w0.n)
                TE.wait_ge(s_ini.h, s_ini.n)
                s_mmb = sem("mmb")
                TE.matmul(pq[:], ones1[:], brow[0:1, 0:512], start=True, stop=True)
                s_mmb.inc(TE.matmul(ps_s[:, 0:512], ones1[:], brow[0:1, 512:1024],
                                    start=True, stop=True), 1)
                SE.wait_ge(s_mmb.h, s_mmb.n)
                SE.copy(bbc[:, 0:512], pq[:])
                s_bbc.inc(SE.copy(bbc[:, 512:1024], ps_s[:, 0:512]), 1)
                qkv_bank_free = 0
                if DBG:
                    barrier("pre")

                s_xld = sem("xld")
                s_lnv = sem("lnv")
                s_sqd = sem("sqd")
                s_sig = sem("sig")
                s_xnb = sem("xnb")
                s_qbb = sem("qbb")
                s_xnt = sem("xnt")
                s_qtt = sem("qtt")
                s_qsp = sem("qsp")
                s_qkv = sem("qkv")
                s_fme = sem("fme")
                s_fla = sem("fla")
                s_kst = sem("kst")
                s_vpr = sem("vpr")
                s_smm = sem("smm")
                s_exp = sem("exp")
                s_vtr = sem("vtr")
                s_vev = sem("vev")
                s_cmm = sem("cmm")
                s_rcp = sem("rcp")
                s_nrm = sem("nrm")
                s_ctr = sem("ctr")
                s_cev = sem("cev")
                s_wom = sem("wom")
                s_aev = sem("aev")
                s_asp = sem("asp")
                s_mid = sem("mid")
                s_wfl = sem("wfl")
                s_wab = sem("wab")
                s_wap = sem("wap")

                xr_free = [0, 0]
                xn_free = [0, 0]
                qb_free = [0, 0]
                qt_free = [0, 0]
                xnt_free = [0, 0, 0, 0]
                xsq_free = 0
                kst_free = [0, 0]
                vpr_free = [0, 0]
                est_free = [0, 0]
                vstk_free = [0, 0]
                ctxn_free = [0, 0]
                ctxt_free = [0, 0]
                atst_free = [0, 0]
                wfs_free = [0, 0]
                fla_prev_mid = 0
                fla_prev_smm = 0
                fla_prev_kst = 0
                fla_prev_vpr = 0
                fme_tok = [0] * 24
                q_sig_prev = [0]
                q_cs_prev = [0]
                q_fin_prev = [0]
                wab_se_prev = [0]
                exp_prev2 = [0, 0]
                vev_prev = 0
                cev_prev = 0
                nrm_prev = 0
                aev_prev = 0

                def f_ln_pre(ti):
                    nonlocal xsq_free
                    bi = ti % 2
                    col = slice(ti, ti + 1)
                    SE.wait_ge(s_lnv.h, xr_free[bi])
                    xtok = s_xld.inc(SE.dma_start(
                        out=xr[bi][:], in_=x_in.ap()[ti * TAU:(ti + 1) * TAU, :]), 16)
                    VE.wait_ge(s_xld.h, xtok)
                    VE.reduce_sum(t_a[:, col], xr[bi][:], AX.X)
                    VE.reduce_max(p_mx[:, col], xr[bi][:], AX.X)
                    VE.tensor_reduce(p_mn[:, col], xr[bi][:], AX.X, ALU.min)
                    SE.wait_ge(s_xld.h, xtok)
                    SE.wait_ge(s_lnv.h, xsq_free)
                    sq_tok = s_sqd.inc(SE.square(xsq[:], xr[bi][:]), 1)
                    VE.wait_ge(s_sqd.h, sq_tok)
                    VE.reduce_sum(t_b[:, col], xsq[:], AX.X)
                    xsq_free = s_lnv.inc(VE.engine_nop(), 1)
                    return xtok

                def f_ln_smalls(cols):
                    c = cols
                    VE.tensor_scalar_mul(p_m[:, c], t_a[:, c], 1.0 / D)
                    VE.tensor_scalar_mul(t_b[:, c], t_b[:, c], 1.0 / D)
                    VE.tensor_mul(t_a[:, c], p_m[:, c], p_m[:, c])
                    VE.tensor_sub(t_b[:, c], t_b[:, c], t_a[:, c])
                    var_tok = s_lnv.inc(
                        VE.tensor_scalar_add(t_b[:, c], t_b[:, c], 1e-5), 1)
                    SE.wait_ge(s_lnv.h, var_tok)
                    sig_tok = s_sig.inc(SE.sqrt(p_sg[:, c], t_b[:, c]), 1)
                    VE.wait_ge(s_sig.h, sig_tok)
                    VE.reciprocal(p_rs[:, c], p_sg[:, c])
                    VE.tensor_sub(p_mx[:, c], p_mx[:, c], p_m[:, c])
                    VE.tensor_sub(p_mn[:, c], p_m[:, c], p_mn[:, c])
                    VE.tensor_max(p_mx[:, c], p_mx[:, c], p_mn[:, c])
                    VE.tensor_scalar_mul(p_mn[:, c], p_sg[:, c], 1e-5)
                    VE.tensor_max(p_mx[:, c], p_mx[:, c], p_mn[:, c])
                    VE.reciprocal(p_sq[:, c], p_mx[:, c])
                    VE.tensor_scalar_mul(p_sq[:, c], p_sq[:, c], 127.0)
                    VE.tensor_mul(p_al[:, c], p_mx[:, c], p_rs[:, c])
                    VE.tensor_scalar_mul(p_al[:, c], p_al[:, c], 1.0 / 254.0)
                    VE.tensor_mul(p_nm[:, c], p_m[:, c], p_rs[:, c])
                    VE.tensor_scalar_mul(p_nm[:, c], p_nm[:, c], -1.0)
                    VE.tensor_mul(p_qb[:, c], p_m[:, c], p_sq[:, c])
                    sml_tok = s_lnv.inc(VE.tensor_scalar_mul(
                        p_qb[:, c], p_qb[:, c], -1.0), 1)
                    return sml_tok

                def f_ln(ti, sml_tok):
                    bi = ti % 2
                    col = slice(ti, ti + 1)
                    SE.wait_ge(s_lnv.h, sml_tok)
                    SE.wait_ge(s_xnt.h, xn_free[bi])
                    xnb_tok = s_xnb.inc(SE.activation(
                        xnst[bi][:], xr[bi][:], AF.Identity,
                        bias=p_nm[:, col], scale=p_rs[:, col]), 1)
                    VE.tensor_scalar(qtf[:], xr[bi][:],
                                     p_sq[:, col], p_qb[:, col], ALU.mult, ALU.add)
                    VE.tensor_copy(qi0[:], qtf[:])
                    ve_q = VE._n
                    GE.wait_ge(VE._sem, ve_q)
                    if q_sig_prev[0]:
                        GE.wait_ge(SE._sem, q_sig_prev[0])
                    GE.tensor_tensor(qd[:], qtf[:], qi0[:], ALU.subtract)
                    ge_d = GE._n
                    SE.wait_ge(GE._sem, ge_d)
                    if q_cs_prev[0]:
                        SE.wait_ge(GE._sem, q_cs_prev[0])
                    SE.activation(qs1[:], qd[:], AF.Sign, bias=chf[:, 0:1], scale=1.0)
                    SE.activation(qs2[:], qd[:], AF.Sign, bias=chf[:, 1:2], scale=1.0)
                    q_sig_prev[0] = SE._n
                    GE.wait_ge(SE._sem, q_sig_prev[0])
                    if q_fin_prev[0]:
                        GE.wait_ge(VE._sem, q_fin_prev[0])
                    GE.tensor_add(qcs[:], qs1[:], qs2[:])
                    q_cs_prev[0] = GE._n
                    VE.wait_ge(GE._sem, q_cs_prev[0])
                    VE.wait_ge(s_qtt.h, qb_free[bi])
                    qbb_tok = s_qbb.inc(VE.scalar_tensor_tensor(
                        qbst[bi][:], qi0[:], 2.0, qcs[:], ALU.mult, ALU.add), 1)
                    q_fin_prev[0] = qbb_tok
                    xr_free[bi] = s_lnv.inc(VE.engine_nop(), 1)
                    slot = ti % 4
                    SY.wait_ge(s_xnb.h, xnb_tok)
                    SY.wait_ge(s_qkv.h, xnt_free[slot])
                    xnt_tok = s_xnt.inc(SY.dma_start_transpose(
                        bass.AP(xnt, slot * D, [[4 * D, 128], [128, 8], [1, 128]]),
                        bass.AP(xnst[bi], 0, [[D, 128], [128, 8], [1, 128]])), 16)
                    xn_free[bi] = xnt_tok
                    SY.wait_ge(s_qbb.h, qbb_tok)
                    SY.wait_ge(s_qsp.h, qt_free[bi])
                    qtt_tok = s_qtt.inc(SY.dma_start_transpose(
                        bass.AP(qts[bi], 0, [[D, 128], [128, 8], [1, 128]]),
                        bass.AP(qbst[bi], 0, [[D, 128], [128, 8], [1, 128]])), 16)
                    qb_free[bi] = qtt_tok
                    GE.wait_ge(s_qtt.h, qtt_tok)
                    qsp_tok = s_qsp.inc(GE.dma_start(
                        out=bass.AP(qt_sp, ti * D * TAU,
                                    [[128, 128], [128 * 128, 8], [1, 128]]),
                        in_=qts[bi][:]), 16)
                    qt_free[bi] = qsp_tok
                    return xnt_tok

                def f_qkv(si, xnt_toks):
                    nonlocal fla_prev_mid, qkv_bank_free
                    TE.wait_ge(s_cst.h, s_cst.n)
                    TE.wait_ge(s_msk.h, s_msk.n)
                    TE.wait_ge(s_bbc.h, s_bbc.n)
                    TE.wait_ge(s_xnt.h, max(xnt_toks))
                    if si > 0:
                        TE.wait_ge(s_fla.h, s_fla.n)
                    sl0 = (si * TPS) % 4
                    for r, dst in ((0, qfl), (1, kfl), (2, vfl)):
                        fla_reg = s_fla.n
                        for j in range(8):
                            m = r * 8 + j
                            TE.wait_ge(s_fme.h, qkv_bank_free)
                            for f in range(8):
                                rhs = bass.AP(xnt, sl0 * D + f * 128,
                                              [[4 * D, 128], [D, TPS], [1, 128]])
                                mm = TE.matmul(
                                    pq[:, 0:SUP],
                                    wq[:, f * 3072 + m * 128:f * 3072 + (m + 1) * 128],
                                    rhs, start=(f == 0), stop=(f == 7))
                            qkv_tok = s_qkv.inc(mm, 1)
                            SE.wait_ge(s_qkv.h, qkv_tok)
                            if r > 0 or si > 0:
                                SE.wait_ge(s_fla.h, fla_reg)
                            fme_tok[m] = s_fme.inc(SE.activation(
                                fmst[:, j * SUP:(j + 1) * SUP],
                                pq[:, 0:SUP],
                                AF.Identity, bias=bcol[:, m:m + 1], scale=1.0), 1)
                            qkv_bank_free = fme_tok[m]
                        SY.wait_ge(s_fme.h, fme_tok[r * 8 + 7])
                        if r == 0:
                            SY.wait_ge(s_mid.h, fla_prev_mid)
                            SY.wait_ge(s_smm.h, fla_prev_smm)
                            SY.wait_ge(s_kst.h, fla_prev_kst)
                            SY.wait_ge(s_vpr.h, fla_prev_vpr)
                        s_fla.inc(SY.dma_start(
                            out=bass.AP(dst, 0,
                                        [[16 * SUP, 64], [2 * SUP, 8], [1, SUP]]),
                            in_=bass.AP(fmst, 0,
                                        [[8 * SUP, 64], [SUP, 8], [1, SUP]])), 16)
                        s_fla.inc(SY.dma_start(
                            out=bass.AP(dst, SUP,
                                        [[16 * SUP, 64], [2 * SUP, 8], [1, SUP]]),
                            in_=bass.AP(fmst, 64 * 8 * SUP,
                                        [[8 * SUP, 64], [SUP, 8], [1, SUP]])), 16)
                    for tl in range(TPS):
                        xnt_free[(si * TPS + tl) % 4] = s_qkv.n

                def f_middle(ti, si):
                    nonlocal vev_prev, cev_prev, nrm_prev, aev_prev
                    nonlocal fla_prev_smm, fla_prev_kst, fla_prev_vpr
                    bi = ti % 2
                    tl = ti - si * TPS
                    fla_tok = s_fla.n
                    VE.wait_ge(s_fla.h, fla_tok)
                    VE.wait_ge(s_smm.h, kst_free[bi])
                    kst_tok = s_kst.inc(VE.tensor_copy(
                        kstk[bi][0:64, :],
                        bass.AP(kfl, tl * TAU,
                                [[16 * SUP, 64], [8, 16], [1, 8], [SUP, 16]])), 1)
                    GE.wait_ge(s_fla.h, fla_tok)
                    GE.wait_ge(s_vtr.h, vpr_free[bi])
                    vpr_tok = s_vpr.inc(GE.tensor_copy(
                        vpre[bi][0:64, :],
                        bass.AP(vfl, tl * TAU,
                                [[16 * SUP, 64], [8, 16], [1, 8], [SUP, 16]])), 1)
                    smm_t = [0] * 16
                    exp_t = [0] * 4
                    vtr_t = [0] * 16
                    cmm_t = [0] * 16
                    ctr_t = [0] * 16
                    nrm_t = [0] * 4
                    vev_t = [0, 0]
                    cev_t = [0, 0]
                    for half in range(2):
                        for b4 in (2 * half, 2 * half + 1):
                            qb_ = (b4 % 2) * 512
                            for g in range(4):
                                gi = b4 * 4 + g
                                if g == 0:
                                    TE.wait_ge(s_kst.h, kst_tok)
                                    TE.wait_ge(s_exp.h, exp_prev2[b4 % 2])
                                rhs = bass.AP(qfl, tl * TAU + gi * 8,
                                              [[16 * SUP, 73], [1, 8], [SUP, 16]])
                                mm = TE.matmul(ps_s[:, qb_ + g * 128:qb_ + (g + 1) * 128],
                                               kstk[bi][0:73, gi * 128:(gi + 1) * 128],
                                               rhs, start=True, stop=True)
                                smm_t[gi] = s_smm.inc(mm, 1)
                            SE.wait_ge(s_smm.h, smm_t[b4 * 4 + 3])
                            if b4 == 0:
                                SE.wait_ge(s_cmm.h, est_free[bi])
                            exp_t[b4] = s_exp.inc(SE.activation(
                                est[bi][:, b4 * 512:(b4 + 1) * 512],
                                ps_s[:, qb_:qb_ + 512],
                                AF.Exp, bias=0.0, scale=0.125), 1)
                            exp_prev2[b4 % 2] = exp_t[b4]
                        for g8 in range(8):
                            gi = half * 8 + g8
                            if g8 == 0:
                                TE.wait_ge(s_vpr.h, vpr_tok)
                                TE.wait_ge(s_vev.h, vev_t[0] if half == 1 else vev_prev)
                            tr = TE.transpose(ptr[:, gi * 64:gi * 64 + 64],
                                              vpre[bi][0:64, gi * 128:(gi + 1) * 128],
                                              sid[0:64, 0:64])
                            vtr_t[gi] = s_vtr.inc(tr, 1)
                        VE.wait_ge(s_vtr.h, vtr_t[half * 8 + 7])
                        if half == 0:
                            VE.wait_ge(s_cmm.h, vstk_free[bi])
                        vev_t[half] = s_vev.inc(VE.tensor_copy(
                            bass.AP(vstk[bi], half * 8 * 65,
                                    [[16 * 65, 128], [65, 8], [1, 64]]),
                            bass.AP(ptr, half * 512,
                                    [[1024, 128], [64, 8], [1, 64]])), 1)
                        for b4 in (2 * half, 2 * half + 1):
                            for g in range(4):
                                gi = b4 * 4 + g
                                if g == 0:
                                    TE.wait_ge(s_exp.h, exp_t[b4])
                                    TE.wait_ge(s_vev.h, vev_t[half])
                                    TE.wait_ge(s_nrm.h, nrm_prev)
                                mm = TE.matmul(pcx[:, g * 128:g * 128 + 65],
                                               est[bi][:, gi * 128:(gi + 1) * 128],
                                               vstk[bi][:, gi * 65:(gi + 1) * 65],
                                               start=True, stop=True)
                                cmm_t[gi] = s_cmm.inc(mm, 1)
                            VE.wait_ge(s_cmm.h, cmm_t[b4 * 4 + 3])
                            VE.wait_ge(s_nrm.h, nrm_prev)
                            rcp_tok = s_rcp.inc(VE.reciprocal(
                                rcp4[:], bass.AP(pcx, 64, [[512, 128], [128, 4]])), 1)
                            SE.wait_ge(s_rcp.h, rcp_tok)
                            if b4 == 0:
                                SE.wait_ge(s_ctr.h, ctxn_free[bi])
                            for g in range(4):
                                gi = b4 * 4 + g
                                nr = SE.activation(
                                    ctxn[bi][:, gi * 64:(gi + 1) * 64],
                                    pcx[:, g * 128:g * 128 + 64],
                                    AF.Identity, bias=0.0, scale=rcp4[:, g:g + 1])
                                s_nrm.inc(nr, 1)
                            nrm_t[b4] = s_nrm.n
                            nrm_prev = s_nrm.n
                        for g8 in range(8):
                            gi = half * 8 + g8
                            if g8 == 0:
                                TE.wait_ge(s_nrm.h, nrm_t[2 * half + 1])
                                TE.wait_ge(s_cev.h, cev_prev)
                            tr = TE.transpose(ptc[0:64, g8 * 128:(g8 + 1) * 128],
                                              ctxn[bi][:, gi * 64:(gi + 1) * 64],
                                              sid[:])
                            ctr_t[gi] = s_ctr.inc(tr, 1)
                        VE.wait_ge(s_ctr.h, ctr_t[half * 8 + 7])
                        if half == 0:
                            VE.wait_ge(s_wom.h, ctxt_free[bi])
                        cev_t[half] = s_cev.inc(VE.tensor_copy(
                            bass.AP(ctxt[bi], half * 8 * 8,
                                    [[16 * 128, 64], [8, 8], [1, 8], [128, 16]]),
                            bass.AP(ptc, 0, [[1024, 64], [128, 8], [16, 8], [1, 16]])), 1)
                        cev_prev = cev_t[half]
                    kst_free[bi] = smm_t[15]
                    vpr_free[bi] = vtr_t[15]
                    est_free[bi] = cmm_t[15]
                    vstk_free[bi] = cmm_t[15]
                    ctxn_free[bi] = ctr_t[15]
                    vev_prev = vev_t[1]
                    TE.wait_ge(s_cev.h, cev_t[1])
                    TE.wait_ge(s_cst.h, s_cst.n)
                    TE.wait_ge(VE._sem, aev_prev)
                    for h in range(16):
                        for eh in range(2):
                            mm = TE.matmul(
                                pwo[:, eh * 512:(eh + 1) * 512],
                                ctxt[bi][0:64, h * 128:(h + 1) * 128],
                                wo[:, h * 1024 + eh * 512:h * 1024 + (eh + 1) * 512],
                                start=(h == 0), stop=(h == 15))
                    wom_tok = s_wom.inc(mm, 1)
                    ctxt_free[bi] = wom_tok
                    VE.wait_ge(s_wom.h, wom_tok)
                    VE.wait_ge(s_asp.h, atst_free[bi])
                    VE.tensor_copy(atst[bi][:], pwo[:])
                    aev_tok = VE._n
                    aev_prev = aev_tok
                    GE.wait_ge(VE._sem, aev_tok)
                    atst_free[bi] = s_asp.inc(GE.dma_start(
                        out=at_sp.ap()[ti * TAU:(ti + 1) * TAU, :],
                        in_=atst[bi][:]), 16)
                    s_mid.inc(VE.engine_nop(), 1)
                    fla_prev_smm = smm_t[15]
                    fla_prev_kst = kst_tok
                    fla_prev_vpr = vpr_tok

                def f_wchunk(mi, ci, src_d):
                    bi = ci % 2
                    ncc = src_d.shape[1] // 512
                    rb, cb = ci // ncc, ci % ncc
                    if mi == 0 and ci == 0:
                        VE.memset(wacc[:], 0.0)
                    SY.wait_ge(VE._sem, wfs_free[bi])
                    ldt = s_wfl.inc(SY.dma_start(
                        out=wfs[bi],
                        in_=src_d.ap()[rb * 128:(rb + 1) * 128,
                                       cb * 512:(cb + 1) * 512]), 16)
                    VE.wait_ge(s_wfl.h, ldt)
                    VE.scalar_tensor_tensor(wneg[:], wfs[bi], -1.0, wfs[bi],
                                            ALU.mult, ALU.max)
                    wfs_free[bi] = VE._n
                    VE.tensor_add(wacc[:], wacc[:], wneg[:])
                    if ci == 63:
                        s_wab.inc(VE.reduce_sum(wabs[:, mi:mi + 1], wacc[:],
                                                AX.X), 1)
                        VE.memset(wacc[:], 0.0)

                wsrc = [(0, wg_d), (1, wu_d), (2, wd_d)]
                wchunks = [(mi, ci, d) for (mi, d) in wsrc for ci in range(64)]
                wi = 0
                ln_toks = {}

                def ln_super(sj):
                    for tl in range(TPS):
                        f_ln_pre(sj * TPS + tl)
                    smt = f_ln_smalls(slice(sj * TPS, (sj + 1) * TPS))
                    for tl in range(TPS):
                        ln_toks[sj * TPS + tl] = f_ln(sj * TPS + tl, smt)

                for sj in range(min(2, NS)):
                    ln_super(sj)
                for si in range(NS):
                    f_qkv(si, [ln_toks[si * TPS + tl] for tl in range(TPS)])
                    for tl in range(TPS):
                        f_middle(si * TPS + tl, si)
                    fla_prev_mid = s_mid.n
                    if si + 2 < NS:
                        ln_super(si + 2)
                    n_w = 13 if si < NS - 1 else len(wchunks) - wi
                    for _ in range(n_w):
                        if wi < len(wchunks):
                            f_wchunk(*wchunks[wi]); wi += 1
                s_mum = sem("mum")
                TE.wait_ge(s_wab.h, s_wab.n)
                TE.wait_ge(s_nrm.h, s_nrm.n)
                for mi in range(3):
                    s_mum.inc(TE.matmul(pcx[0:1, mi:mi + 1], wabs[:, mi:mi + 1],
                                        onesc[:], start=True, stop=True), 1)
                SE.wait_ge(s_mum.h, s_mum.n)
                SE.copy(murow[:], pcx[0:1, 0:3])
                SE.mul(murow[:], murow[:], 1.0 / (D * IFF))
                se_tok = SE._n
                VE.wait_ge(SE._sem, se_tok)
                VE.tensor_scalar_max(murow[:], murow[:], 1e-5)
                VE.tensor_scalar_mul(murow[:], murow[:], 0.5)
                mur_tok = VE._n
                TE.wait_ge(VE._sem, mur_tok)
                for mi in range(3):
                    s_mum.inc(TE.matmul(pcx[:, 4 + mi:5 + mi], ones1[:],
                                        murow[0:1, mi:mi + 1], start=True, stop=True), 1)
                SE.wait_ge(s_mum.h, s_mum.n)
                SE.copy(mu3[:], pcx[:, 4:7])
                se_tok2 = SE._n
                VE.wait_ge(SE._sem, se_tok2)
                VE.tensor_scalar_mul(thr_n[:], mu3[:], -1.0)
                GE.wait_ge(s_qsp.h, s_qsp.n)
                GE.wait_ge(s_asp.h, s_asp.n)
                SY.wait_ge(s_fla.h, s_fla.n)

            barrier("A")

            esB = ExitStack()
            with esB:
                sbb = lambda n, sh, dt: esB.enter_context(nc.sbuf_tensor(n, sh, dt))
                wgt = sbb("wgt", [128, 8 * IFF], BF16)
                wut = sbb("wut", [128, 8 * IFF], BF16)
                s_twl = sem("twl")
                s_ts1 = sem("ts1")
                s_ts2 = sem("ts2")
                s_ter = sem("ter")

                with ExitStack() as esT:
                    st = lambda n, sh, dt: esT.enter_context(nc.sbuf_tensor(n, sh, dt))
                    twf = [st(f"twf{i}", [128, 2048], F32) for i in range(2)]
                    ts1 = [st(f"ts1_{i}", [128, 2048], BF16) for i in range(2)]
                    ts2 = [st(f"ts2_{i}", [128, 2048], BF16) for i in range(2)]
                    tst_free = [0, 0]

                    def f_tern(mi, src_d, dst):
                        for ck in range(16):
                            bi = ck % 2
                            ldt = s_twl.inc(SE.dma_start(
                                out=twf[bi][:],
                                in_=src_d.ap()[(ck // 2) * 128:(ck // 2) * 128 + 128,
                                               (ck % 2) * 2048:(ck % 2) * 2048 + 2048]), 16)
                            SE.wait_ge(s_twl.h, ldt)
                            SE.wait_ge(s_ter.h, tst_free[bi])
                            SE.activation(ts1[bi][:], twf[bi][:], AF.Sign,
                                          bias=thr_n[:, mi:mi + 1], scale=1.0)
                            t2 = s_ts2.inc(SE.activation(
                                ts2[bi][:], twf[bi][:], AF.Sign,
                                bias=mu3[:, mi:mi + 1], scale=1.0), 1)
                            VE.wait_ge(s_ts2.h, t2)
                            tt = s_ter.inc(VE.tensor_add(
                                dst[:, (ck // 2) * IFF + (ck % 2) * 2048:
                                    (ck // 2) * IFF + (ck % 2) * 2048 + 2048],
                                ts1[bi][:], ts2[bi][:]), 1)
                            tst_free[bi] = tt

                    f_tern(0, wg_d, wgt)
                    f_tern(1, wu_d, wut)

                barrier("T1")

                qtl = [sbb(f"qtl{i}", [128, D], BF16) for i in range(2)]
                gst = [sbb("gst0", [128, 1024], F32)] * 2
                xst = [sbb("xst0", [128, 1024], F32)] * 2
                hps = sbb("hps", [128, IFF], F32)
                htf2 = [sbb(f"htf{i}", [128, 1024], F32) for i in range(2)]
                hi02 = [sbb(f"hi0{i}", [128, 1024], mybir.dt.int32)
                        for i in range(2)]
                hs1 = sbb("hs1", [128, 1024], BF16)
                hs2 = sbb("hs2", [128, 1024], BF16)
                hcs = sbb("hcs", [128, 1024], BF16)
                h2b = sbb("h2b", [128, IFF], BF16)
                h2ts = sbb("h2ts", [128, IFF], BF16)
                pg = [esB.enter_context(nc.psum_tensor(f"pg{i}", [128, 1024], F32))
                      for i in range(2)]
                pu = [esB.enter_context(nc.psum_tensor(f"pu{i}", [128, 1024], F32))
                      for i in range(2)]

                s_qtl = sem("qtl")
                s_gmm = sem("gmm")
                s_umm = sem("umm")
                s_gev = sem("gev")
                s_rl2 = sem("rl2")
                s_hpw = sem("hpw")
                s_hmx = sem("hmx")
                s_h2a = sem("h2a")
                s_h2b = sem("h2b")
                s_h2t = sem("h2t")
                s_h2s = sem("h2s")

                VE.tensor_scalar(t_a[:, :], p_al[:, :], mu3[:, 0:1], None, ALU.mult)
                VE.tensor_scalar(t_b[:, :], p_al[:, :], mu3[:, 1:2], None, ALU.mult)
                VE.tensor_mul(t_a[:, :], t_a[:, :], t_a[:, :])
                VE.tensor_mul(p_ct[:, :], t_a[:, :], t_b[:, :])
                ctd_tok = VE._n

                qtl_free = [0, 0]
                qtl_toks = {}
                gst_free = [0, 0]
                xst_free = [0, 0]
                pg_free = [0, 0]
                pu_free = [0, 0]
                hps_q_free = [0, 0, 0, 0]
                h2b_free = [0]
                h2t_free = [0]
                shd_toks = {}
                fin_prev = [0]
                fin_last = [0]

                def emit_qtl_load(tj):
                    bj = tj % 2
                    SY.wait_ge(s_umm.h, qtl_free[bj])
                    qtl_toks[tj] = s_qtl.inc(SY.dma_start(
                        out=qtl[bj][:],
                        in_=bass.AP(qt_sp, tj * D * TAU,
                                    [[128, 128], [128 * 128, 8], [1, 128]])), 16)

                def chain_quarter(tj, hf):
                    b = hf % 2
                    colj = slice(tj, tj + 1)
                    cs2 = slice(hf * 1024, (hf + 1) * 1024)
                    SE.wait_ge(s_hmx.h, shd_toks[tj])
                    SE.activation(htf2[b][:], hps[:, cs2], AF.Copy,
                                  bias=0.0, scale=p_sh[:, colj])
                    ext = SE._n
                    hps_q_free[hf] = ext
                    VE.wait_ge(SE._sem, ext)
                    VE.tensor_copy(hi02[b][:], htf2[b][:])
                    cast = VE._n
                    GE.wait_ge(VE._sem, cast)
                    GE.tensor_tensor(htf2[b][:], htf2[b][:], hi02[b][:],
                                     ALU.subtract)
                    sub = GE._n
                    SE.wait_ge(GE._sem, sub)
                    SE.activation(hs1[:], htf2[b][:], AF.Sign,
                                  bias=chf[:, 0:1], scale=1.0)
                    SE.activation(hs2[:], htf2[b][:], AF.Sign,
                                  bias=chf[:, 1:2], scale=1.0)
                    sig = SE._n
                    GE.wait_ge(SE._sem, sig)
                    if fin_prev[0]:
                        GE.wait_ge(VE._sem, fin_prev[0])
                    GE.tensor_add(hcs[:], hs1[:], hs2[:])
                    addt = GE._n
                    VE.wait_ge(GE._sem, addt)
                    VE.wait_ge(s_h2t.h, h2b_free[0])
                    VE.scalar_tensor_tensor(h2b[:, cs2], hi02[b][:], 2.0,
                                            hcs[:], ALU.mult, ALU.add)
                    fin_prev[0] = VE._n

                def emit_h2_out(tj):
                    SY.wait_ge(VE._sem, fin_prev[0])
                    SY.wait_ge(s_h2s.h, h2t_free[0])
                    for hf in range(2):
                        h2t_tok = s_h2t.inc(SY.dma_start_transpose(
                            bass.AP(h2ts, hf * 2048, [[IFF, 128], [128, 16], [1, 128]]),
                            bass.AP(h2b, hf * 2048, [[IFF, 128], [128, 16], [1, 128]])), 16)
                    h2b_free[0] = h2t_tok
                    GE.wait_ge(s_h2t.h, h2t_tok)
                    h2t_free[0] = s_h2s.inc(GE.dma_start(
                        out=bass.AP(h2_sp, tj * IFF * TAU,
                                    [[128, 128], [128 * 128, 32], [1, 128]]),
                        in_=h2ts[:]), 16)

                emit_qtl_load(0)
                prev = None
                for ti in range(NT):
                    bi = ti % 2
                    col = slice(ti, ti + 1)
                    if ti + 1 < NT:
                        emit_qtl_load(ti + 1)
                    umm_t = [0] * 4
                    for q in range(4):
                        qs = q % 2
                        TE.wait_ge(s_qtl.h, qtl_toks[ti])
                        TE.wait_ge(s_gev.h, pg_free[qs])
                        TE.wait_ge(s_hpw.h, pu_free[qs])
                        for f in range(8):
                            lhs = qtl[bi][:, f * 128:(f + 1) * 128]
                            for c2 in range(2):
                                mmg = TE.matmul(
                                    pg[qs][:, c2 * 512:(c2 + 1) * 512], lhs,
                                    wgt[:, f * IFF + q * 1024 + c2 * 512:
                                        f * IFF + q * 1024 + (c2 + 1) * 512],
                                    start=(f == 0), stop=(f == 7))
                            for c2 in range(2):
                                mmu = TE.matmul(
                                    pu[qs][:, c2 * 512:(c2 + 1) * 512], lhs,
                                    wut[:, f * IFF + q * 1024 + c2 * 512:
                                        f * IFF + q * 1024 + (c2 + 1) * 512],
                                    start=(f == 0), stop=(f == 7))
                        gmm_t = s_gmm.inc(mmg, 1)
                        umm_t[q] = s_umm.inc(mmu, 1)
                        SE.wait_ge(s_gmm.h, gmm_t)
                        SE.wait_ge(s_rl2.h, gst_free[0])
                        gev_tok = s_gev.inc(
                            SE.copy(gst[bi][:], pg[qs][:]), 1)
                        pg_free[qs] = gev_tok
                        VE.wait_ge(s_gev.h, gev_tok)
                        rl2_tok = s_rl2.inc(VE.scalar_tensor_tensor(
                            xst[bi][:], gst[bi][:], 0.0, gst[bi][:],
                            ALU.max, ALU.mult), 1)
                        gst_free[0] = rl2_tok
                        if prev is not None:
                            chain_quarter(prev, q)
                        VE.wait_ge(s_umm.h, umm_t[q])
                        VE.wait_ge(SE._sem, hps_q_free[q])
                        hpw_tok = s_hpw.inc(VE.tensor_mul(
                            hps[:, q * 1024:(q + 1) * 1024], xst[bi][:], pu[qs][:]), 1)
                        xst_free[bi] = hpw_tok
                        pu_free[qs] = hpw_tok
                        s_hmx.inc(VE.reduce_max(
                            hx4[:, q:q + 1], hps[:, q * 1024:(q + 1) * 1024],
                            AX.X, apply_absolute_value=True), 1)
                    if prev is not None:
                        emit_h2_out(prev)
                    VE.wait_ge(VE._sem, ctd_tok)
                    VE.reduce_max(hx4[:, 4:5], hx4[:, 0:4], AX.X)
                    VE.tensor_mul(hx4[:, 5:6], p_ct[:, col], hx4[:, 4:5])
                    VE.tensor_scalar_max(hx4[:, 5:6], hx4[:, 5:6], 1e-5)
                    VE.reciprocal(hx4[:, 6:7], hx4[:, 5:6])
                    VE.tensor_mul(hx4[:, 6:7], hx4[:, 6:7], p_ct[:, col])
                    VE.tensor_scalar_mul(p_sh[:, col], hx4[:, 6:7], 127.0)
                    shd_toks[ti] = s_hmx.inc(VE.tensor_scalar_mul(
                        p_dd[:, col], hx4[:, 5:6], 1.0 / 254.0), 1)
                    qtl_free[bi] = umm_t[3]
                    prev = ti
                for hf in range(4):
                    chain_quarter(prev, hf)
                emit_h2_out(prev)
                GE.wait_ge(s_h2s.h, s_h2s.n)

            barrier("B1")

            esC = ExitStack()
            with esC:
                sc = lambda n, sh, dt: esC.enter_context(nc.sbuf_tensor(n, sh, dt))
                wdt = sc("wdt", [128, 32 * 1024], BF16)
                s_twl = sem("twl")
                s_ts2 = sem("ts2")
                s_ter = sem("ter")
                with ExitStack() as esT:
                    st = lambda n, sh, dt: esT.enter_context(nc.sbuf_tensor(n, sh, dt))
                    twf = [st(f"dwf{i}", [128, 1024], F32) for i in range(2)]
                    ts1 = [st(f"ds1_{i}", [128, 1024], BF16) for i in range(2)]
                    ts2 = [st(f"ds2_{i}", [128, 1024], BF16) for i in range(2)]
                    tst_free = [s_ter.n, s_ter.n]
                    for ck in range(32):
                        bi = ck % 2
                        ldt = s_twl.inc(SE.dma_start(
                            out=twf[bi][:],
                            in_=wd_d.ap()[ck * 128:(ck + 1) * 128, :]), 16)
                        SE.wait_ge(s_twl.h, ldt)
                        SE.wait_ge(s_ter.h, tst_free[bi])
                        SE.activation(ts1[bi][:], twf[bi][:], AF.Sign,
                                      bias=thr_n[:, 2:3], scale=1.0)
                        t2 = s_ts2.inc(SE.activation(
                            ts2[bi][:], twf[bi][:], AF.Sign,
                            bias=mu3[:, 2:3], scale=1.0), 1)
                        VE.wait_ge(s_ts2.h, t2)
                        tt = s_ter.inc(VE.tensor_add(
                            wdt[:, ck * 1024:(ck + 1) * 1024],
                            ts1[bi][:], ts2[bi][:]), 1)
                        tst_free[bi] = tt

                barrier("T2")

                h2l = [sc(f"h2l{i}", [128, IFF], BF16) for i in range(2)]
                x2 = [sc(f"x2_{i}", [128, D], F32) for i in range(2)]
                at2 = [sc(f"at2_{i}", [128, D], F32) for i in range(2)]
                ost = [sc(f"ost{i}", [128, D], F32) for i in range(2)]
                o1b = sc("o1b", [128, D], F32)
                pd = [esC.enter_context(nc.psum_tensor(f"pd{i}", [128, 1024], F32))
                      for i in range(2)]

                s_h2l = sem("h2l")
                s_x2l = sem("x2l")
                s_a2l = sem("a2l")
                s_dmm = sem("dmm")
                s_och = sem("och")
                s_osp = sem("osp")

                VE.tensor_scalar(p_dd[:, :], p_dd[:, :], mu3[:, 2:3], None, ALU.mult)
                ddf_tok = VE._n

                h2l_free = [0, 0]
                x2_free = [0, 0]
                at2_free = [0, 0]
                ost_free = [0, 0]
                pd_free = [0, 0]

                for ti in range(NT):
                    bi = ti % 2
                    col = slice(ti, ti + 1)
                    SE.wait_ge(s_dmm.h, h2l_free[bi])
                    h2tok = s_h2l.inc(SE.dma_start(
                        out=h2l[bi][:],
                        in_=bass.AP(h2_sp, ti * IFF * TAU,
                                    [[128, 128], [128 * 128, 32], [1, 128]])), 16)
                    SE.wait_ge(s_och.h, x2_free[bi])
                    x2tok = s_x2l.inc(SE.dma_start(
                        out=x2[bi][:], in_=x_in.ap()[ti * TAU:(ti + 1) * TAU, :]), 16)
                    SE.wait_ge(s_och.h, at2_free[bi])
                    a2tok = s_a2l.inc(SE.dma_start(
                        out=at2[bi][:], in_=at_sp.ap()[ti * TAU:(ti + 1) * TAU, :]), 16)
                    qs = ti % 2
                    TE.wait_ge(s_h2l.h, h2tok)
                    TE.wait_ge(s_och.h, pd_free[qs])
                    for f2 in range(32):
                        for c2 in range(2):
                            mm = TE.matmul(
                                pd[qs][:, c2 * 512:(c2 + 1) * 512],
                                h2l[bi][:, f2 * 128:(f2 + 1) * 128],
                                wdt[:, f2 * 1024 + c2 * 512:f2 * 1024 + (c2 + 1) * 512],
                                start=(f2 == 0), stop=(f2 == 31))
                    dmm_tok = s_dmm.inc(mm, 1)
                    h2l_free[bi] = dmm_tok
                    VE.wait_ge(s_dmm.h, dmm_tok)
                    VE.wait_ge(s_x2l.h, x2tok)
                    VE.wait_ge(s_a2l.h, a2tok)
                    VE.wait_ge(VE._sem, ddf_tok)
                    VE.wait_ge(s_osp.h, ost_free[bi])
                    VE.scalar_tensor_tensor(o1b[:], pd[qs][:], p_dd[:, col],
                                            x2[bi][:], ALU.mult, ALU.add)
                    VE.tensor_add(o1b[:], o1b[:], at2[bi][:])
                    och_tok = s_och.inc(VE.tensor_add(ost[bi][:], o1b[:], bbc[:]), 1)
                    pd_free[qs] = och_tok
                    x2_free[bi] = och_tok
                    at2_free[bi] = och_tok
                    GE.wait_ge(s_och.h, och_tok)
                    ost_free[bi] = s_osp.inc(GE.dma_start(
                        out=out_d.ap()[ti * TAU:(ti + 1) * TAU, :],
                        in_=ost[bi][:]), 16)
                if DBG:
                    s_dbg = sem("dbg")
                    for i, t in enumerate((p_m, p_rs, p_sg, p_sq, p_al, p_ct,
                                           p_sh, p_dd, mu3, thr_n, wabs)):
                        w = min(t.shape[1], 64)
                        s_dbg.inc(GE.dma_start(
                            out=dbg_p.ap()[:, i * 64:i * 64 + w], in_=t[:, 0:w]), 16)
                    GE.wait_ge(s_dbg.h, s_dbg.n)
                GE.wait_ge(s_osp.h, s_osp.n)
                SE.wait_ge(s_osp.h, s_osp.n)

    return nc


def make_host_layouts(W_qkv, b_qkv):
    perm = np.zeros(3 * D, dtype=np.int64)
    for r in range(3):
        for j in range(8):
            for hh in range(2):
                h = 2 * j + hh
                cols = 192 * h + 64 * r + np.arange(64)
                m = r * 8 + j
                perm[m * 128 + hh * 64: m * 128 + hh * 64 + 64] = cols
    Wp = np.ascontiguousarray(np.asarray(W_qkv)[:, perm])
    bp = np.asarray(b_qkv)[perm]
    btile = np.ascontiguousarray(bp.reshape(24, 128).T).astype(np.float32)
    return Wp, btile


def make_masks():
    mq = np.zeros((9, 16 * SUP), dtype=BF)
    mq[0, :] = 32.0
    cols = np.arange(16 * SUP)
    for tp in range(8):
        mq[1 + tp, (cols % 8) == tp] = 32.0
    mk = np.zeros((9, 16 * TAU), dtype=BF)
    mk[0, :] = -32.0
    for tp in range(8):
        mk[1 + tp, ((cols[:16 * TAU] % 128) // 16) == tp] = 32.0
    return mq, mk


def kernel(L0_attn_x, L0_attn_W_qkv, L0_attn_b_qkv, L0_attn_W_out,
           L0_attn_b_out, L0_ffn_w_gate, L0_ffn_w_up, L0_ffn_w_down):
    x = np.ascontiguousarray(np.asarray(L0_attn_x, dtype=np.float32))
    B, S, _ = x.shape
    Wp, btile = make_host_layouts(L0_attn_W_qkv, L0_attn_b_qkv)
    mq, mk = make_masks()
    shared = {
        "wqkv": Wp.astype(np.float32),
        "btile": btile,
        "wout": np.ascontiguousarray(np.asarray(L0_attn_W_out, dtype=np.float32)),
        "bout": np.asarray(L0_attn_b_out, dtype=np.float32).reshape(1, D),
        "wg_t": np.ascontiguousarray(np.asarray(L0_ffn_w_gate, dtype=np.float32).T),
        "wu_t": np.ascontiguousarray(np.asarray(L0_ffn_w_up, dtype=np.float32).T),
        "wd_t": np.ascontiguousarray(np.asarray(L0_ffn_w_down, dtype=np.float32).T),
        "ident": np.eye(128, dtype=BF),
        "maskq": mq,
        "maskk": mk,
    }
    nc = build_program(S)
    in_maps = [dict(shared, x=np.ascontiguousarray(x[b])) for b in range(B)]
    res = run_bass_kernel_spmd(nc, in_maps, core_ids=list(range(B)))
    out = np.stack([np.asarray(res.results[b]["out"]) for b in range(B)], axis=0)
    return out.astype(np.float32)


# version 12
# speedup vs baseline: 1.2717x; 1.0698x over previous
import numpy as np
import ml_dtypes
from contextlib import ExitStack
import concourse.bass as bass
import concourse.mybir as mybir
from concourse.bass_utils import run_bass_kernel_spmd

F32 = mybir.dt.float32
BF16 = mybir.dt.bfloat16
F16 = mybir.dt.float16
I16 = mybir.dt.int16
AF = mybir.ActivationFunctionType
ALU = mybir.AluOpType
AX = mybir.AxisListType
BF = ml_dtypes.bfloat16

D = 1024
NH = 16
IFF = 4096
TAU = 128
SUP = 256
TPS = SUP // TAU
MQ = 1536.0
N_CORES = 8
DBG = False


class Sem:
    __slots__ = ("h", "n")

    def __init__(self, es, nc, name):
        self.h = es.enter_context(nc.semaphore(name))
        self.n = 0

    def inc(self, instr, k=1):
        instr.then_inc(self.h, k)
        self.n += k
        return self.n


_WRAP_OPS = {
    "tensor_copy", "tensor_tensor", "tensor_add", "tensor_sub", "tensor_mul",
    "tensor_max", "tensor_scalar", "tensor_single_scalar", "tensor_scalar_add",
    "tensor_scalar_sub", "tensor_scalar_mul", "tensor_scalar_max",
    "tensor_scalar_min", "tensor_relu", "tensor_reduce", "reduce_sum",
    "reduce_max", "scalar_tensor_tensor", "reciprocal", "memset", "transpose",
    "activation", "square", "sqrt", "copy", "mul", "add", "sign", "select",
    "bn_stats", "bn_aggr", "engine_nop", "iota",
}


class EngSem:
    __slots__ = ("e",)

    def __init__(self, e):
        self.e = e

    @property
    def h(self):
        return self.e._sem

    @property
    def n(self):
        return self.e._n

    def inc(self, instr, k=1):
        return self.e._n


class SerialEng:

    def __init__(self, eng, nc, es, name):
        self._eng = eng
        self._sem = es.enter_context(nc.semaphore("ser_" + name))
        self._n = 0
        self._last = {}

    def _tnames(self, v, out):
        if isinstance(v, bass.AP):
            out.append(v.tensor.name)

    def __getattr__(self, name):
        fn = getattr(self._eng, name)
        if name not in _WRAP_OPS:
            return fn

        def wrapped(*a, **k):
            names = []
            for v in a:
                self._tnames(v, names)
            for v in k.values():
                self._tnames(v, names)
            need = 0
            for t in names:
                need = max(need, self._last.get(t, 0))
            if need:
                self._eng.wait_ge(self._sem, need)
            instr = fn(*a, **k)
            instr.then_inc(self._sem, 1)
            self._n += 1
            wtoks = []
            for v in list(a[:1]) + [k[x] for x in ("accum_out", "out") if x in k]:
                if isinstance(v, bass.AP):
                    self._last[v.tensor.name] = self._n
            if a and isinstance(a[0], bass.AP):
                self._last[a[0].tensor.name] = self._n
            return instr

        return wrapped


def build_program(S):
    NT = S // TAU
    NS = S // SUP
    nc = bass.Bass("TRN2", target_bir_lowering=False,
                   detect_race_conditions=False)

    x_in = nc.dram_tensor("x", [S, D], F32, kind="ExternalInput")
    wqkv_d = nc.dram_tensor("wqkv", [D, 3 * D], F32, kind="ExternalInput")
    btile_d = nc.dram_tensor("btile", [128, 24], F32, kind="ExternalInput")
    wout_d = nc.dram_tensor("wout", [D, D], F32, kind="ExternalInput")
    bout_d = nc.dram_tensor("bout", [1, D], F32, kind="ExternalInput")
    wg_d = nc.dram_tensor("wg_t", [D, IFF], F32, kind="ExternalInput")
    wu_d = nc.dram_tensor("wu_t", [D, IFF], F32, kind="ExternalInput")
    wd_d = nc.dram_tensor("wd_t", [IFF, D], F32, kind="ExternalInput")
    id_d = nc.dram_tensor("ident", [128, 128], BF16, kind="ExternalInput")
    mq_d = nc.dram_tensor("maskq", [9, 16 * SUP], BF16, kind="ExternalInput")
    mk_d = nc.dram_tensor("maskk", [9, 16 * TAU], BF16, kind="ExternalInput")
    out_d = nc.dram_tensor("out", [S, D], F32, kind="ExternalOutput")
    kind_dbg = {"kind": "ExternalOutput"} if DBG else {}
    qt_sp = nc.dram_tensor("qt_sp", [NT * D * TAU], BF16, **kind_dbg)
    at_sp = nc.dram_tensor("at_sp", [S, D], F32, **kind_dbg)
    h2_sp = nc.dram_tensor("h2_sp", [NT * IFF * TAU], BF16, **kind_dbg)
    if DBG:
        dbg_p = nc.dram_tensor("dbg_p", [128, 1024], F32, kind="ExternalOutput")

    es = ExitStack()
    with es:
        TE, SY = nc.tensor, nc.sync
        VE = SerialEng(nc.vector, nc, es, "ve")
        SE = SerialEng(nc.scalar, nc, es, "se")
        GE = SerialEng(nc.gpsimd, nc, es, "ge")
        sb = lambda n, sh, dt: es.enter_context(nc.sbuf_tensor(n, sh, dt))
        sid = sb("sid", [128, 128], BF16)
        p_m = sb("p_m", [128, NT], F32)
        p_rs = sb("p_rs", [128, NT], F32)
        p_sg = sb("p_sg", [128, NT], F32)
        p_sq = sb("p_sq", [128, NT], F32)
        p_qb = sb("p_qb", [128, NT], F32)
        p_nm = sb("p_nm", [128, NT], F32)
        p_al = sb("p_al", [128, NT], F32)
        p_ct = sb("p_ct", [128, NT], F32)
        p_sh = sb("p_sh", [128, NT], F32)
        p_dd = sb("p_dd", [128, NT], F32)
        t_a = sb("t_a", [128, NT], F32)
        t_b = sb("t_b", [128, NT], F32)
        p_mx = sb("p_mx", [128, NT], F32)
        p_mn = sb("p_mn", [128, NT], F32)
        bbc = sb("bbc", [128, D], F32)
        brow = sb("brow", [1, D], F32)
        ones1 = sb("ones1", [1, 128], F32)
        onesc = sb("onesc", [128, 1], F32)
        bcol = sb("bcol", [128, 24], F32)
        wabs = sb("wabs", [128, 3], F32)
        mu3 = sb("mu3", [128, 3], F32)
        thr_n = sb("thr_n", [128, 3], F32)
        murow = sb("murow", [1, 3], F32)
        hx4 = sb("hx4", [128, 8], F32)
        rcp4 = sb("rcp4", [128, 4], F32)
        chf = sb("chf", [128, 2], F32)

        SEMS = {}
        _VE_SEMS = {"lnv", "qbb", "kst", "vev", "rcp", "cev", "mid", "wab",
                    "ter", "rl2", "hpw", "hmx", "h2b", "och", "ini"}
        _SE_SEMS = {"sqd", "sig", "xnb", "fme", "exp", "nrm", "aev", "wap",
                    "ts2", "gev", "h2a", "bbc"}
        _GE_SEMS = {"vpr"}

        def sem(name):
            if name not in SEMS:
                if name in _VE_SEMS:
                    SEMS[name] = EngSem(VE)
                elif name in _SE_SEMS:
                    SEMS[name] = EngSem(SE)
                elif name in _GE_SEMS:
                    SEMS[name] = EngSem(GE)
                else:
                    SEMS[name] = Sem(es, nc, name)
            return SEMS[name]

        block = es.enter_context(nc.Block())

        def barrier(tag):
            b = sem("bar_" + tag)
            for eng in (TE, VE, SE, GE, SY):
                b.inc(eng.nop(), 1)
            for eng in (TE, VE, SE, GE, SY):
                eng.wait_ge(b.h, b.n)

        @block.sync
        def _(_q):
            esA = ExitStack()
            with esA:
                sa = lambda n, sh, dt: esA.enter_context(nc.sbuf_tensor(n, sh, dt))
                wq = sa("wq", [128, 8 * 3072], BF16)
                wo = sa("wo", [64, 16 * 1024], BF16)
                xr = [sa(f"xr{i}", [128, D], F32) for i in range(2)]
                xsq = sa("xsq", [128, D], F32)
                xnst = [sa(f"xnst{i}", [128, D], BF16) for i in range(2)]
                qtf = sa("qtf", [128, D], F32)
                qi0 = sa("qi0", [128, D], mybir.dt.int32)
                qd = xsq
                qs1 = sa("qs1", [128, D], BF16)
                qs2 = sa("qs2", [128, D], BF16)
                qcs = sa("qcs", [128, D], BF16)
                qbst = [sa(f"qbst{i}", [128, D], BF16) for i in range(2)]
                qts = [sa(f"qts{i}", [128, D], BF16) for i in range(2)]
                xnt = sa("xnt", [128, 4 * D], BF16)
                fmst = sa("fmst", [128, 8 * SUP], BF16)
                qfl = sa("qfl", [128, 16 * SUP], BF16)
                kfl = sa("kfl", [128, 16 * SUP], BF16)
                vfl = sa("vfl", [128, 16 * SUP], BF16)
                kstk = [sa("kstk0", [128, 16 * TAU], BF16)] * 2
                vpre = [sa(f"vpre{i}", [128, 16 * TAU], BF16) for i in range(2)]
                est = [sa("est0", [128, 16 * TAU], BF16)] * 2
                vstk = [sa("vstk0", [128, 16 * 65], BF16)] * 2
                ctxn = [sa(f"ctxn{i}", [128, 16 * 64], BF16) for i in range(2)]
                ctxt = [sa(f"ctxt{i}", [128, 16 * 128], BF16) for i in range(2)]
                atst = [sa("atst0", [128, D], F32)] * 2
                wfs_t = [sa(f"wfs{i}", [128, 512], F32) for i in range(2)]
                wfs = [t[:] for t in wfs_t]
                wneg = sa("wneg", [128, 512], F32)
                wacc = sa("wacc", [128, 512], F32)

                pq = esA.enter_context(nc.psum_tensor("pq", [128, 512], F32))
                pwo = esA.enter_context(nc.psum_tensor("pwo", [128, 1024], F32))
                ps_s = esA.enter_context(nc.psum_tensor("ps_s", [128, 1024], F32))
                pcx = esA.enter_context(nc.psum_tensor("pcx", [128, 512], F32))
                ptr = esA.enter_context(nc.psum_tensor("ptr", [128, 1024], BF16))
                ptc = esA.enter_context(nc.psum_tensor("ptc", [128, 1024], BF16))

                s_w0 = sem("w0")
                s_cst = sem("cst")
                s_msk = sem("msk")
                s_ini = sem("ini")
                s_bbc = sem("bbc")
                s_w0.inc(SY.dma_start(out=sid[:], in_=id_d.ap()), 16)
                s_w0.inc(SY.dma_start(out=brow[:], in_=bout_d.ap()), 16)
                s_w0.inc(SY.dma_start(out=bcol[:], in_=btile_d.ap()), 16)
                s_msk.inc(SY.dma_start(out=qfl[64:73, :], in_=mq_d.ap()), 16)
                s_msk.inc(SY.dma_start(out=kstk[0][64:73, :], in_=mk_d.ap()), 16)
                for c in range(8):
                    s_cst.inc(GE.dma_start(
                        out=wq[:, c * 3072:(c + 1) * 3072],
                        in_=wqkv_d.ap()[c * 128:(c + 1) * 128, :]), 16)
                for h in range(16):
                    s_cst.inc(GE.dma_start(
                        out=wo[:, h * 1024:(h + 1) * 1024],
                        in_=wout_d.ap()[h * 64:(h + 1) * 64, :]), 16)
                VE.memset(ones1[:], 1.0)
                VE.memset(onesc[:], 1.0)
                VE.memset(wabs[:], 0.0)
                VE.memset(chf[:, 0:1], -0.5)
                VE.memset(chf[:, 1:2], 0.5)
                VE.memset(bass.AP(vstk[0], 64, [[16 * 65, 128], [65, 16], [1, 1]]), 1.0)
                s_ini.inc(VE.engine_nop(), 1)
                TE.wait_ge(s_w0.h, s_w0.n)
                TE.wait_ge(s_ini.h, s_ini.n)
                s_mmb = sem("mmb")
                TE.matmul(pq[:], ones1[:], brow[0:1, 0:512], start=True, stop=True)
                s_mmb.inc(TE.matmul(ps_s[:, 0:512], ones1[:], brow[0:1, 512:1024],
                                    start=True, stop=True), 1)
                SE.wait_ge(s_mmb.h, s_mmb.n)
                SE.copy(bbc[:, 0:512], pq[:])
                s_bbc.inc(SE.copy(bbc[:, 512:1024], ps_s[:, 0:512]), 1)
                qkv_bank_free = 0
                if DBG:
                    barrier("pre")

                s_xld = sem("xld")
                s_lnv = sem("lnv")
                s_sqd = sem("sqd")
                s_sig = sem("sig")
                s_xnb = sem("xnb")
                s_qbb = sem("qbb")
                s_xnt = sem("xnt")
                s_qtt = sem("qtt")
                s_qsp = sem("qsp")
                s_qkv = sem("qkv")
                s_fme = sem("fme")
                s_fla = sem("fla")
                s_kst = sem("kst")
                s_vpr = sem("vpr")
                s_smm = sem("smm")
                s_exp = sem("exp")
                s_vtr = sem("vtr")
                s_vev = sem("vev")
                s_cmm = sem("cmm")
                s_rcp = sem("rcp")
                s_nrm = sem("nrm")
                s_ctr = sem("ctr")
                s_cev = sem("cev")
                s_wom = sem("wom")
                s_aev = sem("aev")
                s_asp = sem("asp")
                s_mid = sem("mid")
                s_wfl = sem("wfl")
                s_wab = sem("wab")
                s_wap = sem("wap")

                xr_free = [0, 0]
                xn_free = [0, 0]
                qb_free = [0, 0]
                qt_free = [0, 0]
                xnt_free = [0, 0, 0, 0]
                xsq_free = 0
                kst_free = [0, 0]
                vpr_free = [0, 0]
                est_free = [0, 0]
                vstk_free = [0, 0]
                ctxn_free = [0, 0]
                ctxt_free = [0, 0]
                atst_free = [0, 0]
                wfs_free = [0, 0]
                fla_prev_mid = 0
                fla_prev_smm = 0
                fla_prev_kst = 0
                fla_prev_vpr = 0
                fme_tok = [0] * 24
                q_sig_prev = [0]
                q_cs_prev = [0]
                q_fin_prev = [0]
                wab_se_prev = [0]
                exp_prev2 = [0, 0]
                vev_prev = 0
                cev_prev = 0
                nrm_prev = 0
                aev_prev = 0

                def f_ln_pre(ti):
                    nonlocal xsq_free
                    bi = ti % 2
                    col = slice(ti, ti + 1)
                    SE.wait_ge(s_lnv.h, xr_free[bi])
                    xtok = s_xld.inc(SE.dma_start(
                        out=xr[bi][:], in_=x_in.ap()[ti * TAU:(ti + 1) * TAU, :]), 16)
                    VE.wait_ge(s_xld.h, xtok)
                    VE.reduce_sum(t_a[:, col], xr[bi][:], AX.X)
                    VE.reduce_max(p_mx[:, col], xr[bi][:], AX.X)
                    VE.tensor_reduce(p_mn[:, col], xr[bi][:], AX.X, ALU.min)
                    SE.wait_ge(s_xld.h, xtok)
                    SE.wait_ge(s_lnv.h, xsq_free)
                    sq_tok = s_sqd.inc(SE.square(xsq[:], xr[bi][:]), 1)
                    VE.wait_ge(s_sqd.h, sq_tok)
                    VE.reduce_sum(t_b[:, col], xsq[:], AX.X)
                    xsq_free = s_lnv.inc(VE.engine_nop(), 1)
                    return xtok

                def f_ln_smalls(cols):
                    c = cols
                    VE.tensor_scalar_mul(p_m[:, c], t_a[:, c], 1.0 / D)
                    VE.tensor_scalar_mul(t_b[:, c], t_b[:, c], 1.0 / D)
                    VE.tensor_mul(t_a[:, c], p_m[:, c], p_m[:, c])
                    VE.tensor_sub(t_b[:, c], t_b[:, c], t_a[:, c])
                    var_tok = s_lnv.inc(
                        VE.tensor_scalar_add(t_b[:, c], t_b[:, c], 1e-5), 1)
                    SE.wait_ge(s_lnv.h, var_tok)
                    sig_tok = s_sig.inc(SE.sqrt(p_sg[:, c], t_b[:, c]), 1)
                    VE.wait_ge(s_sig.h, sig_tok)
                    VE.reciprocal(p_rs[:, c], p_sg[:, c])
                    VE.tensor_sub(p_mx[:, c], p_mx[:, c], p_m[:, c])
                    VE.tensor_sub(p_mn[:, c], p_m[:, c], p_mn[:, c])
                    VE.tensor_max(p_mx[:, c], p_mx[:, c], p_mn[:, c])
                    VE.tensor_scalar_mul(p_mn[:, c], p_sg[:, c], 1e-5)
                    VE.tensor_max(p_mx[:, c], p_mx[:, c], p_mn[:, c])
                    VE.reciprocal(p_sq[:, c], p_mx[:, c])
                    VE.tensor_scalar_mul(p_sq[:, c], p_sq[:, c], 127.0)
                    VE.tensor_mul(p_al[:, c], p_mx[:, c], p_rs[:, c])
                    VE.tensor_scalar_mul(p_al[:, c], p_al[:, c], 1.0 / 254.0)
                    VE.tensor_mul(p_nm[:, c], p_m[:, c], p_rs[:, c])
                    VE.tensor_scalar_mul(p_nm[:, c], p_nm[:, c], -1.0)
                    VE.tensor_mul(p_qb[:, c], p_m[:, c], p_sq[:, c])
                    sml_tok = s_lnv.inc(VE.tensor_scalar_mul(
                        p_qb[:, c], p_qb[:, c], -1.0), 1)
                    return sml_tok

                def f_ln(ti, sml_tok):
                    bi = ti % 2
                    col = slice(ti, ti + 1)
                    SE.wait_ge(s_lnv.h, sml_tok)
                    SE.wait_ge(s_xnt.h, xn_free[bi])
                    xnb_tok = s_xnb.inc(SE.activation(
                        xnst[bi][:], xr[bi][:], AF.Identity,
                        bias=p_nm[:, col], scale=p_rs[:, col]), 1)
                    VE.tensor_scalar(qtf[:], xr[bi][:],
                                     p_sq[:, col], p_qb[:, col], ALU.mult, ALU.add)
                    VE.tensor_copy(qi0[:], qtf[:])
                    ve_q = VE._n
                    GE.wait_ge(VE._sem, ve_q)
                    if q_sig_prev[0]:
                        GE.wait_ge(SE._sem, q_sig_prev[0])
                    GE.tensor_tensor(qd[:], qtf[:], qi0[:], ALU.subtract)
                    ge_d = GE._n
                    SE.wait_ge(GE._sem, ge_d)
                    if q_cs_prev[0]:
                        SE.wait_ge(GE._sem, q_cs_prev[0])
                    SE.activation(qs1[:], qd[:], AF.Sign, bias=chf[:, 0:1], scale=1.0)
                    SE.activation(qs2[:], qd[:], AF.Sign, bias=chf[:, 1:2], scale=1.0)
                    q_sig_prev[0] = SE._n
                    GE.wait_ge(SE._sem, q_sig_prev[0])
                    if q_fin_prev[0]:
                        GE.wait_ge(VE._sem, q_fin_prev[0])
                    GE.tensor_add(qcs[:], qs1[:], qs2[:])
                    q_cs_prev[0] = GE._n
                    VE.wait_ge(GE._sem, q_cs_prev[0])
                    VE.wait_ge(s_qtt.h, qb_free[bi])
                    qbb_tok = s_qbb.inc(VE.scalar_tensor_tensor(
                        qbst[bi][:], qi0[:], 2.0, qcs[:], ALU.mult, ALU.add), 1)
                    q_fin_prev[0] = qbb_tok
                    xr_free[bi] = s_lnv.inc(VE.engine_nop(), 1)
                    slot = ti % 4
                    SY.wait_ge(s_xnb.h, xnb_tok)
                    SY.wait_ge(s_qkv.h, xnt_free[slot])
                    xnt_tok = s_xnt.inc(SY.dma_start_transpose(
                        bass.AP(xnt, slot * D, [[4 * D, 128], [128, 8], [1, 128]]),
                        bass.AP(xnst[bi], 0, [[D, 128], [128, 8], [1, 128]])), 16)
                    xn_free[bi] = xnt_tok
                    SY.wait_ge(s_qbb.h, qbb_tok)
                    SY.wait_ge(s_qsp.h, qt_free[bi])
                    qtt_tok = s_qtt.inc(SY.dma_start_transpose(
                        bass.AP(qts[bi], 0, [[D, 128], [128, 8], [1, 128]]),
                        bass.AP(qbst[bi], 0, [[D, 128], [128, 8], [1, 128]])), 16)
                    qb_free[bi] = qtt_tok
                    GE.wait_ge(s_qtt.h, qtt_tok)
                    qsp_tok = s_qsp.inc(GE.dma_start(
                        out=bass.AP(qt_sp, ti * D * TAU,
                                    [[128, 128], [128 * 128, 8], [1, 128]]),
                        in_=qts[bi][:]), 16)
                    qt_free[bi] = qsp_tok
                    return xnt_tok

                def f_qkv(si, xnt_toks):
                    nonlocal fla_prev_mid, qkv_bank_free
                    TE.wait_ge(s_cst.h, s_cst.n)
                    TE.wait_ge(s_msk.h, s_msk.n)
                    TE.wait_ge(s_bbc.h, s_bbc.n)
                    TE.wait_ge(s_xnt.h, max(xnt_toks))
                    if si > 0:
                        TE.wait_ge(s_fla.h, s_fla.n)
                    sl0 = (si * TPS) % 4
                    for r, dst in ((0, qfl), (1, kfl), (2, vfl)):
                        fla_reg = s_fla.n
                        for j in range(8):
                            m = r * 8 + j
                            TE.wait_ge(s_fme.h, qkv_bank_free)
                            for f in range(8):
                                rhs = bass.AP(xnt, sl0 * D + f * 128,
                                              [[4 * D, 128], [D, TPS], [1, 128]])
                                mm = TE.matmul(
                                    pq[:, 0:SUP],
                                    wq[:, f * 3072 + m * 128:f * 3072 + (m + 1) * 128],
                                    rhs, start=(f == 0), stop=(f == 7))
                            qkv_tok = s_qkv.inc(mm, 1)
                            SE.wait_ge(s_qkv.h, qkv_tok)
                            if r > 0 or si > 0:
                                SE.wait_ge(s_fla.h, fla_reg)
                            fme_tok[m] = s_fme.inc(SE.activation(
                                fmst[:, j * SUP:(j + 1) * SUP],
                                pq[:, 0:SUP],
                                AF.Identity, bias=bcol[:, m:m + 1], scale=1.0), 1)
                            qkv_bank_free = fme_tok[m]
                        SY.wait_ge(s_fme.h, fme_tok[r * 8 + 7])
                        if r == 0:
                            SY.wait_ge(s_mid.h, fla_prev_mid)
                            SY.wait_ge(s_smm.h, fla_prev_smm)
                            SY.wait_ge(s_kst.h, fla_prev_kst)
                            SY.wait_ge(s_vpr.h, fla_prev_vpr)
                        s_fla.inc(SY.dma_start(
                            out=bass.AP(dst, 0,
                                        [[16 * SUP, 64], [2 * SUP, 8], [1, SUP]]),
                            in_=bass.AP(fmst, 0,
                                        [[8 * SUP, 64], [SUP, 8], [1, SUP]])), 16)
                        s_fla.inc(SY.dma_start(
                            out=bass.AP(dst, SUP,
                                        [[16 * SUP, 64], [2 * SUP, 8], [1, SUP]]),
                            in_=bass.AP(fmst, 64 * 8 * SUP,
                                        [[8 * SUP, 64], [SUP, 8], [1, SUP]])), 16)
                    for tl in range(TPS):
                        xnt_free[(si * TPS + tl) % 4] = s_qkv.n

                def f_middle(ti, si, hook=None):
                    nonlocal vev_prev, cev_prev, nrm_prev, aev_prev
                    nonlocal fla_prev_smm, fla_prev_kst, fla_prev_vpr
                    bi = ti % 2
                    tl = ti - si * TPS
                    fla_tok = s_fla.n
                    VE.wait_ge(s_fla.h, fla_tok)
                    VE.wait_ge(s_smm.h, kst_free[bi])
                    kst_tok = s_kst.inc(VE.tensor_copy(
                        kstk[bi][0:64, :],
                        bass.AP(kfl, tl * TAU,
                                [[16 * SUP, 64], [8, 16], [1, 8], [SUP, 16]])), 1)
                    GE.wait_ge(s_fla.h, fla_tok)
                    GE.wait_ge(s_vtr.h, vpr_free[bi])
                    vpr_tok = s_vpr.inc(GE.tensor_copy(
                        vpre[bi][0:64, :],
                        bass.AP(vfl, tl * TAU,
                                [[16 * SUP, 64], [8, 16], [1, 8], [SUP, 16]])), 1)
                    smm_t = [0] * 16
                    exp_t = [0] * 4
                    vtr_t = [0] * 16
                    cmm_t = [0] * 16
                    ctr_t = [0] * 16
                    nrm_t = [0] * 4
                    vev_t = [0, 0]
                    cev_t = [0, 0]
                    for half in range(2):
                        for b4 in (2 * half, 2 * half + 1):
                            qb_ = (b4 % 2) * 512
                            for g in range(4):
                                gi = b4 * 4 + g
                                if g == 0:
                                    TE.wait_ge(s_kst.h, kst_tok)
                                    TE.wait_ge(s_exp.h, exp_prev2[b4 % 2])
                                rhs = bass.AP(qfl, tl * TAU + gi * 8,
                                              [[16 * SUP, 73], [1, 8], [SUP, 16]])
                                mm = TE.matmul(ps_s[:, qb_ + g * 128:qb_ + (g + 1) * 128],
                                               kstk[bi][0:73, gi * 128:(gi + 1) * 128],
                                               rhs, start=True, stop=True)
                                smm_t[gi] = s_smm.inc(mm, 1)
                            SE.wait_ge(s_smm.h, smm_t[b4 * 4 + 3])
                            if b4 == 0:
                                SE.wait_ge(s_cmm.h, est_free[bi])
                            exp_t[b4] = s_exp.inc(SE.activation(
                                est[bi][:, b4 * 512:(b4 + 1) * 512],
                                ps_s[:, qb_:qb_ + 512],
                                AF.Exp, bias=0.0, scale=0.125), 1)
                            exp_prev2[b4 % 2] = exp_t[b4]
                        for g8 in range(8):
                            gi = half * 8 + g8
                            if g8 == 0:
                                TE.wait_ge(s_vpr.h, vpr_tok)
                                TE.wait_ge(s_vev.h, vev_t[0] if half == 1 else vev_prev)
                            tr = TE.transpose(ptr[:, gi * 64:gi * 64 + 64],
                                              vpre[bi][0:64, gi * 128:(gi + 1) * 128],
                                              sid[0:64, 0:64])
                            vtr_t[gi] = s_vtr.inc(tr, 1)
                        VE.wait_ge(s_vtr.h, vtr_t[half * 8 + 7])
                        if half == 0:
                            VE.wait_ge(s_cmm.h, vstk_free[bi])
                        vev_t[half] = s_vev.inc(VE.tensor_copy(
                            bass.AP(vstk[bi], half * 8 * 65,
                                    [[16 * 65, 128], [65, 8], [1, 64]]),
                            bass.AP(ptr, half * 512,
                                    [[1024, 128], [64, 8], [1, 64]])), 1)
                        for b4 in (2 * half, 2 * half + 1):
                            for g in range(4):
                                gi = b4 * 4 + g
                                if g == 0:
                                    TE.wait_ge(s_exp.h, exp_t[b4])
                                    TE.wait_ge(s_vev.h, vev_t[half])
                                    TE.wait_ge(s_nrm.h, nrm_prev)
                                mm = TE.matmul(pcx[:, g * 128:g * 128 + 65],
                                               est[bi][:, gi * 128:(gi + 1) * 128],
                                               vstk[bi][:, gi * 65:(gi + 1) * 65],
                                               start=True, stop=True)
                                cmm_t[gi] = s_cmm.inc(mm, 1)
                            VE.wait_ge(s_cmm.h, cmm_t[b4 * 4 + 3])
                            VE.wait_ge(s_nrm.h, nrm_prev)
                            rcp_tok = s_rcp.inc(VE.reciprocal(
                                rcp4[:], bass.AP(pcx, 64, [[512, 128], [128, 4]])), 1)
                            SE.wait_ge(s_rcp.h, rcp_tok)
                            if b4 == 0:
                                SE.wait_ge(s_ctr.h, ctxn_free[bi])
                            for g in range(4):
                                gi = b4 * 4 + g
                                nr = SE.activation(
                                    ctxn[bi][:, gi * 64:(gi + 1) * 64],
                                    pcx[:, g * 128:g * 128 + 64],
                                    AF.Identity, bias=0.0, scale=rcp4[:, g:g + 1])
                                s_nrm.inc(nr, 1)
                            nrm_t[b4] = s_nrm.n
                            nrm_prev = s_nrm.n
                        for g8 in range(8):
                            gi = half * 8 + g8
                            if g8 == 0:
                                TE.wait_ge(s_nrm.h, nrm_t[2 * half + 1])
                                TE.wait_ge(s_cev.h, cev_prev)
                            tr = TE.transpose(ptc[0:64, g8 * 128:(g8 + 1) * 128],
                                              ctxn[bi][:, gi * 64:(gi + 1) * 64],
                                              sid[:])
                            ctr_t[gi] = s_ctr.inc(tr, 1)
                        VE.wait_ge(s_ctr.h, ctr_t[half * 8 + 7])
                        if half == 0:
                            VE.wait_ge(s_wom.h, ctxt_free[bi])
                        cev_t[half] = s_cev.inc(VE.tensor_copy(
                            bass.AP(ctxt[bi], half * 8 * 8,
                                    [[16 * 128, 64], [8, 8], [1, 8], [128, 16]]),
                            bass.AP(ptc, 0, [[1024, 64], [128, 8], [16, 8], [1, 16]])), 1)
                        cev_prev = cev_t[half]
                    kst_free[bi] = smm_t[15]
                    vpr_free[bi] = vtr_t[15]
                    est_free[bi] = cmm_t[15]
                    vstk_free[bi] = cmm_t[15]
                    ctxn_free[bi] = ctr_t[15]
                    vev_prev = vev_t[1]
                    if hook is not None:
                        hook()
                    TE.wait_ge(s_cev.h, cev_t[1])
                    TE.wait_ge(s_cst.h, s_cst.n)
                    TE.wait_ge(VE._sem, aev_prev)
                    for h in range(16):
                        for eh in range(2):
                            mm = TE.matmul(
                                pwo[:, eh * 512:(eh + 1) * 512],
                                ctxt[bi][0:64, h * 128:(h + 1) * 128],
                                wo[:, h * 1024 + eh * 512:h * 1024 + (eh + 1) * 512],
                                start=(h == 0), stop=(h == 15))
                    wom_tok = s_wom.inc(mm, 1)
                    ctxt_free[bi] = wom_tok
                    VE.wait_ge(s_wom.h, wom_tok)
                    VE.wait_ge(s_asp.h, atst_free[bi])
                    VE.tensor_copy(atst[bi][:], pwo[:])
                    aev_tok = VE._n
                    aev_prev = aev_tok
                    GE.wait_ge(VE._sem, aev_tok)
                    atst_free[bi] = s_asp.inc(GE.dma_start(
                        out=at_sp.ap()[ti * TAU:(ti + 1) * TAU, :],
                        in_=atst[bi][:]), 16)
                    s_mid.inc(VE.engine_nop(), 1)
                    fla_prev_smm = smm_t[15]
                    fla_prev_kst = kst_tok
                    fla_prev_vpr = vpr_tok

                def f_wchunk(mi, ci, src_d):
                    bi = ci % 2
                    ncc = src_d.shape[1] // 512
                    rb, cb = ci // ncc, ci % ncc
                    if mi == 0 and ci == 0:
                        VE.memset(wacc[:], 0.0)
                    SY.wait_ge(VE._sem, wfs_free[bi])
                    ldt = s_wfl.inc(SY.dma_start(
                        out=wfs[bi],
                        in_=src_d.ap()[rb * 128:(rb + 1) * 128,
                                       cb * 512:(cb + 1) * 512]), 16)
                    VE.wait_ge(s_wfl.h, ldt)
                    VE.scalar_tensor_tensor(wneg[:], wfs[bi], -1.0, wfs[bi],
                                            ALU.mult, ALU.max)
                    wfs_free[bi] = VE._n
                    VE.tensor_add(wacc[:], wacc[:], wneg[:])
                    if ci == 63:
                        s_wab.inc(VE.reduce_sum(wabs[:, mi:mi + 1], wacc[:],
                                                AX.X), 1)
                        VE.memset(wacc[:], 0.0)

                wsrc = [(0, wg_d), (1, wu_d), (2, wd_d)]
                wchunks = [(mi, ci, d) for (mi, d) in wsrc for ci in range(64)]
                wi = 0
                ln_toks = {}
                smts = {}

                def ln_pre_super(sj):
                    for tl in range(TPS):
                        f_ln_pre(sj * TPS + tl)
                    smts[sj] = f_ln_smalls(slice(sj * TPS, (sj + 1) * TPS))

                def ln_fin(tj):
                    ln_toks[tj] = f_ln(tj, smts[tj // TPS])

                for sj in range(min(2, NS)):
                    ln_pre_super(sj)
                    for tl in range(TPS):
                        ln_fin(sj * TPS + tl)
                if NS > 2:
                    ln_pre_super(2)
                for si in range(NS):
                    f_qkv(si, [ln_toks[si * TPS + tl] for tl in range(TPS)])
                    t2 = (si + 2) * TPS
                    f_middle(si * TPS, si,
                             hook=(lambda t=t2: ln_fin(t)) if si + 2 < NS else None)
                    if si + 2 < NS:
                        ln_fin(t2 + 1)
                    f_middle(si * TPS + 1, si,
                             hook=(lambda s=si + 3: ln_pre_super(s))
                             if si + 3 < NS else None)
                    fla_prev_mid = s_mid.n
                    n_w = 13 if si < NS - 1 else len(wchunks) - wi
                    for _ in range(n_w):
                        if wi < len(wchunks):
                            f_wchunk(*wchunks[wi]); wi += 1
                s_mum = sem("mum")
                TE.wait_ge(s_wab.h, s_wab.n)
                TE.wait_ge(s_nrm.h, s_nrm.n)
                for mi in range(3):
                    s_mum.inc(TE.matmul(pcx[0:1, mi:mi + 1], wabs[:, mi:mi + 1],
                                        onesc[:], start=True, stop=True), 1)
                SE.wait_ge(s_mum.h, s_mum.n)
                SE.copy(murow[:], pcx[0:1, 0:3])
                SE.mul(murow[:], murow[:], 1.0 / (D * IFF))
                se_tok = SE._n
                VE.wait_ge(SE._sem, se_tok)
                VE.tensor_scalar_max(murow[:], murow[:], 1e-5)
                VE.tensor_scalar_mul(murow[:], murow[:], 0.5)
                mur_tok = VE._n
                TE.wait_ge(VE._sem, mur_tok)
                for mi in range(3):
                    s_mum.inc(TE.matmul(pcx[:, 4 + mi:5 + mi], ones1[:],
                                        murow[0:1, mi:mi + 1], start=True, stop=True), 1)
                SE.wait_ge(s_mum.h, s_mum.n)
                SE.copy(mu3[:], pcx[:, 4:7])
                se_tok2 = SE._n
                VE.wait_ge(SE._sem, se_tok2)
                VE.tensor_scalar_mul(thr_n[:], mu3[:], -1.0)
                GE.wait_ge(s_qsp.h, s_qsp.n)
                GE.wait_ge(s_asp.h, s_asp.n)
                SY.wait_ge(s_fla.h, s_fla.n)

            barrier("A")

            esB = ExitStack()
            with esB:
                sbb = lambda n, sh, dt: esB.enter_context(nc.sbuf_tensor(n, sh, dt))
                wgt = sbb("wgt", [128, 8 * IFF], BF16)
                wut = sbb("wut", [128, 8 * IFF], BF16)
                s_twl = sem("twl")
                s_ts1 = sem("ts1")
                s_ts2 = sem("ts2")
                s_ter = sem("ter")

                with ExitStack() as esT:
                    st = lambda n, sh, dt: esT.enter_context(nc.sbuf_tensor(n, sh, dt))
                    twf = [st(f"twf{i}", [128, 2048], F32) for i in range(2)]
                    ts1 = [st(f"ts1_{i}", [128, 2048], BF16) for i in range(2)]
                    ts2 = [st(f"ts2_{i}", [128, 2048], BF16) for i in range(2)]
                    tst_free = [0, 0]

                    def f_tern(mi, src_d, dst):
                        for ck in range(16):
                            bi = ck % 2
                            ldt = s_twl.inc(SE.dma_start(
                                out=twf[bi][:],
                                in_=src_d.ap()[(ck // 2) * 128:(ck // 2) * 128 + 128,
                                               (ck % 2) * 2048:(ck % 2) * 2048 + 2048]), 16)
                            SE.wait_ge(s_twl.h, ldt)
                            SE.wait_ge(s_ter.h, tst_free[bi])
                            SE.activation(ts1[bi][:], twf[bi][:], AF.Sign,
                                          bias=thr_n[:, mi:mi + 1], scale=1.0)
                            t2 = s_ts2.inc(SE.activation(
                                ts2[bi][:], twf[bi][:], AF.Sign,
                                bias=mu3[:, mi:mi + 1], scale=1.0), 1)
                            VE.wait_ge(s_ts2.h, t2)
                            tt = s_ter.inc(VE.tensor_add(
                                dst[:, (ck // 2) * IFF + (ck % 2) * 2048:
                                    (ck // 2) * IFF + (ck % 2) * 2048 + 2048],
                                ts1[bi][:], ts2[bi][:]), 1)
                            tst_free[bi] = tt

                    f_tern(0, wg_d, wgt)
                    f_tern(1, wu_d, wut)

                barrier("T1")

                qtl = [sbb(f"qtl{i}", [128, D], BF16) for i in range(2)]
                gst = [sbb("gst0", [128, 1024], F32)] * 2
                hps = [sbb(f"hps{i}", [128, IFF], F32) for i in range(2)]
                htf = sbb("htf", [128, 1024], F32)
                hi0 = sbb("hi0", [128, 1024], mybir.dt.int32)
                hs1 = sbb("hs1", [128, 1024], BF16)
                hs2 = sbb("hs2", [128, 1024], BF16)
                hcs = sbb("hcs", [128, 1024], BF16)
                h2b = [sbb(f"h2b{i}", [128, 1024], BF16) for i in range(2)]
                h2ts = [sbb(f"h2ts{i}", [128, 1024], BF16) for i in range(2)]
                pg = [esB.enter_context(nc.psum_tensor(f"pg{i}", [128, 1024], F32))
                      for i in range(2)]
                pu = [esB.enter_context(nc.psum_tensor(f"pu{i}", [128, 1024], F32))
                      for i in range(2)]

                s_qtl = sem("qtl")
                s_gmm = sem("gmm")
                s_umm = sem("umm")
                s_gev = sem("gev")
                s_rl2 = sem("rl2")
                s_hpw = sem("hpw")
                s_hmx = sem("hmx")
                s_h2a = sem("h2a")
                s_h2b = sem("h2b")
                s_h2t = sem("h2t")
                s_h2s = sem("h2s")

                VE.tensor_scalar(t_a[:, :], p_al[:, :], mu3[:, 0:1], None, ALU.mult)
                VE.tensor_scalar(t_b[:, :], p_al[:, :], mu3[:, 1:2], None, ALU.mult)
                VE.tensor_mul(t_a[:, :], t_a[:, :], t_a[:, :])
                VE.tensor_mul(p_ct[:, :], t_a[:, :], t_b[:, :])
                ctd_tok = VE._n

                qtl_free = [0, 0]
                qtl_toks = {}
                gst_free = [0, 0]
                pg_free = [0, 0]
                pu_free = [0, 0]
                hps_q_free = [[0] * 4, [0] * 4]
                h2b_free = [0, 0]
                h2t_free = [0, 0]
                shd_toks = {}
                fin_prev = [0]

                def emit_qtl_load(tj):
                    bj = tj % 2
                    SY.wait_ge(s_umm.h, qtl_free[bj])
                    qtl_toks[tj] = s_qtl.inc(SY.dma_start(
                        out=qtl[bj][:],
                        in_=bass.AP(qt_sp, tj * D * TAU,
                                    [[128, 128], [128 * 128, 8], [1, 128]])), 16)

                def chain_quarter(tj, hf):
                    b = hf % 2
                    pj = tj % 2
                    colj = slice(tj, tj + 1)
                    cs2 = slice(hf * 1024, (hf + 1) * 1024)
                    SE.wait_ge(s_hmx.h, shd_toks[tj])
                    SE.activation(htf[:], hps[pj][:, cs2], AF.Copy,
                                  bias=0.0, scale=p_sh[:, colj])
                    ext = SE._n
                    hps_q_free[pj][hf] = ext
                    VE.wait_ge(SE._sem, ext)
                    VE.tensor_copy(hi0[:], htf[:])
                    cast = VE._n
                    GE.wait_ge(VE._sem, cast)
                    GE.tensor_tensor(htf[:], htf[:], hi0[:], ALU.subtract)
                    sub = GE._n
                    SE.wait_ge(GE._sem, sub)
                    SE.activation(hs1[:], htf[:], AF.Sign,
                                  bias=chf[:, 0:1], scale=1.0)
                    SE.activation(hs2[:], htf[:], AF.Sign,
                                  bias=chf[:, 1:2], scale=1.0)
                    sig = SE._n
                    GE.wait_ge(SE._sem, sig)
                    if fin_prev[0]:
                        GE.wait_ge(VE._sem, fin_prev[0])
                    GE.tensor_add(hcs[:], hs1[:], hs2[:])
                    addt = GE._n
                    VE.wait_ge(GE._sem, addt)
                    VE.wait_ge(s_h2t.h, h2b_free[b])
                    VE.scalar_tensor_tensor(h2b[b][:], hi0[:], 2.0,
                                            hcs[:], ALU.mult, ALU.add)
                    fin_prev[0] = VE._n
                    SY.wait_ge(VE._sem, fin_prev[0])
                    SY.wait_ge(s_h2s.h, h2t_free[b])
                    h2t_tok = s_h2t.inc(SY.dma_start_transpose(
                        bass.AP(h2ts[b], 0, [[1024, 128], [128, 8], [1, 128]]),
                        bass.AP(h2b[b], 0, [[1024, 128], [128, 8], [1, 128]])), 16)
                    h2b_free[b] = h2t_tok
                    GE.wait_ge(s_h2t.h, h2t_tok)
                    h2t_free[b] = s_h2s.inc(GE.dma_start(
                        out=bass.AP(h2_sp, tj * IFF * TAU + hf * 8 * 128 * 128,
                                    [[128, 128], [128 * 128, 8], [1, 128]]),
                        in_=h2ts[b][:]), 16)

                emit_qtl_load(0)
                for ti in range(NT):
                    bi = ti % 2
                    pi = ti % 2
                    col = slice(ti, ti + 1)
                    if ti + 1 < NT:
                        emit_qtl_load(ti + 1)
                    umm_t = [0] * 4
                    for q in range(4):
                        qs = q % 2
                        TE.wait_ge(s_qtl.h, qtl_toks[ti])
                        TE.wait_ge(s_gev.h, pg_free[qs])
                        TE.wait_ge(s_hpw.h, pu_free[qs])
                        for f in range(8):
                            lhs = qtl[bi][:, f * 128:(f + 1) * 128]
                            for c2 in range(2):
                                mmg = TE.matmul(
                                    pg[qs][:, c2 * 512:(c2 + 1) * 512], lhs,
                                    wgt[:, f * IFF + q * 1024 + c2 * 512:
                                        f * IFF + q * 1024 + (c2 + 1) * 512],
                                    start=(f == 0), stop=(f == 7))
                            for c2 in range(2):
                                mmu = TE.matmul(
                                    pu[qs][:, c2 * 512:(c2 + 1) * 512], lhs,
                                    wut[:, f * IFF + q * 1024 + c2 * 512:
                                        f * IFF + q * 1024 + (c2 + 1) * 512],
                                    start=(f == 0), stop=(f == 7))
                        gmm_t = s_gmm.inc(mmg, 1)
                        umm_t[q] = s_umm.inc(mmu, 1)
                        SE.wait_ge(s_gmm.h, gmm_t)
                        SE.wait_ge(s_hpw.h, gst_free[0])
                        gev_tok = s_gev.inc(
                            SE.copy(gst[bi][:], pg[qs][:]), 1)
                        pg_free[qs] = gev_tok
                        VE.wait_ge(s_gev.h, gev_tok)
                        s_rl2.inc(VE.scalar_tensor_tensor(
                            gst[bi][:], gst[bi][:], 0.0, gst[bi][:],
                            ALU.max, ALU.mult), 1)
                        if ti >= 2:
                            chain_quarter(ti - 2, q)
                        VE.wait_ge(s_umm.h, umm_t[q])
                        VE.wait_ge(SE._sem, hps_q_free[pi][q])
                        hpw_tok = s_hpw.inc(VE.tensor_mul(
                            hps[pi][:, q * 1024:(q + 1) * 1024],
                            gst[bi][:], pu[qs][:]), 1)
                        gst_free[0] = hpw_tok
                        pu_free[qs] = hpw_tok
                        s_hmx.inc(VE.reduce_max(
                            hx4[:, q:q + 1], hps[pi][:, q * 1024:(q + 1) * 1024],
                            AX.X, apply_absolute_value=True), 1)
                    VE.wait_ge(VE._sem, ctd_tok)
                    VE.reduce_max(hx4[:, 4:5], hx4[:, 0:4], AX.X)
                    VE.tensor_mul(hx4[:, 5:6], p_ct[:, col], hx4[:, 4:5])
                    VE.tensor_scalar_max(hx4[:, 5:6], hx4[:, 5:6], 1e-5)
                    VE.reciprocal(hx4[:, 6:7], hx4[:, 5:6])
                    VE.tensor_mul(hx4[:, 6:7], hx4[:, 6:7], p_ct[:, col])
                    VE.tensor_scalar_mul(p_sh[:, col], hx4[:, 6:7], 127.0)
                    shd_toks[ti] = s_hmx.inc(VE.tensor_scalar_mul(
                        p_dd[:, col], hx4[:, 5:6], 1.0 / 254.0), 1)
                    qtl_free[bi] = umm_t[3]
                for tj in range(max(0, NT - 2), NT):
                    for hf in range(4):
                        chain_quarter(tj, hf)
                GE.wait_ge(s_h2s.h, s_h2s.n)

            barrier("B1")

            esC = ExitStack()
            with esC:
                sc = lambda n, sh, dt: esC.enter_context(nc.sbuf_tensor(n, sh, dt))
                wdt = sc("wdt", [128, 32 * 1024], BF16)
                s_twl = sem("twl")
                s_ts2 = sem("ts2")
                s_ter = sem("ter")
                with ExitStack() as esT:
                    st = lambda n, sh, dt: esT.enter_context(nc.sbuf_tensor(n, sh, dt))
                    twf = [st(f"dwf{i}", [128, 1024], F32) for i in range(2)]
                    ts1 = [st(f"ds1_{i}", [128, 1024], BF16) for i in range(2)]
                    ts2 = [st(f"ds2_{i}", [128, 1024], BF16) for i in range(2)]
                    tst_free = [s_ter.n, s_ter.n]
                    for ck in range(32):
                        bi = ck % 2
                        ldt = s_twl.inc(SE.dma_start(
                            out=twf[bi][:],
                            in_=wd_d.ap()[ck * 128:(ck + 1) * 128, :]), 16)
                        SE.wait_ge(s_twl.h, ldt)
                        SE.wait_ge(s_ter.h, tst_free[bi])
                        SE.activation(ts1[bi][:], twf[bi][:], AF.Sign,
                                      bias=thr_n[:, 2:3], scale=1.0)
                        t2 = s_ts2.inc(SE.activation(
                            ts2[bi][:], twf[bi][:], AF.Sign,
                            bias=mu3[:, 2:3], scale=1.0), 1)
                        VE.wait_ge(s_ts2.h, t2)
                        tt = s_ter.inc(VE.tensor_add(
                            wdt[:, ck * 1024:(ck + 1) * 1024],
                            ts1[bi][:], ts2[bi][:]), 1)
                        tst_free[bi] = tt

                barrier("T2")

                h2l = [sc(f"h2l{i}", [128, IFF], BF16) for i in range(2)]
                x2 = [sc(f"x2_{i}", [128, D], F32) for i in range(2)]
                at2 = [sc(f"at2_{i}", [128, D], F32) for i in range(2)]
                ost = [sc(f"ost{i}", [128, D], F32) for i in range(2)]
                o1b = sc("o1b", [128, D], F32)
                pd = [esC.enter_context(nc.psum_tensor(f"pd{i}", [128, 1024], F32))
                      for i in range(2)]

                s_h2l = sem("h2l")
                s_x2l = sem("x2l")
                s_a2l = sem("a2l")
                s_dmm = sem("dmm")
                s_och = sem("och")
                s_osp = sem("osp")

                VE.tensor_scalar(p_dd[:, :], p_dd[:, :], mu3[:, 2:3], None, ALU.mult)
                ddf_tok = VE._n

                h2l_free = [0, 0]
                x2_free = [0, 0]
                at2_free = [0, 0]
                ost_free = [0, 0]
                pd_free = [0, 0]

                for ti in range(NT):
                    bi = ti % 2
                    col = slice(ti, ti + 1)
                    SE.wait_ge(s_dmm.h, h2l_free[bi])
                    h2tok = s_h2l.inc(SE.dma_start(
                        out=h2l[bi][:],
                        in_=bass.AP(h2_sp, ti * IFF * TAU,
                                    [[128, 128], [128 * 128, 32], [1, 128]])), 16)
                    SE.wait_ge(s_och.h, x2_free[bi])
                    x2tok = s_x2l.inc(SE.dma_start(
                        out=x2[bi][:], in_=x_in.ap()[ti * TAU:(ti + 1) * TAU, :]), 16)
                    SE.wait_ge(s_och.h, at2_free[bi])
                    a2tok = s_a2l.inc(SE.dma_start(
                        out=at2[bi][:], in_=at_sp.ap()[ti * TAU:(ti + 1) * TAU, :]), 16)
                    qs = ti % 2
                    TE.wait_ge(s_h2l.h, h2tok)
                    TE.wait_ge(s_och.h, pd_free[qs])
                    for f2 in range(32):
                        for c2 in range(2):
                            mm = TE.matmul(
                                pd[qs][:, c2 * 512:(c2 + 1) * 512],
                                h2l[bi][:, f2 * 128:(f2 + 1) * 128],
                                wdt[:, f2 * 1024 + c2 * 512:f2 * 1024 + (c2 + 1) * 512],
                                start=(f2 == 0), stop=(f2 == 31))
                    dmm_tok = s_dmm.inc(mm, 1)
                    h2l_free[bi] = dmm_tok
                    VE.wait_ge(s_dmm.h, dmm_tok)
                    VE.wait_ge(s_x2l.h, x2tok)
                    VE.wait_ge(s_a2l.h, a2tok)
                    VE.wait_ge(VE._sem, ddf_tok)
                    VE.wait_ge(s_osp.h, ost_free[bi])
                    VE.scalar_tensor_tensor(o1b[:], pd[qs][:], p_dd[:, col],
                                            x2[bi][:], ALU.mult, ALU.add)
                    VE.tensor_add(o1b[:], o1b[:], at2[bi][:])
                    och_tok = s_och.inc(VE.tensor_add(ost[bi][:], o1b[:], bbc[:]), 1)
                    pd_free[qs] = och_tok
                    x2_free[bi] = och_tok
                    at2_free[bi] = och_tok
                    GE.wait_ge(s_och.h, och_tok)
                    ost_free[bi] = s_osp.inc(GE.dma_start(
                        out=out_d.ap()[ti * TAU:(ti + 1) * TAU, :],
                        in_=ost[bi][:]), 16)
                if DBG:
                    s_dbg = sem("dbg")
                    for i, t in enumerate((p_m, p_rs, p_sg, p_sq, p_al, p_ct,
                                           p_sh, p_dd, mu3, thr_n, wabs)):
                        w = min(t.shape[1], 64)
                        s_dbg.inc(GE.dma_start(
                            out=dbg_p.ap()[:, i * 64:i * 64 + w], in_=t[:, 0:w]), 16)
                    GE.wait_ge(s_dbg.h, s_dbg.n)
                GE.wait_ge(s_osp.h, s_osp.n)
                SE.wait_ge(s_osp.h, s_osp.n)

    return nc


def make_host_layouts(W_qkv, b_qkv):
    perm = np.zeros(3 * D, dtype=np.int64)
    for r in range(3):
        for j in range(8):
            for hh in range(2):
                h = 2 * j + hh
                cols = 192 * h + 64 * r + np.arange(64)
                m = r * 8 + j
                perm[m * 128 + hh * 64: m * 128 + hh * 64 + 64] = cols
    Wp = np.ascontiguousarray(np.asarray(W_qkv)[:, perm])
    bp = np.asarray(b_qkv)[perm]
    btile = np.ascontiguousarray(bp.reshape(24, 128).T).astype(np.float32)
    return Wp, btile


def make_masks():
    mq = np.zeros((9, 16 * SUP), dtype=BF)
    mq[0, :] = 32.0
    cols = np.arange(16 * SUP)
    for tp in range(8):
        mq[1 + tp, (cols % 8) == tp] = 32.0
    mk = np.zeros((9, 16 * TAU), dtype=BF)
    mk[0, :] = -32.0
    for tp in range(8):
        mk[1 + tp, ((cols[:16 * TAU] % 128) // 16) == tp] = 32.0
    return mq, mk


def kernel(L0_attn_x, L0_attn_W_qkv, L0_attn_b_qkv, L0_attn_W_out,
           L0_attn_b_out, L0_ffn_w_gate, L0_ffn_w_up, L0_ffn_w_down):
    x = np.ascontiguousarray(np.asarray(L0_attn_x, dtype=np.float32))
    B, S, _ = x.shape
    Wp, btile = make_host_layouts(L0_attn_W_qkv, L0_attn_b_qkv)
    mq, mk = make_masks()
    shared = {
        "wqkv": Wp.astype(np.float32),
        "btile": btile,
        "wout": np.ascontiguousarray(np.asarray(L0_attn_W_out, dtype=np.float32)),
        "bout": np.asarray(L0_attn_b_out, dtype=np.float32).reshape(1, D),
        "wg_t": np.ascontiguousarray(np.asarray(L0_ffn_w_gate, dtype=np.float32).T),
        "wu_t": np.ascontiguousarray(np.asarray(L0_ffn_w_up, dtype=np.float32).T),
        "wd_t": np.ascontiguousarray(np.asarray(L0_ffn_w_down, dtype=np.float32).T),
        "ident": np.eye(128, dtype=BF),
        "maskq": mq,
        "maskk": mk,
    }
    nc = build_program(S)
    in_maps = [dict(shared, x=np.ascontiguousarray(x[b])) for b in range(B)]
    res = run_bass_kernel_spmd(nc, in_maps, core_ids=list(range(B)))
    out = np.stack([np.asarray(res.results[b]["out"]) for b in range(B)], axis=0)
    return out.astype(np.float32)


# version 14
# speedup vs baseline: 1.3540x; 1.0648x over previous
import numpy as np
import ml_dtypes
from contextlib import ExitStack
import concourse.bass as bass
import concourse.mybir as mybir
from concourse.bass_utils import run_bass_kernel_spmd

F32 = mybir.dt.float32
BF16 = mybir.dt.bfloat16
F16 = mybir.dt.float16
I16 = mybir.dt.int16
AF = mybir.ActivationFunctionType
ALU = mybir.AluOpType
AX = mybir.AxisListType
BF = ml_dtypes.bfloat16

D = 1024
NH = 16
IFF = 4096
TAU = 128
SUP = 256
TPS = SUP // TAU
MQ = 1536.0
N_CORES = 8
DBG = False


class Sem:
    __slots__ = ("h", "n")

    def __init__(self, es, nc, name):
        self.h = es.enter_context(nc.semaphore(name))
        self.n = 0

    def inc(self, instr, k=1):
        instr.then_inc(self.h, k)
        self.n += k
        return self.n


_WRAP_OPS = {
    "tensor_copy", "tensor_tensor", "tensor_add", "tensor_sub", "tensor_mul",
    "tensor_max", "tensor_scalar", "tensor_single_scalar", "tensor_scalar_add",
    "tensor_scalar_sub", "tensor_scalar_mul", "tensor_scalar_max",
    "tensor_scalar_min", "tensor_relu", "tensor_reduce", "reduce_sum",
    "reduce_max", "scalar_tensor_tensor", "reciprocal", "memset", "transpose",
    "activation", "square", "sqrt", "copy", "mul", "add", "sign", "select",
    "bn_stats", "bn_aggr", "engine_nop", "iota",
}


class EngSem:
    __slots__ = ("e",)

    def __init__(self, e):
        self.e = e

    @property
    def h(self):
        return self.e._sem

    @property
    def n(self):
        return self.e._n

    def inc(self, instr, k=1):
        return self.e._n


class SerialEng:

    def __init__(self, eng, nc, es, name):
        self._eng = eng
        self._sem = es.enter_context(nc.semaphore("ser_" + name))
        self._n = 0
        self._last = {}

    def _tnames(self, v, out):
        if isinstance(v, bass.AP):
            out.append(v.tensor.name)

    def __getattr__(self, name):
        fn = getattr(self._eng, name)
        if name not in _WRAP_OPS:
            return fn

        def wrapped(*a, **k):
            names = []
            for v in a:
                self._tnames(v, names)
            for v in k.values():
                self._tnames(v, names)
            need = 0
            for t in names:
                need = max(need, self._last.get(t, 0))
            if need:
                self._eng.wait_ge(self._sem, need)
            instr = fn(*a, **k)
            instr.then_inc(self._sem, 1)
            self._n += 1
            wtoks = []
            for v in list(a[:1]) + [k[x] for x in ("accum_out", "out") if x in k]:
                if isinstance(v, bass.AP):
                    self._last[v.tensor.name] = self._n
            if a and isinstance(a[0], bass.AP):
                self._last[a[0].tensor.name] = self._n
            return instr

        return wrapped


def build_program(S):
    NT = S // TAU
    NS = S // SUP
    nc = bass.Bass("TRN2", target_bir_lowering=False,
                   detect_race_conditions=False)

    x_in = nc.dram_tensor("x", [S, D], F32, kind="ExternalInput")
    wqkv_d = nc.dram_tensor("wqkv", [D, 3 * D], F32, kind="ExternalInput")
    btile_d = nc.dram_tensor("btile", [128, 24], F32, kind="ExternalInput")
    wout_d = nc.dram_tensor("wout", [D, D], F32, kind="ExternalInput")
    bout_d = nc.dram_tensor("bout", [1, D], F32, kind="ExternalInput")
    wg_d = nc.dram_tensor("wg_t", [D, IFF], F32, kind="ExternalInput")
    wu_d = nc.dram_tensor("wu_t", [D, IFF], F32, kind="ExternalInput")
    wd_d = nc.dram_tensor("wd_t", [IFF, D], F32, kind="ExternalInput")
    id_d = nc.dram_tensor("ident", [128, 128], BF16, kind="ExternalInput")
    mq_d = nc.dram_tensor("maskq", [9, 16 * SUP], BF16, kind="ExternalInput")
    mk_d = nc.dram_tensor("maskk", [9, 16 * TAU], BF16, kind="ExternalInput")
    out_d = nc.dram_tensor("out", [S, D], F32, kind="ExternalOutput")
    kind_dbg = {"kind": "ExternalOutput"} if DBG else {}
    qt_sp = nc.dram_tensor("qt_sp", [NT * D * TAU], BF16, **kind_dbg)
    at_sp = nc.dram_tensor("at_sp", [S, D], F32, **kind_dbg)
    h2_sp = nc.dram_tensor("h2_sp", [NT * IFF * TAU], BF16, **kind_dbg)
    if DBG:
        dbg_p = nc.dram_tensor("dbg_p", [128, 1024], F32, kind="ExternalOutput")

    es = ExitStack()
    with es:
        TE, SY = nc.tensor, nc.sync
        VE = SerialEng(nc.vector, nc, es, "ve")
        SE = SerialEng(nc.scalar, nc, es, "se")
        GE = SerialEng(nc.gpsimd, nc, es, "ge")
        sb = lambda n, sh, dt: es.enter_context(nc.sbuf_tensor(n, sh, dt))
        sid = sb("sid", [128, 128], BF16)
        p_m = sb("p_m", [128, NT], F32)
        p_rs = sb("p_rs", [128, NT], F32)
        p_sg = sb("p_sg", [128, NT], F32)
        p_sq = sb("p_sq", [128, NT], F32)
        p_qb = sb("p_qb", [128, NT], F32)
        p_nm = sb("p_nm", [128, NT], F32)
        p_al = sb("p_al", [128, NT], F32)
        p_ct = sb("p_ct", [128, NT], F32)
        p_sh = sb("p_sh", [128, NT], F32)
        p_dd = sb("p_dd", [128, NT], F32)
        t_a = sb("t_a", [128, NT], F32)
        t_b = sb("t_b", [128, NT], F32)
        p_mx = sb("p_mx", [128, NT], F32)
        p_mn = sb("p_mn", [128, NT], F32)
        bbc = sb("bbc", [128, D], F32)
        brow = sb("brow", [1, D], F32)
        ones1 = sb("ones1", [1, 128], F32)
        onesc = sb("onesc", [128, 1], F32)
        bcol = sb("bcol", [128, 24], F32)
        wabs = sb("wabs", [128, 3], F32)
        mu3 = sb("mu3", [128, 3], F32)
        thr_n = sb("thr_n", [128, 3], F32)
        murow = sb("murow", [1, 3], F32)
        hx4 = sb("hx4", [128, 8], F32)
        rcp4 = sb("rcp4", [128, 4], F32)
        chf = sb("chf", [128, 2], F32)

        SEMS = {}
        _VE_SEMS = {"lnv", "qbb", "kst", "vev", "rcp", "cev", "mid", "wab",
                    "ter", "rl2", "hpw", "hmx", "h2b", "och", "ini"}
        _SE_SEMS = {"sqd", "sig", "xnb", "fme", "exp", "nrm", "aev", "wap",
                    "ts2", "gev", "h2a", "bbc"}
        _GE_SEMS = {"vpr"}

        def sem(name):
            if name not in SEMS:
                if name in _VE_SEMS:
                    SEMS[name] = EngSem(VE)
                elif name in _SE_SEMS:
                    SEMS[name] = EngSem(SE)
                elif name in _GE_SEMS:
                    SEMS[name] = EngSem(GE)
                else:
                    SEMS[name] = Sem(es, nc, name)
            return SEMS[name]

        block = es.enter_context(nc.Block())

        def barrier(tag):
            b = sem("bar_" + tag)
            for eng in (TE, VE, SE, GE, SY):
                b.inc(eng.nop(), 1)
            for eng in (TE, VE, SE, GE, SY):
                eng.wait_ge(b.h, b.n)

        @block.sync
        def _(_q):
            esA = ExitStack()
            with esA:
                sa = lambda n, sh, dt: esA.enter_context(nc.sbuf_tensor(n, sh, dt))
                wq = sa("wq", [128, 8 * 3072], BF16)
                wo = sa("wo", [64, 16 * 1024], BF16)
                xr = [sa(f"xr{i}", [128, D], F32) for i in range(2)]
                xsq = sa("xsq", [128, D], F32)
                xnst = [sa(f"xnst{i}", [128, D], BF16) for i in range(2)]
                qtf = sa("qtf", [128, D], F32)
                qi0 = sa("qi0", [128, D], mybir.dt.int32)
                qd = xsq
                qs1 = sa("qs1", [128, D], BF16)
                qs2 = sa("qs2", [128, D], BF16)
                qcs = sa("qcs", [128, D], BF16)
                qbst = [sa(f"qbst{i}", [128, D], BF16) for i in range(2)]
                qts = [sa(f"qts{i}", [128, D], BF16) for i in range(2)]
                xnt = sa("xnt", [128, 4 * D], BF16)
                fmst = sa("fmst", [128, 8 * SUP], BF16)
                qfl = sa("qfl", [128, 16 * SUP], BF16)
                kfl = sa("kfl", [128, 16 * SUP], BF16)
                vfl = sa("vfl", [128, 16 * SUP], BF16)
                kstk = [sa("kstk0", [128, 16 * TAU], BF16)] * 2
                vpre = [sa(f"vpre{i}", [128, 16 * TAU], BF16) for i in range(2)]
                est = [sa("est0", [128, 16 * TAU], BF16)] * 2
                vstk = [sa("vstk0", [128, 16 * 65], BF16)] * 2
                ctxn = [sa(f"ctxn{i}", [128, 16 * 64], BF16) for i in range(2)]
                ctxt = [sa(f"ctxt{i}", [128, 16 * 128], BF16) for i in range(2)]
                atst = [sa("atst0", [128, D], F32)] * 2
                wfs_t = [sa(f"wfs{i}", [128, 512], F32) for i in range(2)]
                wfs = [t[:] for t in wfs_t]
                wneg = sa("wneg", [128, 512], F32)
                wacc = sa("wacc", [128, 512], F32)

                pq = esA.enter_context(nc.psum_tensor("pq", [128, 512], F32))
                pwo = esA.enter_context(nc.psum_tensor("pwo", [128, 1024], F32))
                ps_s = esA.enter_context(nc.psum_tensor("ps_s", [128, 1024], F32))
                pcx = esA.enter_context(nc.psum_tensor("pcx", [128, 512], F32))
                ptr = esA.enter_context(nc.psum_tensor("ptr", [128, 1024], BF16))
                ptc = esA.enter_context(nc.psum_tensor("ptc", [128, 1024], BF16))

                s_w0 = sem("w0")
                s_cst = sem("cst")
                s_msk = sem("msk")
                s_ini = sem("ini")
                s_bbc = sem("bbc")
                s_w0.inc(SY.dma_start(out=sid[:], in_=id_d.ap()), 16)
                s_w0.inc(SY.dma_start(out=brow[:], in_=bout_d.ap()), 16)
                s_w0.inc(SY.dma_start(out=bcol[:], in_=btile_d.ap()), 16)
                s_msk.inc(SY.dma_start(out=qfl[64:73, :], in_=mq_d.ap()), 16)
                s_msk.inc(SY.dma_start(out=kstk[0][64:73, :], in_=mk_d.ap()), 16)
                for c in range(8):
                    s_cst.inc(GE.dma_start(
                        out=wq[:, c * 3072:(c + 1) * 3072],
                        in_=wqkv_d.ap()[c * 128:(c + 1) * 128, :]), 16)
                for h in range(16):
                    s_cst.inc(GE.dma_start(
                        out=wo[:, h * 1024:(h + 1) * 1024],
                        in_=wout_d.ap()[h * 64:(h + 1) * 64, :]), 16)
                VE.memset(ones1[:], 1.0)
                VE.memset(onesc[:], 1.0)
                VE.memset(wabs[:], 0.0)
                VE.memset(chf[:, 0:1], -0.5)
                VE.memset(chf[:, 1:2], 0.5)
                VE.memset(bass.AP(vstk[0], 64, [[16 * 65, 128], [65, 16], [1, 1]]), 1.0)
                s_ini.inc(VE.engine_nop(), 1)
                TE.wait_ge(s_w0.h, s_w0.n)
                TE.wait_ge(s_ini.h, s_ini.n)
                s_mmb = sem("mmb")
                TE.matmul(pq[:], ones1[:], brow[0:1, 0:512], start=True, stop=True)
                s_mmb.inc(TE.matmul(ps_s[:, 0:512], ones1[:], brow[0:1, 512:1024],
                                    start=True, stop=True), 1)
                SE.wait_ge(s_mmb.h, s_mmb.n)
                SE.copy(bbc[:, 0:512], pq[:])
                s_bbc.inc(SE.copy(bbc[:, 512:1024], ps_s[:, 0:512]), 1)
                qkv_bank_free = 0
                if DBG:
                    barrier("pre")

                s_xld = sem("xld")
                s_lnv = sem("lnv")
                s_sqd = sem("sqd")
                s_sig = sem("sig")
                s_xnb = sem("xnb")
                s_qbb = sem("qbb")
                s_xnt = sem("xnt")
                s_qtt = sem("qtt")
                s_qsp = sem("qsp")
                s_qkv = sem("qkv")
                s_fme = sem("fme")
                s_fla = sem("fla")
                s_kst = sem("kst")
                s_vpr = sem("vpr")
                s_smm = sem("smm")
                s_exp = sem("exp")
                s_vtr = sem("vtr")
                s_vev = sem("vev")
                s_cmm = sem("cmm")
                s_rcp = sem("rcp")
                s_nrm = sem("nrm")
                s_ctr = sem("ctr")
                s_cev = sem("cev")
                s_wom = sem("wom")
                s_aev = sem("aev")
                s_asp = sem("asp")
                s_mid = sem("mid")
                s_wfl = sem("wfl")
                s_wab = sem("wab")
                s_wap = sem("wap")

                xr_free = [0, 0]
                xn_free = [0, 0]
                qb_free = [0, 0]
                qt_free = [0, 0]
                xnt_free = [0, 0, 0, 0]
                xsq_free = 0
                kst_free = [0, 0]
                vpr_free = [0, 0]
                est_free = [0, 0]
                vstk_free = [0, 0]
                ctxn_free = [0, 0]
                ctxt_free = [0, 0]
                atst_free = [0, 0]
                wfs_free = [0, 0]
                fla_prev_mid = 0
                fla_prev_smm = 0
                fla_prev_kst = 0
                fla_prev_vpr = 0
                fme_tok = [0] * 24
                q_sig_prev = [0]
                q_cs_prev = [0]
                q_fin_prev = [0]
                wab_se_prev = [0]
                exp_prev2 = [0, 0]
                vev_prev = 0
                cev_prev = 0
                nrm_prev = 0
                aev_prev = 0

                def f_ln_pre(ti):
                    nonlocal xsq_free
                    bi = ti % 2
                    col = slice(ti, ti + 1)
                    SE.wait_ge(s_lnv.h, xr_free[bi])
                    xtok = s_xld.inc(SE.dma_start(
                        out=xr[bi][:], in_=x_in.ap()[ti * TAU:(ti + 1) * TAU, :]), 16)
                    VE.wait_ge(s_xld.h, xtok)
                    VE.reduce_sum(t_a[:, col], xr[bi][:], AX.X)
                    VE.reduce_max(p_mx[:, col], xr[bi][:], AX.X)
                    VE.tensor_reduce(p_mn[:, col], xr[bi][:], AX.X, ALU.min)
                    SE.wait_ge(s_xld.h, xtok)
                    SE.wait_ge(s_lnv.h, xsq_free)
                    sq_tok = s_sqd.inc(SE.square(xsq[:], xr[bi][:]), 1)
                    VE.wait_ge(s_sqd.h, sq_tok)
                    VE.reduce_sum(t_b[:, col], xsq[:], AX.X)
                    xsq_free = s_lnv.inc(VE.engine_nop(), 1)
                    return xtok

                def f_ln_smalls(cols):
                    c = cols
                    VE.tensor_scalar_mul(p_m[:, c], t_a[:, c], 1.0 / D)
                    VE.tensor_scalar_mul(t_b[:, c], t_b[:, c], 1.0 / D)
                    VE.tensor_mul(t_a[:, c], p_m[:, c], p_m[:, c])
                    VE.tensor_sub(t_b[:, c], t_b[:, c], t_a[:, c])
                    var_tok = s_lnv.inc(
                        VE.tensor_scalar_add(t_b[:, c], t_b[:, c], 1e-5), 1)
                    SE.wait_ge(s_lnv.h, var_tok)
                    sig_tok = s_sig.inc(SE.sqrt(p_sg[:, c], t_b[:, c]), 1)
                    VE.wait_ge(s_sig.h, sig_tok)
                    VE.reciprocal(p_rs[:, c], p_sg[:, c])
                    VE.tensor_sub(p_mx[:, c], p_mx[:, c], p_m[:, c])
                    VE.tensor_sub(p_mn[:, c], p_m[:, c], p_mn[:, c])
                    VE.tensor_max(p_mx[:, c], p_mx[:, c], p_mn[:, c])
                    VE.tensor_scalar_mul(p_mn[:, c], p_sg[:, c], 1e-5)
                    VE.tensor_max(p_mx[:, c], p_mx[:, c], p_mn[:, c])
                    VE.reciprocal(p_sq[:, c], p_mx[:, c])
                    VE.tensor_scalar_mul(p_sq[:, c], p_sq[:, c], 127.0)
                    VE.tensor_mul(p_al[:, c], p_mx[:, c], p_rs[:, c])
                    VE.tensor_scalar_mul(p_al[:, c], p_al[:, c], 1.0 / 254.0)
                    VE.tensor_mul(p_nm[:, c], p_m[:, c], p_rs[:, c])
                    VE.tensor_scalar_mul(p_nm[:, c], p_nm[:, c], -1.0)
                    VE.tensor_mul(p_qb[:, c], p_m[:, c], p_sq[:, c])
                    sml_tok = s_lnv.inc(VE.tensor_scalar_mul(
                        p_qb[:, c], p_qb[:, c], -1.0), 1)
                    return sml_tok

                def f_ln(ti, sml_tok):
                    bi = ti % 2
                    col = slice(ti, ti + 1)
                    SE.wait_ge(s_lnv.h, sml_tok)
                    SE.wait_ge(s_xnt.h, xn_free[bi])
                    xnb_tok = s_xnb.inc(SE.activation(
                        xnst[bi][:], xr[bi][:], AF.Identity,
                        bias=p_nm[:, col], scale=p_rs[:, col]), 1)
                    VE.tensor_scalar(qtf[:], xr[bi][:],
                                     p_sq[:, col], p_qb[:, col], ALU.mult, ALU.add)
                    VE.tensor_copy(qi0[:], qtf[:])
                    ve_q = VE._n
                    GE.wait_ge(VE._sem, ve_q)
                    if q_sig_prev[0]:
                        GE.wait_ge(SE._sem, q_sig_prev[0])
                    GE.tensor_tensor(qd[:], qtf[:], qi0[:], ALU.subtract)
                    ge_d = GE._n
                    SE.wait_ge(GE._sem, ge_d)
                    if q_cs_prev[0]:
                        SE.wait_ge(GE._sem, q_cs_prev[0])
                    SE.activation(qs1[:], qd[:], AF.Sign, bias=chf[:, 0:1], scale=1.0)
                    SE.activation(qs2[:], qd[:], AF.Sign, bias=chf[:, 1:2], scale=1.0)
                    q_sig_prev[0] = SE._n
                    GE.wait_ge(SE._sem, q_sig_prev[0])
                    if q_fin_prev[0]:
                        GE.wait_ge(VE._sem, q_fin_prev[0])
                    GE.tensor_add(qcs[:], qs1[:], qs2[:])
                    q_cs_prev[0] = GE._n
                    VE.wait_ge(GE._sem, q_cs_prev[0])
                    VE.wait_ge(s_qtt.h, qb_free[bi])
                    qbb_tok = s_qbb.inc(VE.scalar_tensor_tensor(
                        qbst[bi][:], qi0[:], 2.0, qcs[:], ALU.mult, ALU.add), 1)
                    q_fin_prev[0] = qbb_tok
                    xr_free[bi] = s_lnv.inc(VE.engine_nop(), 1)
                    slot = ti % 4
                    SY.wait_ge(s_xnb.h, xnb_tok)
                    SY.wait_ge(s_qkv.h, xnt_free[slot])
                    xnt_tok = s_xnt.inc(SY.dma_start_transpose(
                        bass.AP(xnt, slot * D, [[4 * D, 128], [128, 8], [1, 128]]),
                        bass.AP(xnst[bi], 0, [[D, 128], [128, 8], [1, 128]])), 16)
                    xn_free[bi] = xnt_tok
                    SY.wait_ge(s_qbb.h, qbb_tok)
                    SY.wait_ge(s_qsp.h, qt_free[bi])
                    qtt_tok = s_qtt.inc(SY.dma_start_transpose(
                        bass.AP(qts[bi], 0, [[D, 128], [128, 8], [1, 128]]),
                        bass.AP(qbst[bi], 0, [[D, 128], [128, 8], [1, 128]])), 16)
                    qb_free[bi] = qtt_tok
                    GE.wait_ge(s_qtt.h, qtt_tok)
                    qsp_tok = s_qsp.inc(GE.dma_start(
                        out=bass.AP(qt_sp, ti * D * TAU,
                                    [[128, 128], [128 * 128, 8], [1, 128]]),
                        in_=qts[bi][:]), 16)
                    qt_free[bi] = qsp_tok
                    return xnt_tok

                def f_qkv(si, xnt_toks):
                    nonlocal fla_prev_mid, qkv_bank_free
                    TE.wait_ge(s_cst.h, s_cst.n)
                    TE.wait_ge(s_msk.h, s_msk.n)
                    TE.wait_ge(s_bbc.h, s_bbc.n)
                    TE.wait_ge(s_xnt.h, max(xnt_toks))
                    if si > 0:
                        TE.wait_ge(s_fla.h, s_fla.n)
                    sl0 = (si * TPS) % 4
                    for r, dst in ((0, qfl), (1, kfl), (2, vfl)):
                        fla_reg = s_fla.n
                        for j in range(8):
                            m = r * 8 + j
                            TE.wait_ge(s_fme.h, qkv_bank_free)
                            for f in range(8):
                                rhs = bass.AP(xnt, sl0 * D + f * 128,
                                              [[4 * D, 128], [D, TPS], [1, 128]])
                                mm = TE.matmul(
                                    pq[:, 0:SUP],
                                    wq[:, f * 3072 + m * 128:f * 3072 + (m + 1) * 128],
                                    rhs, start=(f == 0), stop=(f == 7))
                            qkv_tok = s_qkv.inc(mm, 1)
                            SE.wait_ge(s_qkv.h, qkv_tok)
                            if r > 0 or si > 0:
                                SE.wait_ge(s_fla.h, fla_reg)
                            fme_tok[m] = s_fme.inc(SE.activation(
                                fmst[:, j * SUP:(j + 1) * SUP],
                                pq[:, 0:SUP],
                                AF.Identity, bias=bcol[:, m:m + 1], scale=1.0), 1)
                            qkv_bank_free = fme_tok[m]
                        SY.wait_ge(s_fme.h, fme_tok[r * 8 + 7])
                        if r == 0:
                            SY.wait_ge(s_mid.h, fla_prev_mid)
                            SY.wait_ge(s_smm.h, fla_prev_smm)
                            SY.wait_ge(s_kst.h, fla_prev_kst)
                            SY.wait_ge(s_vpr.h, fla_prev_vpr)
                        s_fla.inc(SY.dma_start(
                            out=bass.AP(dst, 0,
                                        [[16 * SUP, 64], [2 * SUP, 8], [1, SUP]]),
                            in_=bass.AP(fmst, 0,
                                        [[8 * SUP, 64], [SUP, 8], [1, SUP]])), 16)
                        s_fla.inc(SY.dma_start(
                            out=bass.AP(dst, SUP,
                                        [[16 * SUP, 64], [2 * SUP, 8], [1, SUP]]),
                            in_=bass.AP(fmst, 64 * 8 * SUP,
                                        [[8 * SUP, 64], [SUP, 8], [1, SUP]])), 16)
                    for tl in range(TPS):
                        xnt_free[(si * TPS + tl) % 4] = s_qkv.n

                def f_middle(ti, si, hook=None):
                    nonlocal vev_prev, cev_prev, nrm_prev, aev_prev
                    nonlocal fla_prev_smm, fla_prev_kst, fla_prev_vpr
                    bi = ti % 2
                    tl = ti - si * TPS
                    fla_tok = s_fla.n
                    VE.wait_ge(s_fla.h, fla_tok)
                    VE.wait_ge(s_smm.h, kst_free[bi])
                    kst_tok = s_kst.inc(VE.tensor_copy(
                        kstk[bi][0:64, :],
                        bass.AP(kfl, tl * TAU,
                                [[16 * SUP, 64], [8, 16], [1, 8], [SUP, 16]])), 1)
                    GE.wait_ge(s_fla.h, fla_tok)
                    GE.wait_ge(s_vtr.h, vpr_free[bi])
                    vpr_tok = s_vpr.inc(GE.tensor_copy(
                        vpre[bi][0:64, :],
                        bass.AP(vfl, tl * TAU,
                                [[16 * SUP, 64], [8, 16], [1, 8], [SUP, 16]])), 1)
                    smm_t = [0] * 16
                    exp_t = [0] * 4
                    vtr_t = [0] * 16
                    cmm_t = [0] * 16
                    ctr_t = [0] * 16
                    nrm_t = [0] * 4
                    vev_t = [0, 0]
                    cev_t = [0, 0]
                    for half in range(2):
                        for b4 in (2 * half, 2 * half + 1):
                            qb_ = (b4 % 2) * 512
                            for g in range(4):
                                gi = b4 * 4 + g
                                if g == 0:
                                    TE.wait_ge(s_kst.h, kst_tok)
                                    TE.wait_ge(s_exp.h, exp_prev2[b4 % 2])
                                rhs = bass.AP(qfl, tl * TAU + gi * 8,
                                              [[16 * SUP, 73], [1, 8], [SUP, 16]])
                                mm = TE.matmul(ps_s[:, qb_ + g * 128:qb_ + (g + 1) * 128],
                                               kstk[bi][0:73, gi * 128:(gi + 1) * 128],
                                               rhs, start=True, stop=True)
                                smm_t[gi] = s_smm.inc(mm, 1)
                            SE.wait_ge(s_smm.h, smm_t[b4 * 4 + 3])
                            if b4 == 0:
                                SE.wait_ge(s_cmm.h, est_free[bi])
                            exp_t[b4] = s_exp.inc(SE.activation(
                                est[bi][:, b4 * 512:(b4 + 1) * 512],
                                ps_s[:, qb_:qb_ + 512],
                                AF.Exp, bias=0.0, scale=0.125), 1)
                            exp_prev2[b4 % 2] = exp_t[b4]
                        for g8 in range(8):
                            gi = half * 8 + g8
                            if g8 == 0:
                                TE.wait_ge(s_vpr.h, vpr_tok)
                                TE.wait_ge(s_vev.h, vev_t[0] if half == 1 else vev_prev)
                            tr = TE.transpose(ptr[:, gi * 64:gi * 64 + 64],
                                              vpre[bi][0:64, gi * 128:(gi + 1) * 128],
                                              sid[0:64, 0:64])
                            vtr_t[gi] = s_vtr.inc(tr, 1)
                        VE.wait_ge(s_vtr.h, vtr_t[half * 8 + 7])
                        if half == 0:
                            VE.wait_ge(s_cmm.h, vstk_free[bi])
                        vev_t[half] = s_vev.inc(VE.tensor_copy(
                            bass.AP(vstk[bi], half * 8 * 65,
                                    [[16 * 65, 128], [65, 8], [1, 64]]),
                            bass.AP(ptr, half * 512,
                                    [[1024, 128], [64, 8], [1, 64]])), 1)
                        for b4 in (2 * half, 2 * half + 1):
                            for g in range(4):
                                gi = b4 * 4 + g
                                if g == 0:
                                    TE.wait_ge(s_exp.h, exp_t[b4])
                                    TE.wait_ge(s_vev.h, vev_t[half])
                                    TE.wait_ge(s_nrm.h, nrm_prev)
                                mm = TE.matmul(pcx[:, g * 128:g * 128 + 65],
                                               est[bi][:, gi * 128:(gi + 1) * 128],
                                               vstk[bi][:, gi * 65:(gi + 1) * 65],
                                               start=True, stop=True)
                                cmm_t[gi] = s_cmm.inc(mm, 1)
                            VE.wait_ge(s_cmm.h, cmm_t[b4 * 4 + 3])
                            VE.wait_ge(s_nrm.h, nrm_prev)
                            rcp_tok = s_rcp.inc(VE.reciprocal(
                                rcp4[:], bass.AP(pcx, 64, [[512, 128], [128, 4]])), 1)
                            SE.wait_ge(s_rcp.h, rcp_tok)
                            if b4 == 0:
                                SE.wait_ge(s_ctr.h, ctxn_free[bi])
                            for g in range(4):
                                gi = b4 * 4 + g
                                nr = SE.activation(
                                    ctxn[bi][:, gi * 64:(gi + 1) * 64],
                                    pcx[:, g * 128:g * 128 + 64],
                                    AF.Identity, bias=0.0, scale=rcp4[:, g:g + 1])
                                s_nrm.inc(nr, 1)
                            nrm_t[b4] = s_nrm.n
                            nrm_prev = s_nrm.n
                        for g8 in range(8):
                            gi = half * 8 + g8
                            if g8 == 0:
                                TE.wait_ge(s_nrm.h, nrm_t[2 * half + 1])
                                TE.wait_ge(s_cev.h, cev_prev)
                            tr = TE.transpose(ptc[0:64, g8 * 128:(g8 + 1) * 128],
                                              ctxn[bi][:, gi * 64:(gi + 1) * 64],
                                              sid[:])
                            ctr_t[gi] = s_ctr.inc(tr, 1)
                        VE.wait_ge(s_ctr.h, ctr_t[half * 8 + 7])
                        if half == 0:
                            VE.wait_ge(s_wom.h, ctxt_free[bi])
                        cev_t[half] = s_cev.inc(VE.tensor_copy(
                            bass.AP(ctxt[bi], half * 8 * 8,
                                    [[16 * 128, 64], [8, 8], [1, 8], [128, 16]]),
                            bass.AP(ptc, 0, [[1024, 64], [128, 8], [16, 8], [1, 16]])), 1)
                        cev_prev = cev_t[half]
                    kst_free[bi] = smm_t[15]
                    vpr_free[bi] = vtr_t[15]
                    est_free[bi] = cmm_t[15]
                    vstk_free[bi] = cmm_t[15]
                    ctxn_free[bi] = ctr_t[15]
                    vev_prev = vev_t[1]
                    if hook is not None:
                        hook()
                    TE.wait_ge(s_cev.h, cev_t[1])
                    TE.wait_ge(s_cst.h, s_cst.n)
                    TE.wait_ge(VE._sem, aev_prev)
                    for h in range(16):
                        for eh in range(2):
                            mm = TE.matmul(
                                pwo[:, eh * 512:(eh + 1) * 512],
                                ctxt[bi][0:64, h * 128:(h + 1) * 128],
                                wo[:, h * 1024 + eh * 512:h * 1024 + (eh + 1) * 512],
                                start=(h == 0), stop=(h == 15))
                    wom_tok = s_wom.inc(mm, 1)
                    ctxt_free[bi] = wom_tok
                    VE.wait_ge(s_wom.h, wom_tok)
                    VE.wait_ge(s_asp.h, atst_free[bi])
                    VE.tensor_copy(atst[bi][:], pwo[:])
                    aev_tok = VE._n
                    aev_prev = aev_tok
                    GE.wait_ge(VE._sem, aev_tok)
                    atst_free[bi] = s_asp.inc(GE.dma_start(
                        out=at_sp.ap()[ti * TAU:(ti + 1) * TAU, :],
                        in_=atst[bi][:]), 16)
                    s_mid.inc(VE.engine_nop(), 1)
                    fla_prev_smm = smm_t[15]
                    fla_prev_kst = kst_tok
                    fla_prev_vpr = vpr_tok

                def f_wchunk(mi, ci, src_d):
                    bi = ci % 2
                    ncc = src_d.shape[1] // 512
                    rb, cb = ci // ncc, ci % ncc
                    if mi == 0 and ci == 0:
                        VE.memset(wacc[:], 0.0)
                    SY.wait_ge(VE._sem, wfs_free[bi])
                    ldt = s_wfl.inc(SY.dma_start(
                        out=wfs[bi],
                        in_=src_d.ap()[rb * 128:(rb + 1) * 128,
                                       cb * 512:(cb + 1) * 512]), 16)
                    VE.wait_ge(s_wfl.h, ldt)
                    VE.scalar_tensor_tensor(wneg[:], wfs[bi], -1.0, wfs[bi],
                                            ALU.mult, ALU.max)
                    wfs_free[bi] = VE._n
                    VE.tensor_add(wacc[:], wacc[:], wneg[:])
                    if ci == 63:
                        s_wab.inc(VE.reduce_sum(wabs[:, mi:mi + 1], wacc[:],
                                                AX.X), 1)
                        VE.memset(wacc[:], 0.0)

                wsrc = [(0, wg_d), (1, wu_d), (2, wd_d)]
                wchunks = [(mi, ci, d) for (mi, d) in wsrc for ci in range(64)]
                wi = 0
                ln_toks = {}
                smts = {}

                def ln_pre_super(sj):
                    for tl in range(TPS):
                        f_ln_pre(sj * TPS + tl)
                    smts[sj] = f_ln_smalls(slice(sj * TPS, (sj + 1) * TPS))

                def ln_fin(tj):
                    ln_toks[tj] = f_ln(tj, smts[tj // TPS])

                for sj in range(min(2, NS)):
                    ln_pre_super(sj)
                    for tl in range(TPS):
                        ln_fin(sj * TPS + tl)
                if NS > 2:
                    ln_pre_super(2)
                for si in range(NS):
                    f_qkv(si, [ln_toks[si * TPS + tl] for tl in range(TPS)])
                    t2 = (si + 2) * TPS
                    f_middle(si * TPS, si,
                             hook=(lambda t=t2: ln_fin(t)) if si + 2 < NS else None)
                    if si + 2 < NS:
                        ln_fin(t2 + 1)
                    f_middle(si * TPS + 1, si,
                             hook=(lambda s=si + 3: ln_pre_super(s))
                             if si + 3 < NS else None)
                    fla_prev_mid = s_mid.n
                    n_w = 13 if si < NS - 1 else len(wchunks) - wi
                    for _ in range(n_w):
                        if wi < len(wchunks):
                            f_wchunk(*wchunks[wi]); wi += 1
                s_mum = sem("mum")
                TE.wait_ge(s_wab.h, s_wab.n)
                TE.wait_ge(s_nrm.h, s_nrm.n)
                for mi in range(3):
                    s_mum.inc(TE.matmul(pcx[0:1, mi:mi + 1], wabs[:, mi:mi + 1],
                                        onesc[:], start=True, stop=True), 1)
                SE.wait_ge(s_mum.h, s_mum.n)
                SE.copy(murow[:], pcx[0:1, 0:3])
                SE.mul(murow[:], murow[:], 1.0 / (D * IFF))
                se_tok = SE._n
                VE.wait_ge(SE._sem, se_tok)
                VE.tensor_scalar_max(murow[:], murow[:], 1e-5)
                VE.tensor_scalar_mul(murow[:], murow[:], 0.5)
                mur_tok = VE._n
                TE.wait_ge(VE._sem, mur_tok)
                for mi in range(3):
                    s_mum.inc(TE.matmul(pcx[:, 4 + mi:5 + mi], ones1[:],
                                        murow[0:1, mi:mi + 1], start=True, stop=True), 1)
                SE.wait_ge(s_mum.h, s_mum.n)
                SE.copy(mu3[:], pcx[:, 4:7])
                se_tok2 = SE._n
                VE.wait_ge(SE._sem, se_tok2)
                VE.tensor_scalar_mul(thr_n[:], mu3[:], -1.0)
                GE.wait_ge(s_qsp.h, s_qsp.n)
                GE.wait_ge(s_asp.h, s_asp.n)
                SY.wait_ge(s_fla.h, s_fla.n)

            barrier("A")

            esB = ExitStack()
            with esB:
                sbb = lambda n, sh, dt: esB.enter_context(nc.sbuf_tensor(n, sh, dt))
                wgt = sbb("wgt", [128, 8 * IFF], BF16)
                wut = sbb("wut", [128, 8 * IFF], BF16)
                s_twl = sem("twl")
                s_ts1 = sem("ts1")
                s_ts2 = sem("ts2")
                s_ter = sem("ter")

                with ExitStack() as esT:
                    st = lambda n, sh, dt: esT.enter_context(nc.sbuf_tensor(n, sh, dt))
                    twf = [st(f"twf{i}", [128, 2048], F32) for i in range(2)]
                    ts1 = [st(f"ts1_{i}", [128, 2048], BF16) for i in range(2)]
                    ts2 = [st(f"ts2_{i}", [128, 2048], BF16) for i in range(2)]
                    tst_free = [0, 0]

                    def f_tern(mi, src_d, dst):
                        for ck in range(16):
                            bi = ck % 2
                            ldt = s_twl.inc(SE.dma_start(
                                out=twf[bi][:],
                                in_=src_d.ap()[(ck // 2) * 128:(ck // 2) * 128 + 128,
                                               (ck % 2) * 2048:(ck % 2) * 2048 + 2048]), 16)
                            SE.wait_ge(s_twl.h, ldt)
                            SE.wait_ge(s_ter.h, tst_free[bi])
                            SE.activation(ts1[bi][:], twf[bi][:], AF.Sign,
                                          bias=thr_n[:, mi:mi + 1], scale=1.0)
                            t2 = s_ts2.inc(SE.activation(
                                ts2[bi][:], twf[bi][:], AF.Sign,
                                bias=mu3[:, mi:mi + 1], scale=1.0), 1)
                            VE.wait_ge(s_ts2.h, t2)
                            tt = s_ter.inc(VE.tensor_add(
                                dst[:, (ck // 2) * IFF + (ck % 2) * 2048:
                                    (ck // 2) * IFF + (ck % 2) * 2048 + 2048],
                                ts1[bi][:], ts2[bi][:]), 1)
                            tst_free[bi] = tt

                    f_tern(0, wg_d, wgt)
                    f_tern(1, wu_d, wut)

                barrier("T1")

                qtl = [sbb(f"qtl{i}", [128, D], BF16) for i in range(2)]
                gst = [sbb("gst0", [128, 1024], F32)] * 2
                hps = [sbb(f"hps{i}", [128, IFF], F32) for i in range(2)]
                htf = sbb("htf", [128, 1024], F32)
                hi0 = sbb("hi0", [128, 1024], mybir.dt.int32)
                hdf = sbb("hdf", [128, 1024], F32)
                hs1 = sbb("hs1", [128, 1024], BF16)
                hs2 = sbb("hs2", [128, 1024], BF16)
                h2b = [sbb(f"h2b{i}", [128, 1024], BF16) for i in range(2)]
                h2ts = [sbb(f"h2ts{i}", [128, 1024], BF16) for i in range(2)]
                pg = [esB.enter_context(nc.psum_tensor(f"pg{i}", [128, 1024], F32))
                      for i in range(2)]
                pu = [esB.enter_context(nc.psum_tensor(f"pu{i}", [128, 1024], F32))
                      for i in range(2)]

                s_qtl = sem("qtl")
                s_gmm = sem("gmm")
                s_umm = sem("umm")
                s_gev = sem("gev")
                s_rl2 = sem("rl2")
                s_hpw = sem("hpw")
                s_hmx = sem("hmx")
                s_h2a = sem("h2a")
                s_h2b = sem("h2b")
                s_h2t = sem("h2t")
                s_h2s = sem("h2s")

                VE.tensor_scalar(t_a[:, :], p_al[:, :], mu3[:, 0:1], None, ALU.mult)
                VE.tensor_scalar(t_b[:, :], p_al[:, :], mu3[:, 1:2], None, ALU.mult)
                VE.tensor_mul(t_a[:, :], t_a[:, :], t_a[:, :])
                VE.tensor_mul(p_ct[:, :], t_a[:, :], t_b[:, :])
                ctd_tok = VE._n

                qtl_free = [0, 0]
                qtl_toks = {}
                gst_free = [0, 0]
                pg_free = [0, 0]
                pu_free = [0, 0]
                hps_q_free = [[0] * 4, [0] * 4]
                h2b_free = [0, 0]
                h2t_free = [0, 0]
                shd_toks = {}
                fin_prev = [0]

                def emit_qtl_load(tj):
                    bj = tj % 2
                    SY.wait_ge(s_umm.h, qtl_free[bj])
                    qtl_toks[tj] = s_qtl.inc(SY.dma_start(
                        out=qtl[bj][:],
                        in_=bass.AP(qt_sp, tj * D * TAU,
                                    [[128, 128], [128 * 128, 8], [1, 128]])), 16)

                chain_pend = [None]

                def chainB():
                    if chain_pend[0] is None:
                        return
                    tj, hf, addt = chain_pend[0]
                    chain_pend[0] = None
                    b = hf % 2
                    VE.wait_ge(GE._sem, addt)
                    VE.wait_ge(s_h2t.h, h2b_free[b])
                    VE.scalar_tensor_tensor(h2b[b][:], hi0[:], 2.0,
                                            hs1[:], ALU.mult, ALU.add)
                    fin_prev[0] = VE._n
                    SY.wait_ge(VE._sem, fin_prev[0])
                    SY.wait_ge(s_h2s.h, h2t_free[b])
                    h2t_tok = s_h2t.inc(SY.dma_start_transpose(
                        bass.AP(h2ts[b], 0, [[1024, 128], [128, 8], [1, 128]]),
                        bass.AP(h2b[b], 0, [[1024, 128], [128, 8], [1, 128]])), 16)
                    h2b_free[b] = h2t_tok
                    GE.wait_ge(s_h2t.h, h2t_tok)
                    h2t_free[b] = s_h2s.inc(GE.dma_start(
                        out=bass.AP(h2_sp, tj * IFF * TAU + hf * 8 * 128 * 128,
                                    [[128, 128], [128 * 128, 8], [1, 128]]),
                        in_=h2ts[b][:]), 16)

                def chain_quarter(tj, hf):
                    chainB()
                    pj = tj % 2
                    colj = slice(tj, tj + 1)
                    cs2 = slice(hf * 1024, (hf + 1) * 1024)
                    SE.wait_ge(s_hmx.h, shd_toks[tj])
                    if fin_prev[0]:
                        SE.wait_ge(VE._sem, fin_prev[0])
                    SE.activation(htf[:], hps[pj][:, cs2], AF.Copy,
                                  bias=0.0, scale=p_sh[:, colj])
                    ext = SE._n
                    hps_q_free[pj][hf] = ext
                    SE.activation(hi0[:], htf[:], AF.Copy, bias=0.0, scale=1.0)
                    cast = SE._n
                    VE.wait_ge(SE._sem, cast)
                    VE.tensor_tensor(hdf[:], htf[:], hi0[:], ALU.subtract)
                    sub = VE._n
                    SE.wait_ge(VE._sem, sub)
                    SE.activation(hs1[:], hdf[:], AF.Sign,
                                  bias=chf[:, 0:1], scale=1.0)
                    SE.activation(hs2[:], hdf[:], AF.Sign,
                                  bias=chf[:, 1:2], scale=1.0)
                    sig = SE._n
                    GE.wait_ge(SE._sem, sig)
                    if fin_prev[0]:
                        GE.wait_ge(VE._sem, fin_prev[0])
                    GE.tensor_add(hs1[:], hs1[:], hs2[:])
                    chain_pend[0] = (tj, hf, GE._n)

                emit_qtl_load(0)
                for ti in range(NT):
                    bi = ti % 2
                    pi = ti % 2
                    col = slice(ti, ti + 1)
                    if ti + 1 < NT:
                        emit_qtl_load(ti + 1)
                    umm_t = [0] * 4
                    for q in range(4):
                        qs = q % 2
                        TE.wait_ge(s_qtl.h, qtl_toks[ti])
                        TE.wait_ge(s_gev.h, pg_free[qs])
                        TE.wait_ge(s_hpw.h, pu_free[qs])
                        for f in range(8):
                            lhs = qtl[bi][:, f * 128:(f + 1) * 128]
                            for c2 in range(2):
                                mmg = TE.matmul(
                                    pg[qs][:, c2 * 512:(c2 + 1) * 512], lhs,
                                    wgt[:, f * IFF + q * 1024 + c2 * 512:
                                        f * IFF + q * 1024 + (c2 + 1) * 512],
                                    start=(f == 0), stop=(f == 7))
                            for c2 in range(2):
                                mmu = TE.matmul(
                                    pu[qs][:, c2 * 512:(c2 + 1) * 512], lhs,
                                    wut[:, f * IFF + q * 1024 + c2 * 512:
                                        f * IFF + q * 1024 + (c2 + 1) * 512],
                                    start=(f == 0), stop=(f == 7))
                        gmm_t = s_gmm.inc(mmg, 1)
                        umm_t[q] = s_umm.inc(mmu, 1)
                        SE.wait_ge(s_gmm.h, gmm_t)
                        SE.wait_ge(s_hpw.h, gst_free[0])
                        gev_tok = s_gev.inc(
                            SE.copy(gst[bi][:], pg[qs][:]), 1)
                        pg_free[qs] = gev_tok
                        VE.wait_ge(s_gev.h, gev_tok)
                        s_rl2.inc(VE.scalar_tensor_tensor(
                            gst[bi][:], gst[bi][:], 0.0, gst[bi][:],
                            ALU.max, ALU.mult), 1)
                        if ti >= 2:
                            chain_quarter(ti - 2, q)
                        VE.wait_ge(s_umm.h, umm_t[q])
                        VE.wait_ge(SE._sem, hps_q_free[pi][q])
                        hpw_tok = s_hpw.inc(VE.tensor_mul(
                            hps[pi][:, q * 1024:(q + 1) * 1024],
                            gst[bi][:], pu[qs][:]), 1)
                        gst_free[0] = hpw_tok
                        pu_free[qs] = hpw_tok
                        s_hmx.inc(VE.reduce_max(
                            hx4[:, q:q + 1], hps[pi][:, q * 1024:(q + 1) * 1024],
                            AX.X, apply_absolute_value=True), 1)
                    VE.wait_ge(VE._sem, ctd_tok)
                    VE.reduce_max(hx4[:, 4:5], hx4[:, 0:4], AX.X)
                    VE.tensor_mul(hx4[:, 5:6], p_ct[:, col], hx4[:, 4:5])
                    VE.tensor_scalar_max(hx4[:, 5:6], hx4[:, 5:6], 1e-5)
                    VE.reciprocal(hx4[:, 6:7], hx4[:, 5:6])
                    VE.tensor_mul(hx4[:, 6:7], hx4[:, 6:7], p_ct[:, col])
                    VE.tensor_scalar_mul(p_sh[:, col], hx4[:, 6:7], 127.0)
                    shd_toks[ti] = s_hmx.inc(VE.tensor_scalar_mul(
                        p_dd[:, col], hx4[:, 5:6], 1.0 / 254.0), 1)
                    qtl_free[bi] = umm_t[3]
                for tj in range(max(0, NT - 2), NT):
                    for hf in range(4):
                        chain_quarter(tj, hf)
                chainB()
                GE.wait_ge(s_h2s.h, s_h2s.n)

            barrier("B1")

            esC = ExitStack()
            with esC:
                sc = lambda n, sh, dt: esC.enter_context(nc.sbuf_tensor(n, sh, dt))
                wdt = sc("wdt", [128, 32 * 1024], BF16)
                s_twl = sem("twl")
                s_ts2 = sem("ts2")
                s_ter = sem("ter")
                with ExitStack() as esT:
                    st = lambda n, sh, dt: esT.enter_context(nc.sbuf_tensor(n, sh, dt))
                    twf = [st(f"dwf{i}", [128, 1024], F32) for i in range(2)]
                    ts1 = [st(f"ds1_{i}", [128, 1024], BF16) for i in range(2)]
                    ts2 = [st(f"ds2_{i}", [128, 1024], BF16) for i in range(2)]
                    tst_free = [s_ter.n, s_ter.n]
                    for ck in range(32):
                        bi = ck % 2
                        ldt = s_twl.inc(SE.dma_start(
                            out=twf[bi][:],
                            in_=wd_d.ap()[ck * 128:(ck + 1) * 128, :]), 16)
                        SE.wait_ge(s_twl.h, ldt)
                        SE.wait_ge(s_ter.h, tst_free[bi])
                        SE.activation(ts1[bi][:], twf[bi][:], AF.Sign,
                                      bias=thr_n[:, 2:3], scale=1.0)
                        t2 = s_ts2.inc(SE.activation(
                            ts2[bi][:], twf[bi][:], AF.Sign,
                            bias=mu3[:, 2:3], scale=1.0), 1)
                        VE.wait_ge(s_ts2.h, t2)
                        tt = s_ter.inc(VE.tensor_add(
                            wdt[:, ck * 1024:(ck + 1) * 1024],
                            ts1[bi][:], ts2[bi][:]), 1)
                        tst_free[bi] = tt

                barrier("T2")

                h2l = [sc(f"h2l{i}", [128, IFF], BF16) for i in range(2)]
                x2 = [sc(f"x2_{i}", [128, D], F32) for i in range(2)]
                at2 = [sc(f"at2_{i}", [128, D], F32) for i in range(2)]
                ost = [sc(f"ost{i}", [128, D], F32) for i in range(2)]
                o1b = sc("o1b", [128, D], F32)
                pd = [esC.enter_context(nc.psum_tensor(f"pd{i}", [128, 1024], F32))
                      for i in range(2)]

                s_h2l = sem("h2l")
                s_x2l = sem("x2l")
                s_a2l = sem("a2l")
                s_dmm = sem("dmm")
                s_och = sem("och")
                s_osp = sem("osp")

                VE.tensor_scalar(p_dd[:, :], p_dd[:, :], mu3[:, 2:3], None, ALU.mult)
                ddf_tok = VE._n

                h2l_free = [0, 0]
                x2_free = [0, 0]
                at2_free = [0, 0]
                ost_free = [0, 0]
                pd_free = [0, 0]

                for ti in range(NT):
                    bi = ti % 2
                    col = slice(ti, ti + 1)
                    SE.wait_ge(s_dmm.h, h2l_free[bi])
                    h2tok = s_h2l.inc(SE.dma_start(
                        out=h2l[bi][:],
                        in_=bass.AP(h2_sp, ti * IFF * TAU,
                                    [[128, 128], [128 * 128, 32], [1, 128]])), 16)
                    SE.wait_ge(s_och.h, x2_free[bi])
                    x2tok = s_x2l.inc(SE.dma_start(
                        out=x2[bi][:], in_=x_in.ap()[ti * TAU:(ti + 1) * TAU, :]), 16)
                    SE.wait_ge(s_och.h, at2_free[bi])
                    a2tok = s_a2l.inc(SE.dma_start(
                        out=at2[bi][:], in_=at_sp.ap()[ti * TAU:(ti + 1) * TAU, :]), 16)
                    qs = ti % 2
                    TE.wait_ge(s_h2l.h, h2tok)
                    TE.wait_ge(s_och.h, pd_free[qs])
                    for f2 in range(32):
                        for c2 in range(2):
                            mm = TE.matmul(
                                pd[qs][:, c2 * 512:(c2 + 1) * 512],
                                h2l[bi][:, f2 * 128:(f2 + 1) * 128],
                                wdt[:, f2 * 1024 + c2 * 512:f2 * 1024 + (c2 + 1) * 512],
                                start=(f2 == 0), stop=(f2 == 31))
                    dmm_tok = s_dmm.inc(mm, 1)
                    h2l_free[bi] = dmm_tok
                    VE.wait_ge(s_dmm.h, dmm_tok)
                    VE.wait_ge(s_x2l.h, x2tok)
                    VE.wait_ge(s_a2l.h, a2tok)
                    VE.wait_ge(VE._sem, ddf_tok)
                    VE.wait_ge(s_osp.h, ost_free[bi])
                    VE.scalar_tensor_tensor(o1b[:], pd[qs][:], p_dd[:, col],
                                            x2[bi][:], ALU.mult, ALU.add)
                    VE.tensor_add(o1b[:], o1b[:], at2[bi][:])
                    och_tok = s_och.inc(VE.tensor_add(ost[bi][:], o1b[:], bbc[:]), 1)
                    pd_free[qs] = och_tok
                    x2_free[bi] = och_tok
                    at2_free[bi] = och_tok
                    GE.wait_ge(s_och.h, och_tok)
                    ost_free[bi] = s_osp.inc(GE.dma_start(
                        out=out_d.ap()[ti * TAU:(ti + 1) * TAU, :],
                        in_=ost[bi][:]), 16)
                if DBG:
                    s_dbg = sem("dbg")
                    for i, t in enumerate((p_m, p_rs, p_sg, p_sq, p_al, p_ct,
                                           p_sh, p_dd, mu3, thr_n, wabs)):
                        w = min(t.shape[1], 64)
                        s_dbg.inc(GE.dma_start(
                            out=dbg_p.ap()[:, i * 64:i * 64 + w], in_=t[:, 0:w]), 16)
                    GE.wait_ge(s_dbg.h, s_dbg.n)
                GE.wait_ge(s_osp.h, s_osp.n)
                SE.wait_ge(s_osp.h, s_osp.n)

    return nc


def make_host_layouts(W_qkv, b_qkv):
    perm = np.zeros(3 * D, dtype=np.int64)
    for r in range(3):
        for j in range(8):
            for hh in range(2):
                h = 2 * j + hh
                cols = 192 * h + 64 * r + np.arange(64)
                m = r * 8 + j
                perm[m * 128 + hh * 64: m * 128 + hh * 64 + 64] = cols
    Wp = np.ascontiguousarray(np.asarray(W_qkv)[:, perm])
    bp = np.asarray(b_qkv)[perm]
    btile = np.ascontiguousarray(bp.reshape(24, 128).T).astype(np.float32)
    return Wp, btile


def make_masks():
    mq = np.zeros((9, 16 * SUP), dtype=BF)
    mq[0, :] = 32.0
    cols = np.arange(16 * SUP)
    for tp in range(8):
        mq[1 + tp, (cols % 8) == tp] = 32.0
    mk = np.zeros((9, 16 * TAU), dtype=BF)
    mk[0, :] = -32.0
    for tp in range(8):
        mk[1 + tp, ((cols[:16 * TAU] % 128) // 16) == tp] = 32.0
    return mq, mk


def kernel(L0_attn_x, L0_attn_W_qkv, L0_attn_b_qkv, L0_attn_W_out,
           L0_attn_b_out, L0_ffn_w_gate, L0_ffn_w_up, L0_ffn_w_down):
    x = np.ascontiguousarray(np.asarray(L0_attn_x, dtype=np.float32))
    B, S, _ = x.shape
    Wp, btile = make_host_layouts(L0_attn_W_qkv, L0_attn_b_qkv)
    mq, mk = make_masks()
    shared = {
        "wqkv": Wp.astype(np.float32),
        "btile": btile,
        "wout": np.ascontiguousarray(np.asarray(L0_attn_W_out, dtype=np.float32)),
        "bout": np.asarray(L0_attn_b_out, dtype=np.float32).reshape(1, D),
        "wg_t": np.ascontiguousarray(np.asarray(L0_ffn_w_gate, dtype=np.float32).T),
        "wu_t": np.ascontiguousarray(np.asarray(L0_ffn_w_up, dtype=np.float32).T),
        "wd_t": np.ascontiguousarray(np.asarray(L0_ffn_w_down, dtype=np.float32).T),
        "ident": np.eye(128, dtype=BF),
        "maskq": mq,
        "maskk": mk,
    }
    nc = build_program(S)
    in_maps = [dict(shared, x=np.ascontiguousarray(x[b])) for b in range(B)]
    res = run_bass_kernel_spmd(nc, in_maps, core_ids=list(range(B)))
    out = np.stack([np.asarray(res.results[b]["out"]) for b in range(B)], axis=0)
    return out.astype(np.float32)
